# version 02
# baseline (speedup 1.0000x reference)
import math
import types
import numpy as np
from contextlib import ExitStack

import concourse.bass as bass
import concourse.mybir as mybir
from concourse.bass_utils import run_bass_kernel_spmd

F32 = mybir.dt.float32
BF16 = mybir.dt.bfloat16
AF = mybir.ActivationFunctionType
ALU = mybir.AluOpType
AX = mybir.AxisListType

ENGS = ('tensor', 'vector', 'scalar', 'gpsimd', 'sync')
QCH = [f'{q}{i}' for q in ('sync', 'gpsimd') for i in range(4)]
ALLSEM = list(ENGS) + QCH
NSEQ = 2
T = 2048
D = 1024
NCH = 8
NTC = 16
NTT = 4


class FW:
    def __init__(self, nc):
        self.nc = nc
        self.streams = {e: [] for e in ENGS}
        self.cnt = {e: 0 for e in ALLSEM}
        self.sems = {}
        self.lastwait = {}

    def wait(self, eng, deps):
        for d in deps:
            if d is None:
                continue
            se, v = d
            if v <= 0:
                continue
            assert v <= self.cnt[se], f"dep on future {se}#{v} (have {self.cnt[se]})"
            if se not in ENGS:
                v = self.cnt[se]
            k = (eng, se)
            if self.lastwait.get(k, 0) >= v:
                continue
            self.lastwait[k] = v
            sem = self.sems[se]
            self.streams[eng].append(lambda e, sem=sem, v=v: e.wait_ge(sem, v))

    @staticmethod
    def _freeze(f):
        if not f.__closure__:
            return f
        cells = tuple(types.CellType(c.cell_contents) for c in f.__closure__)
        return types.FunctionType(f.__code__, f.__globals__, f.__name__,
                                  f.__defaults__, cells)

    def op(self, eng, f, deps=(), inc=1, sname=None):
        f = self._freeze(f)
        self.wait(eng, deps)
        sname = sname or eng
        self.cnt[sname] += inc
        sem = self.sems[sname]

        def closure(e, f=f, sem=sem, inc=inc):
            f(e).then_inc(sem, inc)
        self.streams[eng].append(closure)
        return (sname, self.cnt[sname])

    def memset(self, eng, ap, val, deps=()):
        return self.op(eng, lambda e, a=ap, v=val: e.memset(a, v), deps=deps)

    def dma(self, q, out, in_, deps=(), ch=0):
        sname = f'{q}{ch}'
        deps = list(deps) + [(sname, self.cnt[sname])]
        return self.op(q, lambda e, o=out, i=in_: e.dma_start(out=o, in_=i),
                       deps=deps, inc=16, sname=sname)

    def tok(self, eng):
        return (eng, self.cnt[eng])

    def all_toks(self):
        return [(e, self.cnt[e]) for e in ALLSEM]

    def barrier(self):
        toks = self.all_toks()
        for e in ENGS:
            self.wait(e, toks)

    def finalize(self, block, final_deps=()):
        self.wait('sync', final_deps)
        for e in ENGS:
            stream = self.streams[e]
            getattr(block, e)(lambda eng, stream=stream: [f(eng) for f in stream])


class SB:
    def __init__(self, handle, words):
        self.h = handle
        self.off = 0
        self.words = words
        self.stack = []
        self.hiwater = 0

    def alloc(self, words, dt=F32):
        off = self.off
        self.off += words
        self.hiwater = max(self.hiwater, self.off)
        assert self.off <= self.words, f"arena overflow {self.off} > {self.words}"
        ap = self.h[:, off:off + words]
        return ap if dt == F32 else ap.bitcast(dt)

    def push(self):
        self.stack.append(self.off)

    def pop(self):
        self.off = self.stack.pop()


class PP:
    def __init__(self, tiles):
        self.t = tiles
        self.rel = [[] for _ in tiles]
        self.held = [False] * len(tiles)
        self.i = 0

    def get(self, hold=False):
        for _ in range(len(self.t)):
            i = self.i % len(self.t)
            self.i += 1
            if not self.held[i]:
                self.held[i] = hold
                return i, self.t[i], list(self.rel[i])
        raise RuntimeError("all psum banks held")

    def done(self, i, toks):
        self.rel[i] = [t for t in toks if t is not None]
        self.held[i] = False


def host_consts():
    c = {}
    c['IDN'] = np.eye(128, dtype=np.float32)
    c['ONES'] = np.ones((128, 128), np.float32)
    a = np.arange(128)
    c['TRIG'] = np.where(a[:, None] > a[None, :], -1e30, 0.0).astype(np.float32)
    c['TRIL'] = np.where(a[:, None] < a[None, :], -1e30, 0.0).astype(np.float32)
    c['SGN'] = np.sign(a[:, None] - a[None, :]).astype(np.float32)
    half = 64
    freqs = np.power(10000.0, -np.arange(half, dtype=np.float64) / half)
    rad = np.arange(T, dtype=np.float64)[:, None] * freqs[None, :]
    s = 1.0 / math.sqrt(128.0)
    c['COS0'] = (np.cos(rad) * s).astype(np.float32)
    c['SIN0'] = (np.sin(rad) * s).astype(np.float32)
    DH = 64
    hi = np.arange(0, DH, 2, dtype=np.float64)
    inv_freq = 1.0 / 10000.0 ** (hi / DH)
    tt_ = np.arange(T, dtype=np.float64)
    fr = tt_[:, None] * inv_freq[None, :]
    fr = np.concatenate([fr, fr], axis=-1)
    base = (hi + 0.4 * DH) / (1.4 * DH)
    power = (tt_ - T // 2) / 256.0
    scale = base[None, :] ** power[:, None]
    scale = np.concatenate([scale, scale], axis=-1)
    c['XCQ'] = (np.cos(fr) * scale * 8.0).astype(np.float32)
    c['XSQ'] = (np.sin(fr) * scale * 8.0).astype(np.float32)
    c['XCK'] = (np.cos(fr) / scale).astype(np.float32)
    c['XSK'] = (np.sin(fr) / scale).astype(np.float32)
    c['POSN'] = np.ascontiguousarray((np.arange(T, dtype=np.float64) / T).astype(np.float32).reshape(16, 128).T)
    n = 4096
    NF = 2176
    f = np.arange(NF, dtype=np.float64)
    tfull = np.arange(n, dtype=np.float64)
    ang = 2.0 * np.pi * tfull[:, None] * f[None, :] / n
    cre = np.cos(ang)
    cim = -np.sin(ang)
    cre[:, 2049:] = 0.0
    cim[:, 2049:] = 0.0
    c['C4R'] = cre.astype(np.float32)
    c['C4I'] = cim.astype(np.float32)
    w = np.full(NF, 2.0 / n, np.float64)
    w[0] = 1.0 / n
    w[2048] = 1.0 / n
    w[2049:] = 0.0
    ii = np.arange(T, dtype=np.float64)
    ang2 = 2.0 * np.pi * f[:, None] * ii[None, :] / n
    c['AMAT'] = (w[:, None] * np.cos(ang2)).astype(np.float32)
    c['BMAT'] = (-w[:, None] * np.sin(ang2)).astype(np.float32)
    tlin = np.linspace(0.0, 1.0, n)
    deltas = np.linspace(math.log(0.01) / 1.5, math.log(0.01) / 0.3, D)
    c['DECAY'] = (np.exp(-tlin[:, None] * np.abs(deltas)[None, :]) + 0.05).astype(np.float32)
    t01 = np.linspace(0.0, 1.0, n)
    tr = np.linspace(0.0, n - 1.0, n)
    wang = 2.0 * np.pi * tr / n
    fband = np.array([0.0001])
    fw_ = fband[None, :] * wang[:, None]
    c['ZAUG'] = np.stack([t01, np.cos(fw_)[:, 0], -np.sin(fw_)[:, 0], np.ones(n)], 0).astype(np.float32)
    c['NEGIDX'] = (-np.arange(64, dtype=np.float64) * 1e-8).astype(np.float32).reshape(1, 64)
    c['ONECOL'] = np.ones((128, 1), np.float32)
    return c


def prep_weights(inp):
    w = {}
    cp = np.ascontiguousarray
    w['L0_wq'] = inp['L0_attn_wq'].reshape(1024, 1024)
    w['L0_wk'] = inp['L0_attn_wk'].reshape(1024, 1024)
    w['L0_wv'] = inp['L0_attn_wv'].reshape(1024, 1024)
    w['L0_wo'] = inp['L0_attn_wo'].reshape(1024, 1024)
    w['L0_guT'] = cp(inp['L0_ffn_experts_gate_up'].transpose(0, 2, 1))
    w['L0_dnT'] = cp(inp['L0_ffn_experts_down'].transpose(0, 2, 1))
    w['L0_gwT'] = cp(inp['L0_ffn_gate_weight'].T)
    w['L0_ebias'] = inp['L0_ffn_e_score_correction_bias'].reshape(1, 32)
    w['L0_sgT'] = cp(inp['L0_ffn_shared_gate_w'].T)
    w['L0_suT'] = cp(inp['L0_ffn_shared_up_w'].T)
    w['L0_sdT'] = cp(inp['L0_ffn_shared_down_w'].T)
    w['L1_WqT'] = cp(inp['L1_attn_Wq'].T)
    w['L1_WkT'] = cp(inp['L1_attn_Wk'].T)
    w['L1_WvT'] = cp(inp['L1_attn_Wv'].T)
    w['L1_lng'] = inp['L1_attn_ln_gamma'].reshape(8, 128, 1)
    w['L1_lnb'] = inp['L1_attn_ln_beta'].reshape(8, 128, 1)
    w['L1_rwT'] = cp(inp['L1_ffn_router_w'].T)
    w['L1_ebias'] = inp['L1_ffn_e_bias'].reshape(1, 64)
    w['L1_gu'] = cp(np.concatenate([inp['L1_ffn_Wg'], inp['L1_ffn_Wu']], axis=2))
    w['L1_dn'] = inp['L1_ffn_Wd']
    w['L1_sgT'] = inp['L1_ffn_SWg']
    w['L1_suT'] = inp['L1_ffn_SWu']
    w['L1_sdT'] = inp['L1_ffn_SWd']
    w['L2_w_in'] = inp['L2_attn_w_in']
    w['L2_b_in'] = inp['L2_attn_b_in'].reshape(24, 128, 1)
    w['L2_sf_w'] = inp['L2_attn_sf_w']
    w['L2_sf_b'] = inp['L2_attn_sf_b'].reshape(24, 128, 1)
    for k in ('f_w1', 'f_w2', 'f_w3', 'f_wout', 'w_out'):
        w['L2_' + k] = inp['L2_attn_' + k]
    for k in ('f_b1', 'f_b2', 'f_b3'):
        w['L2_' + k] = inp['L2_attn_' + k].reshape(1, 64)
    for k in ('freq1', 'freq2', 'freq3'):
        w['L2_' + k] = inp['L2_attn_' + k].reshape(64, 1)
    w['L2_sfw_cols'] = np.ascontiguousarray(inp['L2_attn_sf_w'].T).reshape(3, 24, 128, 1)
    w['L2_filt_bias'] = inp['L2_attn_filt_bias'].reshape(8, 128, 1)
    w['L2_b_out'] = inp['L2_attn_b_out'].reshape(8, 128, 1)
    HID, HP = 2730, 2816
    w1 = inp['L2_ffn_w1']
    w1p = np.zeros((2, 1024, 2 * HP), np.float32)
    w1p[:, :, :HID] = w1[:, :, :HID]
    w1p[:, :, HP:HP + HID] = w1[:, :, HID:]
    w['L2f_w1p'] = w1p
    w2p = np.zeros((2, HP, 1024), np.float32)
    w2p[:, :HID, :] = inp['L2_ffn_w2']
    w['L2f_w2p'] = w2p
    b1 = inp['L2_ffn_b1']
    b1p = np.zeros((2, 2 * HP), np.float32)
    b1p[:, :HID] = b1[:, :HID]
    b1p[:, HP:HP + HID] = b1[:, HID:]
    w['L2f_b1p'] = b1p.reshape(2, 44, 128, 1)
    w['L2f_b2'] = inp['L2_ffn_b2'].reshape(2, 8, 128, 1)
    w['L2f_wqkv'] = inp['L2_ffn_wqkv']
    w['L2f_wo'] = inp['L2_ffn_wo']
    w['L2f_qs'] = inp['L2_ffn_q_scale'].reshape(2, 1, 64)
    w['L2f_ks'] = inp['L2_ffn_k_scale'].reshape(2, 1, 64)
    w['L2f_lng'] = inp['L2_ffn_ln_g'].reshape(2, 8, 128, 1)
    w['L2f_lnb'] = inp['L2_ffn_ln_b'].reshape(2, 8, 128, 1)
    w['L3_winT'] = cp(inp['L3_attn_w_in'].T)
    w['L3_b_in'] = inp['L3_attn_b_in'].reshape(1, 3072)
    w['L3_woutT'] = cp(inp['L3_attn_w_out'].T)
    w['L3_b_out'] = inp['L3_attn_b_out'].reshape(8, 128, 1)
    w['L3_Wg'] = inp['L3_ffn_Wg']
    w['L3_bg'] = inp['L3_ffn_bg'].reshape(1, 8)
    w['L3_W1'] = inp['L3_ffn_W1']
    w['L3_b1'] = inp['L3_ffn_b1'].reshape(8, 8, 128, 1)
    w['L3_W2'] = inp['L3_ffn_W2']
    w['L3_b2'] = inp['L3_ffn_b2'].reshape(8, 8, 128, 1)
    return w


class Ctx:
    pass


def mk_dram(nc, weights, consts):
    d = {}
    for k, v in list(weights.items()) + list(consts.items()):
        d[k] = nc.dram_tensor(k, list(v.shape), F32, kind="ExternalInput").ap()
    d['XIN'] = nc.dram_tensor('XIN', [NSEQ, NCH, 128, T], F32, kind="ExternalInput").ap()
    d['OUT'] = nc.dram_tensor('OUT', [NSEQ, NCH, 128, T], F32, kind="ExternalOutput").ap()
    for nm, shp in [('xres', [NSEQ, NCH, 128, T]), ('xnh', [NSEQ, NCH, 128, T]),
                    ('dlt', [NSEQ, NCH, 128, T]), ('x0h', [NCH, 128, T]),
                    ('vh', [NCH, 128, T]), ('KRE', [17, 128, 1024]),
                    ('KIM', [17, 128, 1024]), ('coefh', [64, T])]:
        d[nm] = nc.dram_tensor(nm, shp, F32, kind="Internal").ap()
    return d


def emit_statics(C):
    fw, dr, sb = C.fw, C.dr, C.sb
    C.idn = sb.alloc(128)
    C.idnb = sb.alloc(64, BF16)
    C.trig = sb.alloc(128)
    C.tril = sb.alloc(128)
    C.sgn = sb.alloc(128)
    C.onesc = sb.alloc(1)
    C.onescb = sb.alloc(1, BF16)[:, 0:1]
    C.onesr = sb.alloc(128)
    C.onesrb = sb.alloc(64, BF16)
    C.posn = sb.alloc(16)
    C.negidx = sb.alloc(64)
    C.negc = sb.alloc(1)
    fw.memset('vector', C.negc, -1.0)
    fw.dma('gpsimd', C.idn, dr['IDN'])
    fw.dma('gpsimd', C.idnb, dr['IDN'])
    fw.dma('gpsimd', C.trig, dr['TRIG'])
    fw.dma('gpsimd', C.tril, dr['TRIL'])
    fw.dma('gpsimd', C.sgn, dr['SGN'])
    fw.dma('gpsimd', C.onesc, dr['ONECOL'])
    fw.dma('gpsimd', C.onescb, dr['ONECOL'])
    fw.dma('gpsimd', C.onesr[0:1, :], dr['ONES'][0:1, :])
    fw.dma('gpsimd', C.onesrb[0:1, :], dr['ONES'][0:1, :])
    fw.dma('gpsimd', C.posn, dr['POSN'])
    fw.dma('gpsimd', C.negidx[0:1, :], dr['NEGIDX'])
    fw.barrier()


def bcast_row(C, row_ap, n, deps, bf=False):
    i, ps, rel = C.pp.get()
    lhs = C.onesrb[0:1, :] if bf else C.onesr[0:1, :]
    t = C.fw.op('tensor', lambda e, r=row_ap, p=ps: e.matmul(
        p[:, 0:n], lhs, r, start=True, stop=True), deps=list(deps) + rel)
    return i, ps, t


def ln_norm_tiles(C, tiles, ttoks, deps0, eps=1e-5):
    fw, pp, sb = C.fw, C.pp, C.sb
    mub = sb.alloc(2048)
    rstdb = sb.alloc(2048)
    xsq = sb.alloc(512)
    r_mu = sb.alloc(512)
    r_v = sb.alloc(512)
    r_t = sb.alloc(512)
    toks = []
    prev_sq_reader = None
    for tt in range(NTT):
        i1, ps1, r1 = pp.get()
        i2, ps2, r2 = pp.get()
        for mc in range(NCH):
            fw.op('tensor', lambda e, a=tiles[(mc, tt)], s=(mc == 0): e.matmul(
                ps1[0:1, :], C.onesc, a, start=s, stop=(mc == NCH - 1)),
                deps=list(deps0) + r1 + [ttoks[(mc, tt)]])
        for mc in range(NCH):
            tsq = fw.op('scalar', lambda e, a=tiles[(mc, tt)]: e.activation(xsq, a, AF.Square),
                        deps=[ttoks[(mc, tt)], prev_sq_reader])
            prev_sq_reader = fw.op('tensor', lambda e, s=(mc == 0): e.matmul(
                ps2[0:1, :], C.onesc, xsq, start=s, stop=(mc == NCH - 1)),
                deps=[tsq] + r2)
        tmm = fw.tok('tensor')
        sl = slice(tt * 512, (tt + 1) * 512)
        t1 = fw.op('vector', lambda e: e.tensor_scalar(r_mu[0:1, :], ps1[0:1, :], 1.0 / D, None, ALU.mult), deps=[tmm])
        t2 = fw.op('vector', lambda e: e.tensor_scalar(r_v[0:1, :], ps2[0:1, :], 1.0 / D, None, ALU.mult), deps=[tmm])
        t3 = fw.op('vector', lambda e: e.tensor_tensor(r_t[0:1, :], r_mu[0:1, :], r_mu[0:1, :], ALU.mult), deps=[t1])
        t4 = fw.op('vector', lambda e: e.tensor_tensor(r_v[0:1, :], r_v[0:1, :], r_t[0:1, :], ALU.subtract), deps=[t2, t3])
        t5 = fw.op('vector', lambda e: e.tensor_scalar(r_v[0:1, :], r_v[0:1, :], eps, None, ALU.add), deps=[t4])
        t6 = fw.op('scalar', lambda e: e.activation(r_v[0:1, :], r_v[0:1, :], AF.Sqrt), deps=[t5])
        t7 = fw.op('vector', lambda e: e.reciprocal(r_t[0:1, :], r_v[0:1, :]), deps=[t6])
        pp.done(i1, [t1])
        pp.done(i2, [t2])
        bi1, bp1, tb1 = bcast_row(C, r_mu[0:1, :], 512, [t1, t3])
        bi2, bp2, tb2 = bcast_row(C, r_t[0:1, :], 512, [t7])
        e1 = fw.op('scalar', lambda e, o=mub[:, sl], p=bp1: e.activation(o, p[:], AF.Identity), deps=[tb1])
        e2 = fw.op('scalar', lambda e, o=rstdb[:, sl], p=bp2: e.activation(o, p[:], AF.Identity), deps=[tb2])
        pp.done(bi1, [e1])
        pp.done(bi2, [e2])
        prev_sq_reader = prev_sq_reader
        toks += [e1, e2]
    return mub, rstdb, toks


def out_phase(C, L, seq, have_delta):
    fw, pp, sb, dr = C.fw, C.pp, C.sb, C.dr
    first = (L == 0)
    last = (L == C.upto - 1)
    sb.push()
    xnew = sb.alloc(8 * 2048)
    xv = xnew.rearrange("p (c t) -> p c t", c=8)
    xsrc = dr['XIN'] if first else dr['xres']
    xdst = dr['OUT'] if last else dr['xres']
    xa = sb.alloc(512)
    da = sb.alloc(512)
    tiles, ttoks = {}, {}
    prev = {}
    for tt in range(NTT):
        sl = slice(tt * 512, (tt + 1) * 512)
        for mc in range(NCH):
            t1 = fw.dma('sync', xa, xsrc[seq, mc, :, sl], deps=[prev.get('xa')])
            if have_delta:
                t2 = fw.dma('sync', da, dr['dlt'][seq, mc, :, sl], deps=[prev.get('da')], ch=1)
                t3 = fw.op('vector', lambda e, o=xv[:, mc, sl]: e.tensor_tensor(o, xa, da, ALU.add), deps=[t1, t2])
                prev['da'] = t3
            else:
                t3 = fw.op('vector', lambda e, o=xv[:, mc, sl]: e.tensor_copy(o, xa), deps=[t1])
            prev['xa'] = t3
            tiles[(mc, tt)] = xv[:, mc, sl]
            ttoks[(mc, tt)] = t3
            fw.dma('sync', xdst[seq, mc, :, sl], xv[:, mc, sl], deps=[t3], ch=2)
    if not last:
        mub, rstdb, stoks = ln_norm_tiles(C, tiles, ttoks, [])
        tmp = sb.alloc(512)
        ptmp = None
        for tt in range(NTT):
            sl = slice(tt * 512, (tt + 1) * 512)
            for mc in range(NCH):
                t1 = fw.op('vector', lambda e, a=tiles[(mc, tt)], m=mub[:, sl]: e.tensor_tensor(tmp, a, m, ALU.subtract),
                           deps=stoks + [ttoks[(mc, tt)], ptmp])
                t2 = fw.op('vector', lambda e, r=rstdb[:, sl]: e.tensor_tensor(tmp, tmp, r, ALU.mult), deps=[t1])
                ptmp = fw.dma('sync', dr['xnh'][seq, mc, :, sl], tmp, deps=[t2], ch=2)
    fw.barrier()
    sb.pop()


def load_wblock(C, dst, dram2d, r0, r1, c0, c1, deps=()):
    nr = r1 - r0
    src = dram2d[r0:r1, c0:c1].rearrange("(c p) n -> p c n", p=128)
    return C.fw.dma('gpsimd', dst, src, deps=deps)


def moe_phase(C, L, seq):
    import os
    dbg = os.environ.get('MOE_DBG', 'rse')
    fw, pp, sb, dr = C.fw, C.pp, C.sb, C.dr
    E = 32 if L == 0 else 64
    NG = 8
    GS = E // NG
    TOPK = 6 if L == 0 else 8
    gwT = dr['L0_gwT'] if L == 0 else dr['L1_rwT']
    ebias = dr['L0_ebias'] if L == 0 else dr['L1_ebias']
    guD = dr['L0_guT'] if L == 0 else dr['L1_gu']
    dnD = dr['L0_dnT'] if L == 0 else dr['L1_dn']
    sg, su, sd = (dr[f'L{L}_sgT'], dr[f'L{L}_suT'], dr[f'L{L}_sdT'])
    sb.push()
    acc = sb.alloc(8 * 2048)
    accv = acc.rearrange("p (c t) -> p c t", c=8)
    xnb = sb.alloc(8 * 1024, BF16)
    xnbv = xnb.rearrange("p (c t) -> p c t", c=8)
    coef = sb.alloc(2048)
    ldx = [fw.dma('gpsimd', xnbv[:, mc, :], dr['xnh'][seq, mc]) for mc in range(NCH)]

    if 'r' not in dbg:
        tzz = fw.memset('vector', coef[0:E, :], 0.01, deps=[fw.tok('scalar')])
        fw.dma('sync', dr['coefh'][0:E, :], coef[0:E, :], deps=[tzz])
        fw.barrier()
    else:
        moe_routing(C, L, seq, E, NG, GS, TOPK, gwT, ebias, coef)
    if 's' in dbg:
        moe_shared(C, seq, sg, su, sd, acc, accv, xnbv, ldx)
    else:
        tzz = fw.memset('vector', acc, 0.0)
        fw.barrier()
    if 'e' in dbg:
        moe_experts(C, L, seq, E, guD, dnD, acc, accv, xnbv)
    for mc in range(NCH):
        fw.dma('sync', dr['dlt'][seq, mc], accv[:, mc, :], deps=[fw.tok('vector'), fw.tok('scalar')])
    fw.barrier()
    sb.pop()
    return


def moe_routing(C, L, seq, E, NG, GS, TOPK, gwT, ebias, coef):
    fw, pp, sb, dr = C.fw, C.pp, C.sb, C.dr
    sb.push()
    rw = sb.alloc(8 * E)
    rwv = rw.rearrange("p (c e) -> p c e", c=8)
    trw = fw.dma('sync', rwv, gwT.rearrange("(c p) e -> p c e", p=128))
    ebrow = sb.alloc(E)
    teb = fw.dma('sync', ebrow[0:1, :], ebias)
    ebb = sb.alloc(E)
    nib = sb.alloc(E)
    bi, bp, tb = bcast_row(C, ebrow[0:1, :], E, [teb])
    t = fw.op('scalar', lambda e: e.activation(ebb, bp[:, 0:E], AF.Identity), deps=[tb])
    pp.done(bi, [t])
    bi, bp, tb = bcast_row(C, C.negidx[0:1, 0:E], E, [t])
    t = fw.op('scalar', lambda e: e.activation(nib, bp[:, 0:E], AF.Identity), deps=[tb])
    pp.done(bi, [t])
    xf = [sb.alloc(128) for _ in range(2)]
    score = sb.alloc(E)
    sfc = sb.alloc(E)
    gsc = sb.alloc(NG)
    gwork = sb.alloc(NG)
    gmask = sb.alloc(NG)
    gtmp = sb.alloc(E)
    msk = sb.alloc(E)
    sel = sb.alloc(E)
    wrow = sb.alloc(E)
    colp = sb.alloc(4)
    prevx = [None, None]
    prev_tr = None
    for tc in range(NTC):
        ip, ps, rel = pp.get()
        lastmm = None
        for mc in range(NCH):
            xa = xf[mc % 2]
            tl = fw.dma('sync', xa, dr['xnh'][seq, mc, :, tc * 128:(tc + 1) * 128],
                        deps=[prevx[mc % 2]], ch=mc % 2)
            lastmm = fw.op('tensor', lambda e, a=xa, s=(mc == 0): e.matmul(
                ps[:, 0:E], a, rwv[:, mc, :], start=s, stop=(mc == NCH - 1)),
                deps=[tl, trw] + rel)
            prevx[mc % 2] = lastmm
        t0 = fw.op('scalar', lambda e: e.activation(score, ps[:, 0:E], AF.Sigmoid), deps=[lastmm, fw.tok('vector')])
        pp.done(ip, [t0])
        t1 = fw.op('vector', lambda e: e.tensor_tensor(sfc, score, ebb, ALU.add), deps=[t0])
        sfc3 = sfc.rearrange("p (g s) -> p g s", g=NG)
        if L == 0:
            tg = fw.op('vector', lambda e: e.reduce_max(gsc.rearrange("p (g o) -> p g o", g=NG), sfc3, AX.X), deps=[t1])
        else:
            gt3 = gtmp.rearrange("p (g s) -> p g s", g=NG)
            tm1 = fw.op('vector', lambda e: e.reduce_max(gwork.rearrange("p (g o) -> p g o", g=NG), sfc3, AX.X), deps=[t1])
            to = fw.op('vector', lambda e: e.tensor_tensor(
                gt3, sfc3, gwork.rearrange("p (g o) -> p g o", g=NG).broadcast_to([128, NG, GS]), ALU.is_ge), deps=[tm1])
            tv = fw.op('vector', lambda e: e.scalar_tensor_tensor(gt3, gt3, -1e30, sfc3, ALU.mult, ALU.add), deps=[to])
            tm2 = fw.op('vector', lambda e: e.reduce_max(gsc.rearrange("p (g o) -> p g o", g=NG), gt3, AX.X), deps=[tv])
            tg = fw.op('vector', lambda e: e.tensor_tensor(gsc, gsc, gwork, ALU.add), deps=[tm2])
        tg = fw.op('vector', lambda e: e.tensor_copy(gwork, gsc), deps=[tg])
        tg = fw.memset('vector', gmask, 0.0, deps=[tg])
        for r in range(4):
            tg = fw.op('vector', lambda e: e.reduce_max(colp[:, 0:1], gwork, AX.X), deps=[tg])
            tg = fw.op('vector', lambda e: e.tensor_scalar(gtmp[:, 0:NG], gwork, colp[:, 0:1], None, ALU.is_ge), deps=[tg])
            tg = fw.op('vector', lambda e: e.tensor_tensor(gmask, gmask, gtmp[:, 0:NG], ALU.add), deps=[tg])
            tg = fw.op('vector', lambda e: e.scalar_tensor_tensor(gwork, gtmp[:, 0:NG], -1e30, gwork, ALU.mult, ALU.add), deps=[tg])
        g3 = gmask.rearrange("p (g o) -> p g o", g=NG)
        m3 = msk.rearrange("p (g s) -> p g s", g=NG)
        tg = fw.op('vector', lambda e: e.tensor_tensor(m3, sfc3, g3.broadcast_to([128, NG, GS]), ALU.mult), deps=[tg, t1])
        tg2 = fw.op('vector', lambda e: e.tensor_scalar(gwork, gmask, -1.0, 1.0, ALU.mult, ALU.add), deps=[tg])
        tg2 = fw.op('vector', lambda e: e.tensor_tensor(
            gtmp.rearrange("p (g s) -> p g s", g=NG), nib.rearrange("p (g s) -> p g s", g=NG),
            gwork.rearrange("p (g o) -> p g o", g=NG).broadcast_to([128, NG, GS]), ALU.mult), deps=[tg2])
        tg = fw.op('vector', lambda e: e.tensor_tensor(msk, msk, gtmp, ALU.add), deps=[tg2])
        ts_ = fw.memset('vector', sel, 0.0, deps=[tg])
        for r in range(TOPK):
            ts_ = fw.op('vector', lambda e: e.reduce_max(colp[:, 0:1], msk, AX.X), deps=[ts_])
            ts_ = fw.op('vector', lambda e: e.tensor_scalar(gtmp[:, 0:E] if False else wrow, msk, colp[:, 0:1], None, ALU.is_ge), deps=[ts_])
            ts_ = fw.op('vector', lambda e: e.tensor_tensor(sel, sel, wrow, ALU.add), deps=[ts_])
            ts_ = fw.op('vector', lambda e: e.scalar_tensor_tensor(msk, wrow, -1e30, msk, ALU.mult, ALU.add), deps=[ts_])
        ts_ = fw.op('vector', lambda e: e.tensor_tensor(wrow, sel, score, ALU.mult), deps=[ts_, t0])
        ts_ = fw.op('vector', lambda e: e.reduce_sum(colp[:, 1:2], wrow, AX.X), deps=[ts_])
        ts_ = fw.op('vector', lambda e: e.tensor_scalar(colp[:, 1:2], colp[:, 1:2], 1e-20, None, ALU.add), deps=[ts_])
        ts_ = fw.op('vector', lambda e: e.reciprocal(colp[:, 2:3], colp[:, 1:2]), deps=[ts_])
        ts_ = fw.op('vector', lambda e: e.tensor_scalar(wrow, wrow, colp[:, 2:3], 2.5, ALU.mult, ALU.mult), deps=[ts_])
        it, pst, relt = pp.get()
        tt_ = fw.op('tensor', lambda e: e.transpose(pst[0:E, 0:128], wrow, C.idn), deps=[ts_] + relt)
        tev = fw.op('scalar', lambda e, o=coef[0:E, tc * 128:(tc + 1) * 128]: e.activation(o, pst[0:E, 0:128], AF.Identity),
                    deps=[tt_, prev_tr])
        pp.done(it, [tev])
        prev_tr = tev
    tch = fw.dma('sync', dr['coefh'][0:E, :], coef[0:E, :], deps=[prev_tr])
    fw.barrier()
    sb.pop()


def moe_shared(C, seq, sg, su, sd, acc, accv, xnbv, ldx):
    fw, pp, sb, dr = C.fw, C.pp, C.sb, C.dr
    sb.push()
    hsh = sb.alloc(8 * 1024, BF16)
    hshv = hsh.rearrange("p (c t) -> p c t", c=8)
    wga = [sb.alloc(8 * 64, BF16) for _ in range(2)]
    wua = [sb.alloc(8 * 64, BF16) for _ in range(2)]
    tmpf = sb.alloc(512)
    relg = [None, None]
    for ic in range(NCH):
        s = ic % 2
        wg = wga[s].rearrange("p (c n) -> p c n", c=8)
        wu = wua[s].rearrange("p (c n) -> p c n", c=8)
        tg = fw.dma('gpsimd', wg, sg[0:1024, ic * 128:(ic + 1) * 128].rearrange("(c p) n -> p c n", p=128), deps=[relg[s]], ch=s)
        tu = fw.dma('gpsimd', wu, su[0:1024, ic * 128:(ic + 1) * 128].rearrange("(c p) n -> p c n", p=128), deps=[relg[s]], ch=2 + s)
        lastu = None
        for tt in range(NTT):
            sl = slice(tt * 512, (tt + 1) * 512)
            ig, pg, rg = pp.get()
            iu, pu, ru = pp.get()
            for mc in range(NCH):
                fw.op('tensor', lambda e, m=mc, s0=(mc == 0): e.matmul(
                    pg[:], wg[:, m, :], xnbv[:, m, sl], start=s0, stop=(m == NCH - 1)),
                    deps=[tg] + ldx + rg)
            for mc in range(NCH):
                lastu = fw.op('tensor', lambda e, m=mc, s0=(mc == 0): e.matmul(
                    pu[:], wu[:, m, :], xnbv[:, m, sl], start=s0, stop=(m == NCH - 1)),
                    deps=[tu] + ru)
            t1 = fw.op('scalar', lambda e: e.activation(tmpf, pg[:], AF.Silu), deps=[lastu, fw.tok('vector')])
            t2 = fw.op('vector', lambda e, o=hshv[:, ic, sl]: e.tensor_tensor(o, tmpf, pu[:], ALU.mult), deps=[t1])
            pp.done(ig, [t1])
            pp.done(iu, [t2])
        relg[s] = fw.tok('tensor')
    wda = [sb.alloc(8 * 64, BF16) for _ in range(2)]
    reld = [None, None]
    hold = fw.tok('vector')
    accw = []
    for mc in range(NCH):
        s = mc % 2
        wd = wda[s].rearrange("p (c n) -> p c n", c=8)
        td = fw.dma('gpsimd', wd, sd[0:1024, mc * 128:(mc + 1) * 128].rearrange("(c p) n -> p c n", p=128), deps=[reld[s]], ch=s)
        for tt in range(NTT):
            sl = slice(tt * 512, (tt + 1) * 512)
            idx, pd, rd = pp.get()
            last = None
            for ic in range(NCH):
                last = fw.op('tensor', lambda e, i=ic, s0=(ic == 0): e.matmul(
                    pd[:], wd[:, i, :], hshv[:, i, sl], start=s0, stop=(i == NCH - 1)),
                    deps=[td, hold] + rd)
            t1 = fw.op('scalar', lambda e, o=accv[:, mc, sl]: e.activation(o, pd[:], AF.Identity), deps=[last])
            pp.done(idx, [t1])
            accw.append(t1)
        reld[s] = fw.tok('tensor')
    fw.barrier()
    sb.pop()


def moe_experts(C, L, seq, E, guD, dnD, acc, accv, xnbv):
    fw, pp, sb, dr = C.fw, C.pp, C.sb, C.dr
    sb.push()
    gus = [sb.alloc(8 * 512, BF16) for _ in range(2)]
    dns = [sb.alloc(4 * 512, BF16) for _ in range(2)]
    hbuf = sb.alloc(4 * 256, BF16)
    hv = hbuf.rearrange("p (c t) -> p c t", c=4)
    cbf = sb.alloc(2048)
    crow = sb.alloc(512)
    crow_rel = []
    tmps = sb.alloc(512)
    tmp2 = sb.alloc(512)
    wrel = [[], []]
    t0_ = fw.tok('scalar')
    acc_tok = {(mc, tt): t0_ for mc in range(NCH) for tt in range(NTT)}
    cbf_rel = []
    h_rel = []
    prev_silu = None
    prev_tmp2 = None
    prev_t3 = None
    for e_ in range(E):
        s = e_ % 2
        gv = gus[s].rearrange("p (c n) -> p c n", c=8)
        dv = dns[s].rearrange("p (c n) -> p c n", c=4)
        tw1 = fw.dma('gpsimd', gv, guD[e_].rearrange("(c p) n -> p c n", p=128), deps=wrel[s], ch=s)
        tw2 = fw.dma('gpsimd', dv, dnD[e_].rearrange("(c p) n -> p c n", p=128), deps=wrel[s], ch=2 + s)
        cbts = []
        for tt in range(NTT):
            sl = slice(tt * 512, (tt + 1) * 512)
            tl = fw.dma('sync', crow[0:1, :], dr['coefh'][e_, sl], deps=crow_rel)
            bi, bp, tb = bcast_row(C, crow[0:1, :], 512, cbf_rel + [tl])
            crow_rel = [tb]
            t = fw.op('scalar', lambda e, o=cbf[:, sl], p=bp: e.activation(o, p[:], AF.Identity),
                      deps=[tb] + cbf_rel)
            pp.done(bi, [t])
            cbts.append(t)
        cbf_rel = []
        for tt in range(NTT):
            sl = slice(tt * 512, (tt + 1) * 512)
            htoks = []
            for icg in range(4):
                ig, pg, rg = pp.get()
                iu, pu, ru = pp.get()
                lastg = lastu = None
                for mc in range(NCH):
                    lastg = fw.op('tensor', lambda e, m=mc, i=icg, s0=(mc == 0): e.matmul(
                        pg[:], gv[:, m, i * 128:(i + 1) * 128], xnbv[:, m, sl],
                        start=s0, stop=(m == NCH - 1)), deps=[tw1] + rg)
                for mc in range(NCH):
                    lastu = fw.op('tensor', lambda e, m=mc, i=icg, s0=(mc == 0): e.matmul(
                        pu[:], gv[:, m, 512 + i * 128:512 + (i + 1) * 128], xnbv[:, m, sl],
                        start=s0, stop=(m == NCH - 1)), deps=ru)
                t1 = fw.op('scalar', lambda e: e.activation(tmps, pg[:], AF.Silu),
                           deps=[lastg, prev_tmp2])
                t2 = fw.op('vector', lambda e: e.tensor_tensor(tmp2, tmps, pu[:], ALU.mult),
                           deps=[t1, lastu, prev_t3])
                t3 = fw.op('vector', lambda e, o=hv[:, icg, :], c=cbf[:, sl]: e.tensor_tensor(o, tmp2, c, ALU.mult),
                           deps=[t2, cbts[tt]] + h_rel)
                pp.done(ig, [t1])
                pp.done(iu, [t2])
                prev_t3 = t3
                prev_tmp2 = t2
                cbf_rel.append(t3)
                htoks.append(t3)
            for mc in range(NCH):
                idx2, pd, rd = pp.get()
                last = None
                for ic in range(4):
                    last = fw.op('tensor', lambda e, i=ic, m=mc, s0=(ic == 0): e.matmul(
                        pd[:], dv[:, i, m * 128:(m + 1) * 128], hv[:, i, :],
                        start=s0, stop=(i == 3)), deps=[tw2] + rd + htoks)
                t4 = fw.op('vector', lambda e, o=accv[:, mc, sl]: e.tensor_tensor(o, o, pd[:], ALU.add),
                           deps=[last, acc_tok[(mc, tt)]])
                pp.done(idx2, [t4])
                acc_tok[(mc, tt)] = t4
            h_rel = [fw.tok('tensor')]
        wrel[s] = [fw.tok('tensor')]
    fw.barrier()
    sb.pop()


def l0_attn_phase(C, seq):
    fw, pp, sb, dr = C.fw, C.pp, C.sb, C.dr
    sb.push()
    o_sb = sb.alloc(8 * 1024, BF16)
    ov = o_sb.rearrange("p (h t) -> p h t", h=8)
    sb.push()
    q_fm = sb.alloc(8 * 1024, BF16)
    qfv = q_fm.rearrange("p (h t) -> p h t", h=8)
    k_fm = sb.alloc(8 * 1024, BF16)
    kfv = k_fm.rearrange("p (h t) -> p h t", h=8)
    v_tm = sb.alloc(16 * 512, BF16)
    vtv = v_tm.rearrange("p (c d) -> p c d", c=16)
    cosv = sb.alloc(16 * 64)
    cv = cosv.rearrange("p (c d) -> p c d", c=16)
    sinv = sb.alloc(16 * 64)
    sv = sinv.rearrange("p (c d) -> p c d", c=16)
    tcs = []
    for tc in range(NTC):
        tcs.append(fw.dma('sync', cv[:, tc, :], dr['COS0'][tc * 128:(tc + 1) * 128, :]))
        tcs.append(fw.dma('sync', sv[:, tc, :], dr['SIN0'][tc * 128:(tc + 1) * 128, :]))
    sb.push()
    xns = [sb.alloc(8 * 64, BF16) for _ in range(2)]
    wslot = sb.alloc(8 * 512, BF16)
    wv = wslot.rearrange("p (c n) -> p c n", c=8)
    qtmp = sb.alloc(1024)
    q3 = qtmp.rearrange("p (h d) -> p h d", h=8)
    qsq = sb.alloc(1024)
    qq3 = qsq.rearrange("p (h d) -> p h d", h=8)
    ssc = sb.alloc(16)
    qrt = sb.alloc(1024)
    r3 = qrt.rearrange("p (h d) -> p h d", h=8)
    qb = sb.alloc(256, BF16)

    def qkv_pass(kind, wdram, dstv):
        tw = fw.dma('gpsimd', wv, wdram.rearrange("(c p) n -> p c n", p=128),
                    deps=[fw.tok('tensor')])
        prevx = [None, None]
        prev_cons = None
        for tc in range(NTC):
            s_ = tc % 2
            xa = xns[s_].rearrange("p (c n) -> p c n", c=8)
            tl = fw.dma('gpsimd', xa, dr['xnh'][seq].rearrange("c p t -> p c t")[:, :, tc * 128:(tc + 1) * 128],
                        deps=[prevx[s_]], ch=s_)
            if kind == 'v':
                ip, ps, rel = pp.get()
                ip2, ps2, rel2 = pp.get()
                last = None
                for mc in range(NCH):
                    fw.op('tensor', lambda e, m=mc, s0=(mc == 0): e.matmul(
                        ps[:], xa[:, m, :], wv[:, m, 0:512], start=s0, stop=(m == NCH - 1)),
                        deps=[tl, tw] + rel)
                for mc in range(NCH):
                    last = fw.op('tensor', lambda e, m=mc, s0=(mc == 0): e.matmul(
                        ps2[:], xa[:, m, :], wv[:, m, 512:1024], start=s0, stop=(m == NCH - 1)),
                        deps=rel2)
                prevx[s_] = last
                t1 = fw.op('scalar', lambda e, o=vtv[:, tc, 0:512]: e.activation(o, ps[:], AF.Identity), deps=[last])
                t2 = fw.op('scalar', lambda e, o=vtv[:, tc, 512:1024]: e.activation(o, ps2[:], AF.Identity), deps=[last])
                pp.done(ip, [t1])
                pp.done(ip2, [t2])
                continue
            ip, ps, rel = pp.get()
            ip2, ps2, rel2 = pp.get()
            last = None
            for mc in range(NCH):
                fw.op('tensor', lambda e, m=mc, s0=(mc == 0): e.matmul(
                    ps[:], xa[:, m, :], wv[:, m, 0:512], start=s0, stop=(m == NCH - 1)),
                    deps=[tl, tw] + rel)
            for mc in range(NCH):
                last = fw.op('tensor', lambda e, m=mc, s0=(mc == 0): e.matmul(
                    ps2[:], xa[:, m, :], wv[:, m, 512:1024], start=s0, stop=(m == NCH - 1)),
                    deps=rel2)
            prevx[s_] = last
            t1 = fw.op('scalar', lambda e: e.activation(qtmp[:, 0:512], ps[:], AF.Identity), deps=[last, prev_cons])
            t2 = fw.op('scalar', lambda e: e.activation(qtmp[:, 512:1024], ps2[:], AF.Identity), deps=[last, prev_cons])
            pp.done(ip, [t1])
            pp.done(ip2, [t2])
            t3 = fw.op('scalar', lambda e: e.activation(qsq, qtmp, AF.Square), deps=[t1, t2])
            t4 = fw.op('vector', lambda e: e.reduce_sum(ssc[:, 0:8].rearrange("p (h o) -> p h o", h=8), qq3, AX.X), deps=[t3])
            t4 = fw.op('vector', lambda e: e.tensor_scalar(ssc[:, 0:8], ssc[:, 0:8], 1.0 / 128.0, 1e-6, ALU.mult, ALU.add), deps=[t4])
            t4 = fw.op('scalar', lambda e: e.activation(ssc[:, 8:16], ssc[:, 0:8], AF.Sqrt), deps=[t4])
            t4 = fw.op('vector', lambda e: e.reciprocal(ssc[:, 0:8], ssc[:, 8:16]), deps=[t4])
            tn = []
            for h in range(8):
                tn.append(fw.op('vector', lambda e, h=h: e.tensor_scalar(
                    q3[:, h, :], q3[:, h, :], ssc[:, h:h + 1], None, ALU.mult), deps=[t4]))
            cb = cv[:, tc:tc + 1, :].broadcast_to([128, 8, 64])
            sbt = sv[:, tc:tc + 1, :].broadcast_to([128, 8, 64])
            ua = fw.op('vector', lambda e: e.tensor_tensor(r3[:, :, 0:64], q3[:, :, 0:64], cb, ALU.mult), deps=tn + tcs)
            ub = fw.op('vector', lambda e: e.tensor_tensor(qq3[:, :, 0:64], q3[:, :, 64:128], sbt, ALU.mult), deps=tn + [t3])
            uc = fw.op('vector', lambda e: e.tensor_tensor(r3[:, :, 0:64], r3[:, :, 0:64], qq3[:, :, 0:64], ALU.subtract), deps=[ua, ub])
            ud = fw.op('vector', lambda e: e.tensor_tensor(r3[:, :, 64:128], q3[:, :, 0:64], sbt, ALU.mult), deps=tn)
            ue = fw.op('vector', lambda e: e.tensor_tensor(qq3[:, :, 64:128], q3[:, :, 64:128], cb, ALU.mult), deps=tn)
            uf = fw.op('vector', lambda e: e.tensor_tensor(r3[:, :, 64:128], r3[:, :, 64:128], qq3[:, :, 64:128], ALU.add), deps=[ud, ue])
            dv = qfv if kind == 'q' else kfv
            qb_rel = None
            for hh in range(2):
                g1 = fw.op('vector', lambda e, hh=hh: e.tensor_copy(
                    qb[:, 0:512], qrt[:, hh * 512:(hh + 1) * 512]), deps=[uc, uf, qb_rel])
                for h2 in range(4):
                    it, pst, relt = pp.get()
                    pb16a = pst[:, 0:64].bitcast(BF16)
                    tt_ = fw.op('tensor', lambda e, h2=h2, pb16a=pb16a: e.transpose(
                        pb16a, qb[:, h2 * 128:(h2 + 1) * 128], C.idnb), deps=[g1] + relt)
                    h = hh * 4 + h2
                    ev = fw.op('scalar', lambda e, h=h, tc=tc, pb16a=pb16a: e.activation(
                        dv[:, h, tc * 128:(tc + 1) * 128], pb16a, AF.Identity), deps=[tt_])
                    pp.done(it, [ev])
                    qb_rel = tt_
            prev_cons = fw.tok('vector')

    qkv_pass('q', dr['L0_wq'], qfv)
    qkv_pass('k', dr['L0_wk'], kfv)
    qkv_pass('v', dr['L0_wv'], None)
    fw.barrier()
    sb.pop()
    sb.push()
    pts = [sb.alloc(256, BF16) for _ in range(4)]
    rrow = sb.alloc(512)
    rrec = sb.alloc(512)
    rb = sb.alloc(512)
    otmp = None
    prel = [[], [], [], []]
    pi = 0
    for h in range(8):
        for qt in range(NTT):
            sl = slice(qt * 512, (qt + 1) * 512)
            iR, psR, relR = pp.get(hold=True)
            iO, psO, relO = pp.get(hold=True)
            nkc = qt * 4 + 4
            lastR = lastO = None
            for kc in range(nkc):
                iS, psS, relS = pp.get()
                tm = fw.op('tensor', lambda e, kc=kc, h=h: e.matmul(
                    psS[:], kfv[:, h, kc * 128:(kc + 1) * 128], qfv[:, h, sl],
                    start=True, stop=True), deps=relS)
                mts = [tm]
                for m in range(4):
                    qc = qt * 4 + m
                    if kc == qc:
                        mts.append(fw.op('vector', lambda e, m=m: e.tensor_tensor(
                            psS[:, m * 128:(m + 1) * 128], psS[:, m * 128:(m + 1) * 128], C.trig, ALU.add), deps=[tm]))
                    elif kc > qc:
                        mts.append(fw.op('vector', lambda e, m=m: e.tensor_scalar(
                            psS[:, m * 128:(m + 1) * 128], psS[:, m * 128:(m + 1) * 128], -1e30, None, ALU.add), deps=[tm]))
                pt = pts[pi % 4]
                texp = fw.op('scalar', lambda e, pt=pt: e.activation(pt, psS[:], AF.Exp), deps=mts + prel[pi % 4])
                pp.done(iS, [texp])
                lastR = fw.op('tensor', lambda e, pt=pt, s0=(kc == 0): e.matmul(
                    psR[0:1, :], C.onescb, pt, start=s0, stop=(kc == nkc - 1)), deps=[texp] + (relR if kc == 0 else []))
                lastO = fw.op('tensor', lambda e, pt=pt, kc=kc, h=h, s0=(kc == 0): e.matmul(
                    psO[:], vtv[:, kc, h * 128:(h + 1) * 128], pt, start=s0, stop=(kc == nkc - 1)), deps=[texp] + (relO if kc == 0 else []))
                prel[pi % 4] = [lastR, lastO]
                pi += 1
            t1 = fw.op('vector', lambda e: e.tensor_copy(rrow[0:1, :], psR[0:1, :]), deps=[lastR])
            pp.done(iR, [t1])
            t2 = fw.op('vector', lambda e: e.reciprocal(rrec[0:1, :], rrow[0:1, :]), deps=[t1])
            bi, bp, tb = bcast_row(C, rrec[0:1, :], 512, [t2])
            t3 = fw.op('scalar', lambda e: e.activation(rb, bp[:], AF.Identity), deps=[tb, otmp])
            pp.done(bi, [t3])
            t4 = fw.op('vector', lambda e, h=h: e.tensor_tensor(ov[:, h, sl], psO[:], rb, ALU.mult), deps=[t3, lastO])
            pp.done(iO, [t4])
            otmp = t4
    fw.barrier()
    sb.pop()
    sb.pop()
    sb.push()
    wslot = sb.alloc(8 * 512, BF16)
    wv2 = wslot.rearrange("p (c n) -> p c n", c=8)
    tw = fw.dma('gpsimd', wv2, dr['L0_wo'].rearrange("(c p) n -> p c n", p=128))
    tmpo = sb.alloc(512)
    dta = sb.alloc(512)
    pv1 = pv2 = None
    for mc in range(NCH):
        for tt in range(NTT):
            sl = slice(tt * 512, (tt + 1) * 512)
            ip, ps, rel = pp.get()
            last = None
            for h in range(8):
                last = fw.op('tensor', lambda e, h=h, m=mc, s0=(h == 0): e.matmul(
                    ps[:], wv2[:, h, m * 128:(m + 1) * 128], ov[:, h, sl],
                    start=s0, stop=(h == NCH - 1)), deps=[tw] + rel)
            t1 = fw.op('scalar', lambda e: e.activation(tmpo, ps[:], AF.Identity), deps=[last, pv1])
            pp.done(ip, [t1])
            t2 = fw.dma('sync', dta, dr['dlt'][seq, mc, :, sl], deps=[pv2], ch=1)
            t3 = fw.op('vector', lambda e: e.tensor_tensor(dta, dta, tmpo, ALU.add), deps=[t1, t2])
            pv1 = t3
            pv2 = fw.dma('sync', dr['dlt'][seq, mc, :, sl], dta, deps=[t3], ch=2)
    fw.barrier()
    sb.pop()
    sb.pop()


def build_program(weights, consts, upto=4, skip=()):
    nc = bass.Bass("TRN2", target_bir_lowering=False)
    C = Ctx()
    C.upto = upto
    C.nc = nc
    C.dr = mk_dram(nc, weights, consts)
    with ExitStack() as stk:
        E = stk.enter_context
        ar = E(nc.sbuf_tensor("arena", [128, 48800], F32))
        psums = [E(nc.psum_tensor(f"ps{i}", [128, 512], F32)) for i in range(8)]
        sems = {e: E(nc.semaphore(f"sem_{e}")) for e in ALLSEM}
        blk = E(nc.Block())
        C.fw = FW(nc)
        C.fw.sems = sems
        C.sb = SB(ar, 48800)
        C.pp = PP([p[:] for p in psums])
        emit_statics(C)
        for L in range(upto):
            if L == 0:
                for seq in range(NSEQ):
                    pre_ln(C, seq)
            if L == 2 and (2, 'attn') not in skip:
                l2_filter_phase(C)
            for seq in range(NSEQ):
                hav_f = (L, 'ffn') not in skip
                hav_a = (L, 'attn') not in skip
                if hav_f:
                    if L in (0, 1):
                        moe_phase(C, L, seq)
                    elif L == 3:
                        l3_ffn_phase(C, seq)
                    else:
                        l2_ffn_phase(C, seq)
                elif hav_a:
                    zero_dlt(C, seq)
                if hav_a:
                    if L == 0:
                        l0_attn_phase(C, seq)
                    elif L == 1:
                        l1_attn_phase(C, seq)
                    elif L == 2:
                        l2_attn_phase(C, seq)
                    else:
                        l3_attn_phase(C, seq)
            for seq in range(NSEQ):
                out_phase(C, L, seq, have_delta=((L, 'ffn') not in skip or (L, 'attn') not in skip))
        ft = C.fw.all_toks()
        C.fw.finalize(blk, final_deps=ft)
    return nc


def zero_dlt(C, seq):
    fw, sb, dr = C.fw, C.sb, C.dr
    sb.push()
    z = sb.alloc(2048)
    t = fw.memset('vector', z, 0.0)
    for mc in range(NCH):
        fw.dma('sync', dr['dlt'][seq, mc], z, deps=[t])
    fw.barrier()
    sb.pop()


def pre_ln(C, seq):
    fw, sb, dr = C.fw, C.sb, C.dr
    sb.push()
    xbuf = sb.alloc(8 * 2048)
    xv = xbuf.rearrange("p (c t) -> p c t", c=8)
    tiles, ttoks = {}, {}
    for tt in range(NTT):
        sl = slice(tt * 512, (tt + 1) * 512)
        for mc in range(NCH):
            t1 = fw.dma('sync', xv[:, mc, sl], dr['XIN'][seq, mc, :, sl])
            tiles[(mc, tt)] = xv[:, mc, sl]
            ttoks[(mc, tt)] = t1
    mub, rstdb, stoks = ln_norm_tiles(C, tiles, ttoks, [])
    tmp = sb.alloc(512)
    ptmp = None
    for tt in range(NTT):
        sl = slice(tt * 512, (tt + 1) * 512)
        for mc in range(NCH):
            t1 = fw.op('vector', lambda e, a=tiles[(mc, tt)], m=mub[:, sl]: e.tensor_tensor(tmp, a, m, ALU.subtract),
                       deps=stoks + [ttoks[(mc, tt)], ptmp])
            t2 = fw.op('vector', lambda e, r=rstdb[:, sl]: e.tensor_tensor(tmp, tmp, r, ALU.mult), deps=[t1])
            ptmp = fw.dma('sync', dr['xnh'][seq, mc, :, sl], tmp, deps=[t2], ch=2)
    fw.barrier()
    sb.pop()


def l1_attn_phase(C, seq):
    fw, pp, sb, dr = C.fw, C.pp, C.sb, C.dr
    sb.push()
    o_sb = sb.alloc(8 * 2048)
    osv = o_sb.rearrange("p (c t) -> p c t", c=8)
    for hg in range(2):
        sb.push()
        qf = sb.alloc(4 * 1024, BF16)
        qfv = qf.rearrange("p (c t) -> p c t", c=4)
        qnf = sb.alloc(4 * 1024, BF16)
        qnv = qnf.rearrange("p (c t) -> p c t", c=4)
        kf = sb.alloc(4 * 1024, BF16)
        kfv = kf.rearrange("p (c t) -> p c t", c=4)
        ksf = sb.alloc(4 * 1024, BF16)
        ksv = ksf.rearrange("p (c t) -> p c t", c=4)
        vtm = sb.alloc(16 * 256, BF16)
        vtv = vtm.rearrange("p (c d) -> p c d", c=16)
        xns = [sb.alloc(8 * 64, BF16) for _ in range(2)]
        wslot = sb.alloc(8 * 256, BF16)
        wv = wslot.rearrange("p (c n) -> p c n", c=8)
        qtmp = sb.alloc(512)
        qstmp = sb.alloc(512)
        qb = sb.alloc(256, BF16)
        qsb = sb.alloc(256, BF16)

        def qkv_pass(kind, wd):
            tw = fw.dma('gpsimd', wv, wd[:, hg * 512:(hg + 1) * 512].rearrange("(c p) n -> p c n", p=128),
                        deps=[fw.tok('tensor')])
            prevx = [None, None]
            prev_cons = None
            for tc in range(NTC):
                s_ = tc % 2
                xa = xns[s_].rearrange("p (c n) -> p c n", c=8)
                tl = fw.dma('gpsimd', xa, dr['xnh'][seq].rearrange("c p t -> p c t")[:, :, tc * 128:(tc + 1) * 128],
                            deps=[prevx[s_]], ch=s_)
                ip, ps, rel = pp.get()
                last = None
                for mc in range(NCH):
                    last = fw.op('tensor', lambda e, m=mc, s0=(mc == 0): e.matmul(
                        ps[:], xa[:, m, :], wv[:, m, :], start=s0, stop=(m == NCH - 1)),
                        deps=[tl, tw] + rel)
                prevx[s_] = last
                if kind == 'v':
                    t1 = fw.op('scalar', lambda e, tc=tc: e.activation(vtv[:, tc, :], ps[:], AF.Identity), deps=[last])
                    pp.done(ip, [t1])
                    continue
                t1 = fw.op('scalar', lambda e: e.activation(qtmp, ps[:], AF.Identity), deps=[last, prev_cons])
                pp.done(ip, [t1])
                sgn = (-1.0 / 32.0) if kind == 'q' else 1.0
                t2 = fw.op('vector', lambda e, tc=tc, sgn=sgn: e.tensor_scalar(
                    qstmp, qtmp, C.posn[:, tc:tc + 1], sgn, ALU.mult, ALU.mult), deps=[t1])
                if kind == 'q':
                    t3 = fw.op('vector', lambda e: e.tensor_scalar(qb, qtmp, 1.0 / 32.0, None, ALU.mult), deps=[t1])
                else:
                    t3 = fw.op('vector', lambda e: e.tensor_copy(qb, qtmp), deps=[t1])
                t4 = fw.op('vector', lambda e: e.tensor_copy(qsb, qstmp), deps=[t2])
                dv1 = qfv if kind == 'q' else kfv
                dv2 = qnv if kind == 'q' else ksv
                qb_rel = None
                for h2 in range(4):
                    it, pst, relt = pp.get()
                    pb16a = pst[:, 0:64].bitcast(BF16)
                    tt_ = fw.op('tensor', lambda e, h2=h2, pb16a=pb16a: e.transpose(
                        pb16a, qb[:, h2 * 128:(h2 + 1) * 128], C.idnb), deps=[t3] + relt)
                    ev = fw.op('scalar', lambda e, h2=h2, tc=tc, pb16a=pb16a: e.activation(
                        dv1[:, h2, tc * 128:(tc + 1) * 128], pb16a, AF.Identity), deps=[tt_])
                    pp.done(it, [ev])
                    it2, pst2, relt2 = pp.get()
                    pb16b = pst2[:, 0:64].bitcast(BF16)
                    tt2 = fw.op('tensor', lambda e, h2=h2, pb16b=pb16b: e.transpose(
                        pb16b, qsb[:, h2 * 128:(h2 + 1) * 128], C.idnb), deps=[t4] + relt2)
                    ev2 = fw.op('scalar', lambda e, h2=h2, tc=tc, pb16b=pb16b: e.activation(
                        dv2[:, h2, tc * 128:(tc + 1) * 128], pb16b, AF.Identity), deps=[tt2])
                    pp.done(it2, [ev2])
                    qb_rel = tt2
                prev_cons = fw.tok('tensor')
        qkv_pass('q', dr['L1_WqT'])
        qkv_pass('k', dr['L1_WkT'])
        qkv_pass('v', dr['L1_WvT'])
        fw.barrier()
        pts = [sb.alloc(256, BF16) for _ in range(4)]
        rrow = sb.alloc(512)
        rrec = sb.alloc(512)
        rbv = sb.alloc(512)
        otmp = None
        prel = [[], [], [], []]
        pi = 0
        for h in range(8):
            hp, ho = h // 2, (h % 2) * 64
            gh = hg * 8 + h
            for qt in range(NTT):
                sl = slice(qt * 512, (qt + 1) * 512)
                iR, psR, relR = pp.get(hold=True)
                iO, psO, relO = pp.get(hold=True)
                lastR = lastO = None
                for kc in range(NTC):
                    iS, psS, relS = pp.get()
                    fw.op('tensor', lambda e, kc=kc: e.matmul(
                        psS[:], ksv[ho:ho + 64, hp, kc * 128:(kc + 1) * 128],
                        qfv[ho:ho + 64, hp, sl], start=True, stop=False), deps=relS)
                    tm = fw.op('tensor', lambda e, kc=kc: e.matmul(
                        psS[:], kfv[ho:ho + 64, hp, kc * 128:(kc + 1) * 128],
                        qnv[ho:ho + 64, hp, sl], start=False, stop=True), deps=[])
                    kcm = kc - qt * 4
                    mts = [tm]
                    pt = pts[pi % 4]
                    if kcm < 0:
                        texp = fw.op('scalar', lambda e, pt=pt: e.activation(pt, psS[:], AF.Exp, scale=C.negc[:, 0:1]),
                                     deps=mts + prel[pi % 4])
                    elif kcm > 3:
                        texp = fw.op('scalar', lambda e, pt=pt: e.activation(pt, psS[:], AF.Exp),
                                     deps=mts + prel[pi % 4])
                    else:
                        tsg = fw.op('vector', lambda e, m=kcm: e.tensor_tensor(
                            psS[:, m * 128:(m + 1) * 128], psS[:, m * 128:(m + 1) * 128], C.sgn, ALU.mult), deps=[tm])
                        mts.append(tsg)
                        texp = None
                        for (a, b, neg) in [(0, kcm + 1, False), (kcm + 1, 4, True)]:
                            if a >= b:
                                continue
                            if neg:
                                texp = fw.op('scalar', lambda e, pt=pt, a=a, b=b: e.activation(
                                    pt[:, a * 128:b * 128], psS[:, a * 128:b * 128], AF.Exp, scale=C.negc[:, 0:1]),
                                    deps=mts + prel[pi % 4])
                            else:
                                texp = fw.op('scalar', lambda e, pt=pt, a=a, b=b: e.activation(
                                    pt[:, a * 128:b * 128], psS[:, a * 128:b * 128], AF.Exp),
                                    deps=mts + prel[pi % 4])
                    pp.done(iS, [texp])
                    lastR = fw.op('tensor', lambda e, pt=pt, s0=(kc == 0): e.matmul(
                        psR[0:1, :], C.onescb, pt, start=s0, stop=(kc == NTC - 1)),
                        deps=[texp] + (relR if kc == 0 else []))
                    lastO = fw.op('tensor', lambda e, pt=pt, kc=kc, h=h, s0=(kc == 0): e.matmul(
                        psO[0:64, :], vtv[:, kc, h * 64:(h + 1) * 64], pt, start=s0, stop=(kc == NTC - 1)),
                        deps=[texp] + (relO if kc == 0 else []))
                    prel[pi % 4] = [lastR, lastO]
                    pi += 1
                t1 = fw.op('vector', lambda e: e.tensor_copy(rrow[0:1, :], psR[0:1, :]), deps=[lastR])
                pp.done(iR, [t1])
                t2 = fw.op('vector', lambda e: e.reciprocal(rrec[0:1, :], rrow[0:1, :]), deps=[t1])
                bi, bp, tb = bcast_row(C, rrec[0:1, :], 512, [t2])
                t3 = fw.op('scalar', lambda e: e.activation(rbv[0:64, :], bp[0:64, :], AF.Identity), deps=[tb, otmp])
                pp.done(bi, [t3])
                t4 = fw.op('vector', lambda e, gh=gh: e.tensor_tensor(
                    osv[ho:ho + 64, gh // 2, sl], psO[0:64, :], rbv[0:64, :], ALU.mult), deps=[t3, lastO])
                pp.done(iO, [t4])
                otmp = t4
        fw.barrier()
        sb.pop()
    sb.push()
    vt = fw.tok('vector')
    tiles = {(mc, tt): osv[:, mc, tt * 512:(tt + 1) * 512] for mc in range(NCH) for tt in range(NTT)}
    ttoks = {k: vt for k in tiles}
    mub, rstdb, stoks = ln_norm_tiles(C, tiles, ttoks, [])
    gcol = sb.alloc(8)
    bcol = sb.alloc(8)
    tgb = None
    for mc in range(NCH):
        fw.dma('sync', gcol[:, mc:mc + 1], dr['L1_lng'][mc])
        tgb = fw.dma('sync', bcol[:, mc:mc + 1], dr['L1_lnb'][mc])
    tmp = sb.alloc(512)
    tmpo = sb.alloc(512)
    dta = sb.alloc(512)
    pv1 = pv2 = pv0 = None
    for mc in range(NCH):
        for tt in range(NTT):
            sl = slice(tt * 512, (tt + 1) * 512)
            t1 = fw.op('vector', lambda e, a=tiles[(mc, tt)], m=mub[:, sl]: e.tensor_tensor(tmp, a, m, ALU.subtract),
                       deps=stoks + [pv0])
            t2 = fw.op('vector', lambda e, r=rstdb[:, sl]: e.tensor_tensor(tmp, tmp, r, ALU.mult), deps=[t1])
            t3 = fw.op('scalar', lambda e, m=mc: e.activation(tmpo, tmp, AF.Identity, scale=gcol[:, m:m + 1], bias=bcol[:, m:m + 1]),
                       deps=[t2, tgb, pv1])
            t4 = fw.dma('sync', dta, dr['dlt'][seq, mc, :, sl], deps=[pv2], ch=1)
            t5 = fw.op('vector', lambda e: e.tensor_tensor(dta, dta, tmpo, ALU.add), deps=[t3, t4])
            pv0 = t3
            pv1 = t5
            pv2 = fw.dma('sync', dr['dlt'][seq, mc, :, sl], dta, deps=[t5], ch=2)
    fw.barrier()
    sb.pop()
    sb.pop()


def l2_attn_phase(C, seq):
    fw, pp, sb, dr = C.fw, C.pp, C.sb, C.dr
    sb.push()
    vtm = sb.alloc(16 * 512, BF16)
    vtv = vtm.rearrange("p (c d) -> p c d", c=16)
    Pre = sb.alloc(17 * 512, BF16)
    Prv = Pre.rearrange("p (c d) -> p c d", c=17)
    Pim = sb.alloc(17 * 512, BF16)
    Piv = Pim.rearrange("p (c d) -> p c d", c=17)
    sb.push()
    xnb = sb.alloc(8 * 1024, BF16)
    xnbv = xnb.rearrange("p (c t) -> p c t", c=8)
    ldx = [fw.dma('gpsimd', xnbv[:, mc, :], dr['xnh'][seq, mc], ch=2) for mc in range(NCH)]
    wslot = [sb.alloc(8 * 64, BF16) for _ in range(2)]
    upc = sb.alloc(1026, BF16)
    dwt = [sb.alloc(64 * 6, BF16) for _ in range(2)]
    wcol = sb.alloc(4)
    x1h = sb.alloc(2048)
    vtmp = sb.alloc(2048)
    ucres = sb.alloc(2048)
    bic = sb.alloc(1)
    sfc_ = sb.alloc(1)
    wrel = [[], []]
    drel = [[], []]
    pup = None
    pdi = None
    puc = {}

    def upconv(c, dst, ddeps):
        nonlocal pup, pdi
        s_ = c % 2
        wv = wslot[s_].rearrange("p (c n) -> p c n", c=8)
        tw = fw.dma('gpsimd', wv, dr['L2_w_in'][:, c * 128:(c + 1) * 128].rearrange("(c p) n -> p c n", p=128),
                    deps=wrel[s_], ch=s_)
        tbi = fw.dma('sync', bic, dr['L2_b_in'][c], deps=[pup], ch=3)
        tsf = fw.dma('sync', sfc_, dr['L2_sf_b'][c], deps=[pup], ch=3)
        dv6 = dwt[s_].rearrange("p (j n) -> p j n", j=6)
        dts = []
        pdil = drel[s_]
        for j in range(3):
            twc = fw.dma('sync', wcol[:, j:j + 1], dr['L2_sfw_cols'][j, c], deps=pdil + [pdi], ch=3)
            t1 = fw.op('vector', lambda e, j=j: e.tensor_copy(wcol[:, 3:4].bitcast(BF16)[:, 0:1], wcol[:, j:j + 1]), deps=[twc])
            t2 = fw.op('vector', lambda e, j=j: e.tensor_copy(wcol[:, 3:4], wcol[:, 3:4].bitcast(BF16)[:, 0:1]), deps=[t1])
            t3 = fw.op('vector', lambda e, j=j: e.tensor_scalar(dv6[:, 2 * j, :], C.idn, wcol[:, 3:4], None, ALU.mult), deps=[t2] + pdil)
            t4 = fw.op('vector', lambda e, j=j: e.tensor_tensor(wcol[:, 3:4], wcol[:, j:j + 1], wcol[:, 3:4], ALU.subtract), deps=[t3])
            t5 = fw.op('vector', lambda e, j=j: e.tensor_scalar(dv6[:, 2 * j + 1, :], C.idn, wcol[:, 3:4], None, ALU.mult), deps=[t4] + pdil)
            dts.append(t5)
            pdi = t5
        tz = fw.memset('vector', upc[:, 0:2], 0.0, deps=[pup])
        evs = []
        for tt in range(NTT):
            sl = slice(tt * 512, (tt + 1) * 512)
            ip, ps, rel = pp.get()
            last = None
            for mc in range(NCH):
                last = fw.op('tensor', lambda e, m=mc, sl=sl, ps=ps, wv=wv, s0=(mc == 0): e.matmul(
                    ps[:], wv[:, m, :], xnbv[:, m, sl], start=s0, stop=(m == NCH - 1)),
                    deps=[tw] + rel + ldx)
            ev = fw.op('scalar', lambda e, tt=tt, ps=ps: e.activation(
                upc[:, 2 + tt * 512:2 + (tt + 1) * 512], ps[:], AF.Identity, bias=bic[:, 0:1]),
                deps=[last, tbi, pup, tz])
            pp.done(ip, [ev])
            evs.append(ev)
        cts = []
        for tt in range(NTT):
            ip, ps, rel = pp.get()
            last = None
            for j in range(3):
                for k in range(2):
                    last = fw.op('tensor', lambda e, j=j, k=k, tt=tt, ps=ps, dv6=dv6: e.matmul(
                        ps[:], dv6[:, 2 * j + k, :],
                        upc[:, tt * 512 + j:tt * 512 + j + 512],
                        start=(j == 0 and k == 0), stop=(j == 2 and k == 1)),
                        deps=dts + evs + rel)
            ev = fw.op('scalar', lambda e, tt=tt, ps=ps, dst=dst: e.activation(
                dst[:, tt * 512:(tt + 1) * 512], ps[:], AF.Identity, bias=sfc_[:, 0:1]),
                deps=[last, tsf] + list(ddeps))
            pp.done(ip, [ev])
            cts.append(ev)
        pup = cts[-1]
        wrel[s_] = [fw.tok('tensor')]
        drel[s_] = [fw.tok('tensor')]
        return cts

    pvw = None
    ptr = None
    ptm = None
    for i in range(8):
        t_x0 = upconv(i, ucres, [pvw])
        pvw = fw.dma('sync', dr['x0h'][i], ucres, deps=t_x0, ch=2)
        t_x1 = upconv(8 + i, x1h, [ptm])
        t_v = upconv(16 + i, vtmp, [ptr, ptm])
        tm = fw.op('vector', lambda e: e.tensor_tensor(vtmp, vtmp, x1h, ALU.mult), deps=t_x1 + t_v)
        ptm = tm
        tvw = fw.dma('sync', dr['vh'][i], vtmp, deps=[tm], ch=2)
        for tc in range(NTC):
            it, pst, relt = pp.get()
            tt_ = fw.op('tensor', lambda e, tc=tc, pst=pst: e.transpose(
                pst[:, 0:128], vtmp[:, tc * 128:(tc + 1) * 128], C.idn), deps=[tm] + relt)
            ev = fw.op('scalar', lambda e, tc=tc, i=i, pst=pst: e.activation(
                vtv[:, tc, i * 128:(i + 1) * 128], pst[:, 0:128], AF.Identity), deps=[tt_, ptr])
            pp.done(it, [ev])
            ptr = ev
        pvw = tvw
    fw.barrier()
    sb.pop()
    sb.push()
    crs = [sb.alloc(64, BF16) for _ in range(2)]
    cis = [sb.alloc(64, BF16) for _ in range(2)]
    vre = sb.alloc(1024)
    vim = sb.alloc(1024)
    kre = sb.alloc(1024)
    kim = sb.alloc(1024)
    ptmp = sb.alloc(1024)
    pcr = [None, None]
    pev = None
    pkl = None
    for fc in range(17):
        ire0, pre0, r0 = pp.get(hold=True)
        ire1, pre1, r1 = pp.get(hold=True)
        iim0, pim0, r2 = pp.get(hold=True)
        iim1, pim1, r3 = pp.get(hold=True)
        lasts = []
        for tc in range(NTC):
            s_ = tc % 2
            tcr = fw.dma('gpsimd', crs[s_], dr['C4R'][tc * 128:(tc + 1) * 128, fc * 128:(fc + 1) * 128],
                         deps=[pcr[s_]], ch=s_)
            tci = fw.dma('gpsimd', cis[s_], dr['C4I'][tc * 128:(tc + 1) * 128, fc * 128:(fc + 1) * 128],
                         deps=[pcr[s_]], ch=s_)
            lasts = []
            lst = None
            for (pp_, cc, hh, rr) in ((pre0, crs[s_], 0, r0), (pre1, crs[s_], 1, r1),
                                      (pim0, cis[s_], 0, r2), (pim1, cis[s_], 1, r3)):
                lst = fw.op('tensor', lambda e, p=pp_, c=cc, hh=hh, tc=tc: e.matmul(
                    p[:], c, vtv[:, tc, hh * 512:(hh + 1) * 512], start=(tc == 0), stop=(tc == NTC - 1)),
                    deps=[tcr, tci] + (list(rr) if tc == 0 else []))
                lasts.append(lst)
            pcr[s_] = lst
        e1 = fw.op('scalar', lambda e: e.activation(vre[:, 0:512], pre0[:], AF.Identity), deps=[lasts[0], pev])
        e2 = fw.op('scalar', lambda e: e.activation(vre[:, 512:1024], pre1[:], AF.Identity), deps=[lasts[1], pev])
        e3 = fw.op('scalar', lambda e: e.activation(vim[:, 0:512], pim0[:], AF.Identity), deps=[lasts[2], pev])
        e4 = fw.op('scalar', lambda e: e.activation(vim[:, 512:1024], pim1[:], AF.Identity), deps=[lasts[3], pev])
        pp.done(ire0, [e1])
        pp.done(ire1, [e2])
        pp.done(iim0, [e3])
        pp.done(iim1, [e4])
        tk1 = fw.dma('sync', kre, dr['KRE'][fc], deps=[pkl], ch=1)
        tk2 = fw.dma('sync', kim, dr['KIM'][fc], deps=[pkl], ch=1)
        t1 = fw.op('vector', lambda e: e.tensor_tensor(ptmp, vre, kre, ALU.mult), deps=[e1, e2, tk1])
        t2 = fw.op('vector', lambda e, fc=fc: e.tensor_tensor(Prv[:, fc, :], vim, kim, ALU.mult), deps=[e3, e4, tk2])
        t3 = fw.op('vector', lambda e, fc=fc: e.tensor_tensor(ptmp, ptmp, Prv[:, fc, :], ALU.subtract), deps=[t1, t2])
        t3b = fw.op('vector', lambda e, fc=fc: e.tensor_copy(Prv[:, fc, :], ptmp), deps=[t3])
        t4 = fw.op('vector', lambda e: e.tensor_tensor(ptmp, vre, kim, ALU.mult), deps=[t3b, tk2])
        t5 = fw.op('vector', lambda e, fc=fc: e.tensor_tensor(Piv[:, fc, :], vim, kre, ALU.mult), deps=[t4, tk1])
        t6 = fw.op('vector', lambda e, fc=fc: e.tensor_tensor(ptmp, ptmp, Piv[:, fc, :], ALU.add), deps=[t5])
        t6b = fw.op('vector', lambda e, fc=fc: e.tensor_copy(Piv[:, fc, :], ptmp), deps=[t6])
        pkl = t6b
        pev = t6b
    fw.barrier()
    sb.pop()
    sb.push()
    obf = sb.alloc(8 * 1024, BF16)
    obv = obf.rearrange("p (c t) -> p c t", c=8)
    sb.push()
    abs_ = sb.alloc(17 * 256, BF16)
    abv = abs_.rearrange("p (c n) -> p c n", c=17)
    bbs = sb.alloc(17 * 256, BF16)
    bbv = bbs.rearrange("p (c n) -> p c n", c=17)
    vt_ = sb.alloc(512)
    x0t = sb.alloc(512)
    fbc = sb.alloc(8)
    ytmp = sb.alloc(512)
    tfb = None
    for mc in range(NCH):
        tfb = fw.dma('sync', fbc[:, mc:mc + 1], dr['L2_filt_bias'][mc], ch=3)
    pab = None
    pvl = None
    pgl = None
    for it in range(NTT):
        tas = []
        for fc in range(17):
            tas.append(fw.dma('gpsimd', abv[:, fc, :], dr['AMAT'][fc * 128:(fc + 1) * 128, it * 512:(it + 1) * 512],
                              deps=[pab], ch=0))
            tas.append(fw.dma('gpsimd', bbv[:, fc, :], dr['BMAT'][fc * 128:(fc + 1) * 128, it * 512:(it + 1) * 512],
                              deps=[pab], ch=1))
        for ch in range(NCH):
            ip, ps, rel = pp.get()
            last = None
            for fc in range(17):
                last = fw.op('tensor', lambda e, fc=fc, ch=ch, ps=ps, s0=(fc == 0): e.matmul(
                    ps[:], Prv[:, fc, ch * 128:(ch + 1) * 128], abv[:, fc, :], start=s0, stop=False),
                    deps=tas + rel)
                last = fw.op('tensor', lambda e, fc=fc, ch=ch, ps=ps, st=(fc == 16): e.matmul(
                    ps[:], Piv[:, fc, ch * 128:(ch + 1) * 128], bbv[:, fc, :], start=False, stop=st),
                    deps=[])
            tvl = fw.dma('sync', vt_, dr['vh'][ch, :, it * 512:(it + 1) * 512], deps=[pvl], ch=1)
            txl = fw.dma('sync', x0t, dr['x0h'][ch, :, it * 512:(it + 1) * 512], deps=[pvl], ch=2)
            t1 = fw.op('vector', lambda e, ch=ch: e.tensor_scalar(ytmp, vt_, fbc[:, ch:ch + 1], None, ALU.mult),
                       deps=[tvl, tfb, pgl])
            t2 = fw.op('vector', lambda e, ps=ps: e.tensor_tensor(ytmp, ytmp, ps[:], ALU.add), deps=[t1, last])
            t3 = fw.op('vector', lambda e, ch=ch, it=it: e.tensor_tensor(
                obv[:, ch, it * 512:(it + 1) * 512], ytmp, x0t, ALU.mult), deps=[t2, txl])
            pp.done(ip, [t2])
            pvl = t3
            pgl = t3
        pab = fw.tok('tensor')
    fw.barrier()
    sb.pop()
    wslot2 = sb.alloc(8 * 512, BF16)
    wv2 = wslot2.rearrange("p (c n) -> p c n", c=8)
    tw = fw.dma('gpsimd', wv2, dr['L2_w_out'].rearrange("(c p) n -> p c n", p=128), ch=0)
    boc = sb.alloc(8)
    tbo = None
    for mc in range(NCH):
        tbo = fw.dma('sync', boc[:, mc:mc + 1], dr['L2_b_out'][mc], ch=3)
    tmpo = sb.alloc(512)
    dta = sb.alloc(512)
    pv1 = pv2 = None
    for mc in range(NCH):
        for tt in range(NTT):
            sl = slice(tt * 512, (tt + 1) * 512)
            ip, ps, rel = pp.get()
            last = None
            for c in range(NCH):
                last = fw.op('tensor', lambda e, c=c, m=mc, sl=sl, ps=ps, s0=(c == 0): e.matmul(
                    ps[:], wv2[:, c, m * 128:(m + 1) * 128], obv[:, c, sl],
                    start=s0, stop=(c == NCH - 1)), deps=[tw] + rel)
            t1 = fw.op('scalar', lambda e, m=mc, ps=ps: e.activation(tmpo, ps[:], AF.Identity, bias=boc[:, m:m + 1]),
                       deps=[last, tbo, pv1])
            pp.done(ip, [t1])
            t2 = fw.dma('sync', dta, dr['dlt'][seq, mc, :, sl], deps=[pv2], ch=1)
            t3 = fw.op('vector', lambda e: e.tensor_tensor(dta, dta, tmpo, ALU.add), deps=[t1, t2])
            pv1 = t3
            pv2 = fw.dma('sync', dr['dlt'][seq, mc, :, sl], dta, deps=[t3], ch=2)
    fw.barrier()
    sb.pop()
    sb.pop()



def l2_filter_phase(C):
    fw, pp, sb, dr = C.fw, C.pp, C.sb, C.dr
    sb.push()
    zaug = sb.alloc(4096)
    tz = fw.dma('sync', zaug[0:4, :], dr['ZAUG'])
    h1 = sb.alloc(4096)
    h2 = sb.alloc(4096)
    tmpa = sb.alloc(512)
    tmpb = sb.alloc(512)
    wa = sb.alloc(64)
    fq = sb.alloc(1)
    t1_ = fw.memset('vector', h1[64:65, :], 1.0)
    t2_ = fw.memset('vector', h2[64:65, :], 1.0)
    PI = math.pi

    def mlp_layer(wname, bname, fname, K, hsrc, hdst, deps):
        nw = 3 if K == 4 else 64
        tw = fw.dma('sync', wa[0:nw, 0:64], dr[wname], deps=[fw.tok('tensor'), fw.tok('scalar')])
        tb = fw.dma('sync', wa[nw:nw + 1, 0:64], dr[bname], deps=[])
        tf = fw.dma('sync', fq[0:64, :], dr[fname], deps=[fw.tok('vector')])
        toks = []
        for nt in range(8):
            sl = slice(nt * 512, (nt + 1) * 512)
            ip, ps, rel = pp.get()
            tm = fw.op('tensor', lambda e, sl=sl, K=K: e.matmul(
                ps[0:64, :], wa[0:K, 0:64], hsrc[0:K, sl], start=True, stop=True),
                deps=[tw, tb] + rel + list(deps))
            tv = fw.op('vector', lambda e: e.tensor_scalar(tmpa[0:64, :], ps[0:64, :], fq[0:64, 0:1], None, ALU.mult),
                       deps=[tm, tf, fw.tok('scalar')])
            pp.done(ip, [tv])
            for _ in range(3):
                tv = fw.op('vector', lambda e: e.tensor_scalar(tmpb[0:64, :], tmpa[0:64, :], PI, None, ALU.is_gt), deps=[tv])
                tv = fw.op('vector', lambda e: e.scalar_tensor_tensor(tmpa[0:64, :], tmpb[0:64, :], -2 * PI, tmpa[0:64, :], ALU.mult, ALU.add), deps=[tv])
                tv = fw.op('vector', lambda e: e.tensor_scalar(tmpb[0:64, :], tmpa[0:64, :], -PI, None, ALU.is_lt), deps=[tv])
                tv = fw.op('vector', lambda e: e.scalar_tensor_tensor(tmpa[0:64, :], tmpb[0:64, :], 2 * PI, tmpa[0:64, :], ALU.mult, ALU.add), deps=[tv])
            ts_ = fw.op('scalar', lambda e, sl=sl: e.activation(hdst[0:64, sl], tmpa[0:64, :], AF.Sin), deps=[tv])
            toks.append(ts_)
        return toks

    tk = mlp_layer('L2_f_w1', 'L2_f_b1', 'L2_freq1', 4, zaug, h1, [tz, t1_])
    tk = mlp_layer('L2_f_w2', 'L2_f_b2', 'L2_freq2', 65, h1, h2, tk + [t2_])
    tk = mlp_layer('L2_f_w3', 'L2_f_b3', 'L2_freq3', 65, h2, h1, tk + [t1_])
    fw.barrier()
    ktm = sb.alloc(32 * 512, BF16)
    kv = ktm.rearrange("p (c n) -> p c n", c=32)
    fwout = sb.alloc(1024)
    tfo = fw.dma('sync', fwout[0:64, :], dr['L2_f_wout'])
    dec = [sb.alloc(512) for _ in range(2)]
    pdec = [None, None]
    for tc in range(32):
        for chh in range(2):
            s_ = chh
            td = fw.dma('sync', dec[s_], dr['DECAY'][tc * 128:(tc + 1) * 128, chh * 512:(chh + 1) * 512],
                        deps=[pdec[s_]], ch=1)
            ip, ps, rel = pp.get()
            tm = fw.op('tensor', lambda e, tc=tc, chh=chh: e.matmul(
                ps[:], h1[0:64, tc * 128:(tc + 1) * 128], fwout[0:64, chh * 512:(chh + 1) * 512],
                start=True, stop=True), deps=[tfo] + rel)
            tv = fw.op('vector', lambda e, tc=tc, chh=chh, s_=s_: e.tensor_tensor(
                kv[:, tc, chh * 512:(chh + 1) * 512], ps[:], dec[s_], ALU.mult), deps=[tm, td])
            pp.done(ip, [tv])
            pdec[s_] = tv
    crs = [sb.alloc(64, BF16) for _ in range(2)]
    cis = [sb.alloc(64, BF16) for _ in range(2)]
    kreb = sb.alloc(1024)
    kimb = sb.alloc(1024)
    pcr = [None, None]
    pev = None
    ktv = fw.tok('vector')
    for fc in range(17):
        ire0, pre0, r0 = pp.get(hold=True)
        ire1, pre1, r1 = pp.get(hold=True)
        iim0, pim0, r2 = pp.get(hold=True)
        iim1, pim1, r3 = pp.get(hold=True)
        lasts = []
        for tc in range(32):
            s_ = tc % 2
            tcr = fw.dma('gpsimd', crs[s_], dr['C4R'][tc * 128:(tc + 1) * 128, fc * 128:(fc + 1) * 128],
                         deps=[pcr[s_]], ch=s_)
            tci = fw.dma('gpsimd', cis[s_], dr['C4I'][tc * 128:(tc + 1) * 128, fc * 128:(fc + 1) * 128],
                         deps=[pcr[s_]], ch=s_)
            lst = None
            lasts = []
            for (pp_, cc, hh, rr) in ((pre0, crs[s_], 0, r0), (pre1, crs[s_], 1, r1),
                                      (pim0, cis[s_], 0, r2), (pim1, cis[s_], 1, r3)):
                lst = fw.op('tensor', lambda e, p=pp_, c=cc, hh=hh, tc=tc: e.matmul(
                    p[:], c, kv[:, tc, hh * 512:(hh + 1) * 512], start=(tc == 0), stop=(tc == 31)),
                    deps=[tcr, tci, ktv] + (list(rr) if tc == 0 else []))
                lasts.append(lst)
            pcr[s_] = lst
        e1 = fw.op('scalar', lambda e: e.activation(kreb[:, 0:512], pre0[:], AF.Identity), deps=[lasts[0], pev])
        e2 = fw.op('scalar', lambda e: e.activation(kreb[:, 512:1024], pre1[:], AF.Identity), deps=[lasts[1]])
        e3 = fw.op('scalar', lambda e: e.activation(kimb[:, 0:512], pim0[:], AF.Identity), deps=[lasts[2]])
        e4 = fw.op('scalar', lambda e: e.activation(kimb[:, 512:1024], pim1[:], AF.Identity), deps=[lasts[3]])
        pp.done(ire0, [e1])
        pp.done(ire1, [e2])
        pp.done(iim0, [e3])
        pp.done(iim1, [e4])
        s1 = fw.dma('sync', dr['KRE'][fc], kreb, deps=[e1, e2], ch=2)
        s2 = fw.dma('sync', dr['KIM'][fc], kimb, deps=[e3, e4], ch=2)
        pev = s2
    fw.barrier()
    sb.pop()


def l2_ffn_phase(C, seq):
    fw, pp, sb, dr = C.fw, C.pp, C.sb, C.dr
    sb.push()
    xi = sb.alloc(8 * 2052)
    xiv = xi.rearrange("p (c t) -> p c t", c=8)
    tz = fw.memset('vector', xiv[:, :, 0:1], 0.0)
    tz2 = fw.memset('vector', xiv[:, :, 2049:2052], 0.0)
    ldx = [fw.dma('sync', xiv[:, mc, 1:2049], dr['xnh'][seq, mc], deps=[tz, tz2]) for mc in range(NCH)]
    fw.barrier()
    for li in range(2):
        sb.push()
        xb = sb.alloc(8 * 1026, BF16)
        xbv = xb.rearrange("p (c t) -> p c t", c=8)
        cts = []
        pcv = None
        for mc in range(NCH):
            pcv = fw.op('vector', lambda e, mc=mc: e.tensor_copy(xbv[:, mc, :], xiv[:, mc, 0:2052]), deps=[pcv])
            cts.append(pcv)
        qfm = sb.alloc(4 * 1024, BF16)
        qfv = qfm.rearrange("p (c t) -> p c t", c=4)
        kfm = sb.alloc(4 * 1024, BF16)
        kfv = kfm.rearrange("p (c t) -> p c t", c=4)
        vtm = sb.alloc(16 * 256, BF16)
        vtv = vtm.rearrange("p (c d) -> p c d", c=16)
        osb = sb.alloc(4 * 1024, BF16)
        ov = osb.rearrange("p (c t) -> p c t", c=4)
        wslot = sb.alloc(8 * 256, BF16)
        wv = wslot.rearrange("p (c n) -> p c n", c=8)
        xpos = [sb.alloc(64) for _ in range(4)]
        qsb = sb.alloc(64)
        ksb = sb.alloc(64)
        qrow = sb.alloc(64)
        qtmp = sb.alloc(512)
        qsq = sb.alloc(512)
        qrt = sb.alloc(512)
        qb = sb.alloc(256, BF16)
        sscol = sb.alloc(16)
        t = fw.dma('sync', qrow[0:1, 0:64], dr['L2f_qs'][li], ch=3)
        bi, bp, tb = bcast_row(C, qrow[0:1, 0:64], 64, [t])
        tqs = fw.op('scalar', lambda e: e.activation(qsb, bp[:, 0:64], AF.Identity), deps=[tb])
        pp.done(bi, [tqs])
        t2 = fw.dma('sync', qrow[0:1, 0:64], dr['L2f_ks'][li], deps=[tb], ch=3)
        bi, bp, tb = bcast_row(C, qrow[0:1, 0:64], 64, [t2])
        tks = fw.op('scalar', lambda e: e.activation(ksb, bp[:, 0:64], AF.Identity), deps=[tb])
        pp.done(bi, [tks])

        def shslice(mc, a, b):
            off = 1 if mc < 4 else 0
            return xbv[:, mc, off + a: off + b]

        def qkv_pass(kind, c0):
            tw = fw.dma('gpsimd', wv, dr['L2f_wqkv'][li][:, c0:c0 + 512].rearrange("(c p) n -> p c n", p=128),
                        deps=[fw.tok('tensor')])
            prev_cons = None
            for tc in range(NTC):
                ip, ps, rel = pp.get()
                last = None
                for mc in range(NCH):
                    last = fw.op('tensor', lambda e, m=mc, tc=tc, s0=(mc == 0): e.matmul(
                        ps[:], shslice(m, tc * 128, (tc + 1) * 128), wv[:, m, :],
                        start=s0, stop=(m == NCH - 1)), deps=[tw] + rel + cts)
                if kind == 'v':
                    t1 = fw.op('scalar', lambda e, tc=tc: e.activation(vtv[:, tc, :], ps[:], AF.Identity), deps=[last])
                    pp.done(ip, [t1])
                    continue
                s_ = tc % 2
                ct, st_ = xpos[2 * s_], xpos[2 * s_ + 1]
                cn = 'XCQ' if kind == 'q' else 'XCK'
                sn = 'XSQ' if kind == 'q' else 'XSK'
                tct = fw.dma('sync', ct, dr[cn][tc * 128:(tc + 1) * 128, :], deps=[prev_cons], ch=1)
                tst = fw.dma('sync', st_, dr[sn][tc * 128:(tc + 1) * 128, :], deps=[prev_cons], ch=1)
                t1 = fw.op('scalar', lambda e: e.activation(qtmp, ps[:], AF.Identity), deps=[last, prev_cons])
                pp.done(ip, [t1])
                t3 = fw.op('scalar', lambda e: e.activation(qsq, qtmp, AF.Square), deps=[t1, prev_cons])
                q3 = qtmp.rearrange("p (h d) -> p h d", h=8)
                t4 = fw.op('vector', lambda e: e.reduce_sum(sscol[:, 0:8].rearrange("p (h o) -> p h o", h=8),
                                                            qsq.rearrange("p (h d) -> p h d", h=8), AX.X), deps=[t3])
                t4 = fw.op('scalar', lambda e: e.activation(sscol[:, 8:16], sscol[:, 0:8], AF.Sqrt), deps=[t4])
                t4 = fw.op('vector', lambda e: e.tensor_scalar(sscol[:, 8:16], sscol[:, 8:16], 1e-12, None, ALU.max), deps=[t4])
                t4 = fw.op('vector', lambda e: e.reciprocal(sscol[:, 0:8], sscol[:, 8:16]), deps=[t4])
                tn = []
                for h in range(8):
                    tn.append(fw.op('vector', lambda e, h=h: e.tensor_scalar(
                        q3[:, h, :], q3[:, h, :], sscol[:, h:h + 1], None, ALU.mult), deps=[t4]))
                scb = (qsb if kind == 'q' else ksb).rearrange("p (o d) -> p o d", o=1).broadcast_to([128, 8, 64])
                tqn = fw.op('vector', lambda e, scb=scb: e.tensor_tensor(q3, q3, scb, ALU.mult), deps=tn + [tqs, tks])
                cb = ct.rearrange("p (o d) -> p o d", o=1).broadcast_to([128, 8, 64])
                sb_ = st_.rearrange("p (o d) -> p o d", o=1).broadcast_to([128, 8, 64])
                r3 = qrt.rearrange("p (h d) -> p h d", h=8)
                s3 = qsq.rearrange("p (h d) -> p h d", h=8)
                u1 = fw.op('vector', lambda e, cb=cb: e.tensor_tensor(r3, q3, cb, ALU.mult), deps=[tqn, tct])
                u2 = fw.op('vector', lambda e, sb_=sb_: e.tensor_tensor(s3[:, :, 0:32], q3[:, :, 32:64], sb_[:, :, 0:32], ALU.mult), deps=[tqn, tst, t4])
                u3 = fw.op('vector', lambda e: e.tensor_tensor(r3[:, :, 0:32], r3[:, :, 0:32], s3[:, :, 0:32], ALU.subtract), deps=[u1, u2])
                u4 = fw.op('vector', lambda e, sb_=sb_: e.tensor_tensor(s3[:, :, 32:64], q3[:, :, 0:32], sb_[:, :, 32:64], ALU.mult), deps=[u1, u2])
                u5 = fw.op('vector', lambda e: e.tensor_tensor(r3[:, :, 32:64], r3[:, :, 32:64], s3[:, :, 32:64], ALU.add), deps=[u3, u4])
                dv = qfv if kind == 'q' else kfv
                qb_rel = None
                t6 = fw.op('vector', lambda e: e.tensor_copy(qb, qrt), deps=[u3, u5])
                for h2 in range(4):
                    it, pst, relt = pp.get()
                    pb16 = pst[:, 0:64].bitcast(BF16)
                    tt_ = fw.op('tensor', lambda e, h2=h2, pb16=pb16: e.transpose(
                        pb16, qb[:, h2 * 128:(h2 + 1) * 128], C.idnb), deps=[t6] + relt)
                    ev = fw.op('scalar', lambda e, h2=h2, tc=tc, pb16=pb16, dv=dv: e.activation(
                        dv[:, h2, tc * 128:(tc + 1) * 128], pb16, AF.Identity), deps=[tt_])
                    pp.done(it, [ev])
                    qb_rel = tt_
                prev_cons = fw.tok('vector')
        qkv_pass('q', 0)
        qkv_pass('k', 512)
        qkv_pass('v', 1024)
        fw.barrier()
        pts = [sb.alloc(256, BF16) for _ in range(4)]
        rrow = qtmp
        rrec = qsq
        rbv = qrt
        otmp = None
        prel = [[], [], [], []]
        pi = 0
        for h in range(8):
            hp, ho = h // 2, (h % 2) * 64
            for qw in range(NTT):
                sl = slice(qw * 512, (qw + 1) * 512)
                iR, psR, relR = pp.get(hold=True)
                iO, psO, relO = pp.get(hold=True)
                kcs = [kc for kc in range(qw * 4 - 4, qw * 4 + 4) if kc >= 0]
                lastR = lastO = None
                for ik, kc in enumerate(kcs):
                    iS, psS, relS = pp.get()
                    tm = fw.op('tensor', lambda e, kc=kc, sl=sl, hp=hp, ho=ho, psS=psS: e.matmul(
                        psS[:], kfv[ho:ho + 64, hp, kc * 128:(kc + 1) * 128],
                        qfv[ho:ho + 64, hp, sl], start=True, stop=True), deps=relS)
                    mts = [tm]
                    for m in range(4):
                        qc = qw * 4 + m
                        if kc == qc:
                            mts.append(fw.op('vector', lambda e, m=m, psS=psS: e.tensor_tensor(
                                psS[:, m * 128:(m + 1) * 128], psS[:, m * 128:(m + 1) * 128], C.trig, ALU.add), deps=[tm]))
                        elif kc == qc - 4:
                            mts.append(fw.op('vector', lambda e, m=m, psS=psS: e.tensor_tensor(
                                psS[:, m * 128:(m + 1) * 128], psS[:, m * 128:(m + 1) * 128], C.tril, ALU.add), deps=[tm]))
                        elif kc > qc or kc < qc - 4:
                            mts.append(fw.op('vector', lambda e, m=m, psS=psS: e.tensor_scalar(
                                psS[:, m * 128:(m + 1) * 128], psS[:, m * 128:(m + 1) * 128], -1e30, None, ALU.add), deps=[tm]))
                    pt = pts[pi % 4]
                    texp = fw.op('scalar', lambda e, pt=pt, psS=psS: e.activation(pt, psS[:], AF.Exp), deps=mts + prel[pi % 4])
                    pp.done(iS, [texp])
                    lastR = fw.op('tensor', lambda e, pt=pt, psR=psR, s0=(ik == 0), st=(ik == len(kcs) - 1): e.matmul(
                        psR[0:1, :], C.onescb, pt, start=s0, stop=st),
                        deps=[texp] + (relR if ik == 0 else []))
                    lastO = fw.op('tensor', lambda e, pt=pt, kc=kc, h=h, psO=psO, s0=(ik == 0), st=(ik == len(kcs) - 1): e.matmul(
                        psO[0:64, :], vtv[:, kc, h * 64:(h + 1) * 64], pt, start=s0, stop=st),
                        deps=[texp] + (relO if ik == 0 else []))
                    prel[pi % 4] = [lastR, lastO]
                    pi += 1
                t1 = fw.op('vector', lambda e, psR=psR: e.tensor_copy(rrow[0:1, :], psR[0:1, :]), deps=[lastR])
                pp.done(iR, [t1])
                t2 = fw.op('vector', lambda e: e.reciprocal(rrec[0:1, :], rrow[0:1, :]), deps=[t1])
                bi, bp, tb = bcast_row(C, rrec[0:1, :], 512, [t2])
                t3 = fw.op('scalar', lambda e, bp=bp: e.activation(rbv[0:64, :], bp[0:64, :], AF.Identity), deps=[tb, otmp])
                pp.done(bi, [t3])
                t4 = fw.op('vector', lambda e, h=h, sl=sl, hp=hp, ho=ho, psO=psO: e.tensor_tensor(
                    ov[ho:ho + 64, hp, sl], psO[0:64, :], rbv[0:64, :], ALU.mult), deps=[t3, lastO])
                pp.done(iO, [t4])
                otmp = t4
        fw.barrier()
        wv4 = wslot.rearrange("p (c n) -> p c n", c=4)
        tw = fw.dma('gpsimd', wv4, dr['L2f_wo'][li].rearrange("(c p) n -> p c n", p=128), deps=[fw.tok('tensor')])
        tmpo = qtmp
        pv1 = None
        for mc in range(NCH):
            for tt in range(NTT):
                sl = slice(tt * 512, (tt + 1) * 512)
                ip, ps, rel = pp.get()
                last = None
                for c in range(4):
                    last = fw.op('tensor', lambda e, c=c, m=mc, sl=sl, ps=ps, s0=(c == 0): e.matmul(
                        ps[:], wv4[:, c, m * 128:(m + 1) * 128], ov[:, c, sl],
                        start=s0, stop=(c == 3)), deps=[tw] + rel)
                t1 = fw.op('scalar', lambda e, ps=ps: e.activation(tmpo, ps[:], AF.Identity), deps=[last, pv1])
                pp.done(ip, [t1])
                t2 = fw.op('vector', lambda e, mc=mc, tt=tt: e.tensor_tensor(
                    xiv[:, mc, 1 + tt * 512:1 + (tt + 1) * 512], xiv[:, mc, 1 + tt * 512:1 + (tt + 1) * 512],
                    tmpo, ALU.add), deps=[t1])
                pv1 = t2
        fw.barrier()
        sb.pop()
        sb.push()
        xnin = sb.alloc(8 * 1024, BF16)
        xnv = xnin.rearrange("p (c t) -> p c t", c=8)
        vt0 = fw.tok('vector')

        def shx(mc, tt):
            off = 1 if mc < 4 else 0
            return xiv[:, mc, off + tt * 512: off + (tt + 1) * 512]
        tiles = {(mc, tt): shx(mc, tt) for mc in range(NCH) for tt in range(NTT)}
        ttoks = {k: vt0 for k in tiles}
        sb.push()
        mub, rstdb, stoks = ln_norm_tiles(C, tiles, ttoks, [])
        gcol = sb.alloc(8)
        bcol = sb.alloc(8)
        tgb = None
        for mc in range(NCH):
            fw.dma('sync', gcol[:, mc:mc + 1], dr['L2f_lng'][li, mc], ch=3)
            tgb = fw.dma('sync', bcol[:, mc:mc + 1], dr['L2f_lnb'][li, mc], ch=3)
        tmp = sb.alloc(512)
        pvx = None
        for mc in range(NCH):
            for tt in range(NTT):
                sl = slice(tt * 512, (tt + 1) * 512)
                t1 = fw.op('vector', lambda e, a=tiles[(mc, tt)], m=mub[:, sl]: e.tensor_tensor(tmp, a, m, ALU.subtract),
                           deps=stoks + [pvx])
                t2 = fw.op('vector', lambda e, r=rstdb[:, sl]: e.tensor_tensor(tmp, tmp, r, ALU.mult), deps=[t1])
                pvx = fw.op('scalar', lambda e, m=mc, o=xnv[:, mc, sl]: e.activation(
                    o, tmp, AF.Identity, scale=gcol[:, m:m + 1], bias=bcol[:, m:m + 1]), deps=[t2, tgb])
        fw.barrier()
        sb.pop()
        NI = 22
        hsb = sb.alloc(NI * 512, BF16)
        hv = hsb.rearrange("p (c t) -> p c t", c=NI)
        w1a = [sb.alloc(8 * 64, BF16) for _ in range(2)]
        w1g = [sb.alloc(8 * 64, BF16) for _ in range(2)]
        w2s = [sb.alloc(NI * 64, BF16) for _ in range(2)]
        b1ac = sb.alloc(NI)
        b1gc = sb.alloc(NI)
        b2c = sb.alloc(8)
        tmpf = sb.alloc(512)
        tb1 = None
        for i in range(NI):
            fw.dma('sync', b1ac[:, i:i + 1], dr['L2f_b1p'][li, i], ch=3)
            tb1 = fw.dma('sync', b1gc[:, i:i + 1], dr['L2f_b1p'][li, 22 + i], ch=3)
        tb2 = None
        for mc in range(NCH):
            tb2 = fw.dma('sync', b2c[:, mc:mc + 1], dr['L2f_b2'][li, mc], ch=3)
        HP = 2816
        for th in range(2):
            wrel1 = [[], []]
            hrel = []
            prev_ga = None
            prev_h = None
            for i in range(NI):
                s_ = i % 2
                wva = w1a[s_].rearrange("p (c n) -> p c n", c=8)
                wvg = w1g[s_].rearrange("p (c n) -> p c n", c=8)
                ta = fw.dma('gpsimd', wva, dr['L2f_w1p'][li][:, i * 128:(i + 1) * 128].rearrange("(c p) n -> p c n", p=128), deps=wrel1[s_], ch=s_)
                tg = fw.dma('gpsimd', wvg, dr['L2f_w1p'][li][:, HP + i * 128:HP + (i + 1) * 128].rearrange("(c p) n -> p c n", p=128), deps=wrel1[s_], ch=2 + s_)
                for tt2 in range(2):
                    sl = slice(th * 1024 + tt2 * 512, th * 1024 + (tt2 + 1) * 512)
                    ia, pa, ra = pp.get()
                    ig, pg, rg = pp.get()
                    lasta = lastg = None
                    for mc in range(NCH):
                        lasta = fw.op('tensor', lambda e, m=mc, sl=sl, pa=pa, wva=wva, s0=(mc == 0): e.matmul(
                            pa[:], wva[:, m, :], xnv[:, m, sl], start=s0, stop=(m == NCH - 1)),
                            deps=[ta] + ra)
                    for mc in range(NCH):
                        lastg = fw.op('tensor', lambda e, m=mc, sl=sl, pg=pg, wvg=wvg, s0=(mc == 0): e.matmul(
                            pg[:], wvg[:, m, :], xnv[:, m, sl], start=s0, stop=(m == NCH - 1)),
                            deps=[tg] + rg)
                    t1 = fw.op('scalar', lambda e, i=i, pg=pg: e.activation(tmpf, pg[:], AF.Gelu, bias=b1gc[:, i:i + 1]),
                               deps=[lastg, tb1, prev_h])
                    t2 = fw.op('vector', lambda e, i=i, tt2=tt2, pa=pa: e.scalar_tensor_tensor(
                        hv[:, i, tt2 * 512:(tt2 + 1) * 512], pa[:], b1ac[:, i:i + 1], tmpf, ALU.add, ALU.mult),
                        deps=[t1, lasta] + hrel)
                    pp.done(ia, [t2])
                    pp.done(ig, [t1])
                    prev_ga = t2
                    prev_h = t2
                wrel1[s_] = [fw.tok('tensor')]
            hrel = []
            wrel2 = [[], []]
            for mc in range(NCH):
                s_ = mc % 2
                wv2 = w2s[s_].rearrange("p (c n) -> p c n", c=NI)
                t2w = fw.dma('gpsimd', wv2, dr['L2f_w2p'][li][:, mc * 128:(mc + 1) * 128].rearrange("(c p) n -> p c n", p=128), deps=wrel2[s_], ch=s_)
                for tt2 in range(2):
                    sl = slice(th * 1024 + tt2 * 512, th * 1024 + (tt2 + 1) * 512)
                    ip, ps, rel = pp.get()
                    last = None
                    for i in range(NI):
                        last = fw.op('tensor', lambda e, i=i, tt2=tt2, ps=ps, wv2=wv2, s0=(i == 0): e.matmul(
                            ps[:], wv2[:, i, :], hv[:, i, tt2 * 512:(tt2 + 1) * 512],
                            start=s0, stop=(i == NI - 1)), deps=[t2w, prev_ga] + rel)
                    t1 = fw.op('scalar', lambda e, m=mc, ps=ps: e.activation(tmpf, ps[:], AF.Identity, bias=b2c[:, m:m + 1]),
                               deps=[last, tb2, fw.tok('vector')])
                    t2 = fw.op('vector', lambda e, mc=mc, sl2=(1 + th * 1024 + tt2 * 512): e.tensor_tensor(
                        xiv[:, mc, sl2:sl2 + 512], xiv[:, mc, sl2:sl2 + 512], tmpf, ALU.add), deps=[t1])
                    pp.done(ip, [t2])
                wrel2[s_] = [fw.tok('tensor')]
            fw.barrier()
        sb.pop()
    for mc in range(NCH):
        fw.dma('sync', dr['dlt'][seq, mc], xiv[:, mc, 1:2049], deps=[fw.tok('vector')], ch=2)
    fw.barrier()
    sb.pop()






def l3_attn_phase(C, seq):
    fw, pp, sb, dr = C.fw, C.pp, C.sb, C.dr
    NH = 16
    sb.push()
    o_sb = sb.alloc(8 * 1024, BF16)
    ov = o_sb.rearrange("p (c t) -> p c t", c=8)
    q_fm = sb.alloc(8 * 1024, BF16)
    qfv = q_fm.rearrange("p (c t) -> p c t", c=8)
    k_fm = sb.alloc(8 * 1024, BF16)
    kfv = k_fm.rearrange("p (c t) -> p c t", c=8)
    v_tm = sb.alloc(16 * 512, BF16)
    vtv = v_tm.rearrange("p (c d) -> p c d", c=16)
    sb.push()
    binr = sb.alloc(1536, BF16)
    tbin = fw.dma('gpsimd', binr[0:1, :], dr['L3_b_in'])
    xns = [sb.alloc(8 * 64, BF16) for _ in range(2)]
    wslot = sb.alloc(8 * 512, BF16)
    wv = wslot.rearrange("p (c n) -> p c n", c=8)
    qb = sb.alloc(256, BF16)

    def qkv_pass(kind, c0):
        tw = fw.dma('gpsimd', wv, dr['L3_winT'][:, c0:c0 + 1024].rearrange("(c p) n -> p c n", p=128),
                    deps=[fw.tok('tensor')])
        prevx = [None, None]
        prev_cons = None
        for tc in range(NTC):
            s_ = tc % 2
            xa = xns[s_].rearrange("p (c n) -> p c n", c=8)
            tl = fw.dma('gpsimd', xa, dr['xnh'][seq].rearrange("c p t -> p c t")[:, :, tc * 128:(tc + 1) * 128],
                        deps=[prevx[s_]], ch=s_)
            evs = []
            for half in range(2):
                ip, ps, rel = pp.get()
                last = None
                for mc in range(NCH):
                    last = fw.op('tensor', lambda e, m=mc, h=half, s0=(mc == 0): e.matmul(
                        ps[:], xa[:, m, :], wv[:, m, h * 512:(h + 1) * 512], start=s0, stop=False),
                        deps=[tl, tw] + rel)
                last = fw.op('tensor', lambda e, h=half: e.matmul(
                    ps[:], C.onesrb[0:1, :], binr[0:1, c0 + half * 512:c0 + (half + 1) * 512],
                    start=False, stop=True), deps=[tbin])
                prevx[s_] = last
                if kind == 'v':
                    t1 = fw.op('scalar', lambda e, h=half, tc=tc: e.activation(
                        vtv[:, tc, h * 512:(h + 1) * 512], ps[:], AF.Identity), deps=[last])
                    pp.done(ip, [t1])
                    continue
                if kind == 'q':
                    t1 = fw.op('vector', lambda e: e.tensor_scalar(qb, ps[:], 0.125, None, ALU.mult),
                               deps=[last, prev_cons])
                else:
                    t1 = fw.op('vector', lambda e: e.tensor_copy(qb, ps[:]), deps=[last, prev_cons])
                pp.done(ip, [t1])
                dv = qfv if kind == 'q' else kfv
                qb_rel = None
                for h2 in range(4):
                    it, pst, relt = pp.get()
                    pb16a = pst[:, 0:64].bitcast(BF16)
                    tt_ = fw.op('tensor', lambda e, h2=h2, pb16a=pb16a: e.transpose(
                        pb16a, qb[:, h2 * 128:(h2 + 1) * 128], C.idnb), deps=[t1] + relt)
                    c = half * 4 + h2
                    ev = fw.op('scalar', lambda e, c=c, tc=tc, pb16a=pb16a: e.activation(
                        dv[:, c, tc * 128:(tc + 1) * 128], pb16a, AF.Identity), deps=[tt_])
                    pp.done(it, [ev])
                    qb_rel = tt_
                prev_cons = qb_rel
    qkv_pass('q', 0)
    qkv_pass('k', 1024)
    qkv_pass('v', 2048)
    fw.barrier()
    import os
    if os.environ.get('L3DBG'):
        for c in range(NCH):
            fw.dma('gpsimd', dr['OUT'][0, c], qfv[:, c, :])
            fw.dma('gpsimd', dr['OUT'][1, c], kfv[:, c, :])
        fw.barrier()
    sb.pop()
    sb.push()
    pts = [sb.alloc(256, BF16) for _ in range(4)]
    rrow = sb.alloc(512)
    rrec = sb.alloc(512)
    rbv = sb.alloc(512)
    otmp = None
    prel = [[], [], [], []]
    pi = 0
    for h in range(NH):
        hp, ho = h // 2, (h % 2) * 64
        for qt in range(NTT):
            sl = slice(qt * 512, (qt + 1) * 512)
            iR, psR, relR = pp.get(hold=True)
            iO, psO, relO = pp.get(hold=True)
            started = [False] * 4
            lastR = lastO = None
            pieces = []
            for m in range(-1, 4):
                kc = qt * 4 + m
                if kc < 0:
                    continue
                qlo, qhi = max(m, 0), min(m + 1, 3)
                pieces.append((kc, m, qlo, qhi))
            for (kc, m, qlo, qhi) in pieces:
                N = (qhi - qlo + 1) * 128
                iS, psS, relS = pp.get()
                tm = fw.op('tensor', lambda e, kc=kc, qlo=qlo, N=N: e.matmul(
                    psS[:, 0:N], kfv[ho:ho + 64, hp, kc * 128:(kc + 1) * 128],
                    qfv[ho:ho + 64, hp, qt * 512 + qlo * 128:qt * 512 + qlo * 128 + N],
                    start=True, stop=True), deps=relS)
                mts = [tm]
                for qc in range(qlo, qhi + 1):
                    off = (qc - qlo) * 128
                    tri = C.trig if (qt * 4 + qc) == kc else C.tril
                    mts.append(fw.op('vector', lambda e, off=off, tri=tri: e.tensor_tensor(
                        psS[:, off:off + 128], psS[:, off:off + 128], tri, ALU.add), deps=[tm]))
                pt = pts[pi % 4]
                texp = fw.op('scalar', lambda e, pt=pt, N=N: e.activation(pt[:, 0:N], psS[:, 0:N], AF.Exp),
                             deps=mts + prel[pi % 4])
                pp.done(iS, [texp])
                for qc in range(qlo, qhi + 1):
                    off = (qc - qlo) * 128
                    s0 = not started[qc]
                    started[qc] = True
                    stp = (m == qc)
                    lastR = fw.op('tensor', lambda e, pt=pt, off=off, qc=qc, s0=s0, stp=stp: e.matmul(
                        psR[0:1, qc * 128:(qc + 1) * 128], C.onescb, pt[:, off:off + 128],
                        start=s0, stop=stp), deps=[texp] + (relR if s0 else []))
                    lastO = fw.op('tensor', lambda e, pt=pt, off=off, qc=qc, kc=kc, s0=s0, stp=stp: e.matmul(
                        psO[0:64, qc * 128:(qc + 1) * 128], vtv[:, kc, h * 64:(h + 1) * 64],
                        pt[:, off:off + 128], start=s0, stop=stp), deps=[texp] + (relO if s0 else []))
                prel[pi % 4] = [lastR, lastO]
                pi += 1
            t1 = fw.op('vector', lambda e: e.tensor_copy(rrow[0:1, :], psR[0:1, :]), deps=[lastR])
            pp.done(iR, [t1])
            t2 = fw.op('vector', lambda e: e.reciprocal(rrec[0:1, :], rrow[0:1, :]), deps=[t1])
            bi, bp, tb = bcast_row(C, rrec[0:1, :], 512, [t2])
            t3 = fw.op('scalar', lambda e: e.activation(rbv[0:64, :], bp[0:64, :], AF.Identity), deps=[tb, otmp])
            pp.done(bi, [t3])
            t4 = fw.op('vector', lambda e: e.tensor_tensor(
                ov[ho:ho + 64, hp, sl], psO[0:64, :], rbv[0:64, :], ALU.mult), deps=[t3, lastO])
            pp.done(iO, [t4])
            otmp = t4
    fw.barrier()
    sb.pop()
    sb.push()
    wslot = sb.alloc(8 * 512, BF16)
    wv2 = wslot.rearrange("p (c n) -> p c n", c=8)
    tw = fw.dma('gpsimd', wv2, dr['L3_woutT'].rearrange("(c p) n -> p c n", p=128))
    boc = sb.alloc(8)
    tbo = None
    for mc in range(NCH):
        tbo = fw.dma('sync', boc[:, mc:mc + 1], dr['L3_b_out'][mc])
    tmpo = sb.alloc(512)
    dta = sb.alloc(512)
    pv1 = pv2 = None
    for mc in range(NCH):
        for tt in range(NTT):
            sl = slice(tt * 512, (tt + 1) * 512)
            ip, ps, rel = pp.get()
            last = None
            for c in range(NCH):
                last = fw.op('tensor', lambda e, c=c, m=mc, s0=(c == 0): e.matmul(
                    ps[:], wv2[:, c, m * 128:(m + 1) * 128], ov[:, c, sl],
                    start=s0, stop=(c == NCH - 1)), deps=[tw] + rel)
            t1 = fw.op('scalar', lambda e, m=mc: e.activation(tmpo, ps[:], AF.Identity, bias=boc[:, m:m + 1]),
                       deps=[last, tbo, pv1])
            pp.done(ip, [t1])
            t2 = fw.dma('sync', dta, dr['dlt'][seq, mc, :, sl], deps=[pv2], ch=1)
            t3 = fw.op('vector', lambda e: e.tensor_tensor(dta, dta, tmpo, ALU.add), deps=[t1, t2])
            pv1 = t3
            pv2 = fw.dma('sync', dr['dlt'][seq, mc, :, sl], dta, deps=[t3], ch=2)
    fw.barrier()
    sb.pop()
    sb.pop()


def l3_ffn_phase(C, seq):
    fw, pp, sb, dr = C.fw, C.pp, C.sb, C.dr
    sb.push()
    acc = sb.alloc(8 * 2048)
    accv = acc.rearrange("p (c t) -> p c t", c=8)
    xnb = sb.alloc(8 * 1024, BF16)
    xnbv = xnb.rearrange("p (c t) -> p c t", c=8)
    Mfm = sb.alloc(2048)
    ldx = [fw.dma('gpsimd', xnbv[:, mc, :], dr['xnh'][seq, mc]) for mc in range(NCH)]
    tz = fw.memset('vector', acc, 0.0)
    acc_tok = {(mc, tt): tz for mc in range(NCH) for tt in range(NTT)}
    sb.push()
    wgr = sb.alloc(64)
    wgv = wgr.rearrange("p (c e) -> p c e", c=8)
    t_w = fw.dma('sync', wgv, dr['L3_Wg'].rearrange("(c p) e -> p c e", p=128))
    bgrow = sb.alloc(8)
    t_b = fw.dma('sync', bgrow[0:1, :], dr['L3_bg'])
    bgb = sb.alloc(8)
    bi, bp, tb = bcast_row(C, bgrow[0:1, 0:8], 8, [t_b])
    t = fw.op('scalar', lambda e: e.activation(bgb, bp[:, 0:8], AF.Identity), deps=[tb])
    pp.done(bi, [t])
    xf = [sb.alloc(128) for _ in range(2)]
    lrow = sb.alloc(8)
    prob = sb.alloc(8)
    cols = sb.alloc(4)
    mtm = sb.alloc(8)
    prevx = [None, None]
    prev_tr = None
    prev_v = None
    for tc in range(NTC):
        ip, ps, rel = pp.get()
        lastmm = None
        for mc in range(NCH):
            xa = xf[mc % 2]
            tl = fw.dma('sync', xa, dr['xnh'][seq, mc, :, tc * 128:(tc + 1) * 128],
                        deps=[prevx[mc % 2]], ch=mc % 2)
            lastmm = fw.op('tensor', lambda e, a=xa, m=mc, s0=(mc == 0): e.matmul(
                ps[:, 0:8], a, wgv[:, m, :], start=s0, stop=(m == NCH - 1)),
                deps=[tl, t_w] + rel)
            prevx[mc % 2] = lastmm
        t1 = fw.op('vector', lambda e: e.tensor_tensor(lrow, ps[:, 0:8], bgb, ALU.add), deps=[lastmm, t, prev_v])
        pp.done(ip, [t1])
        t2 = fw.op('vector', lambda e: e.reduce_max(cols[:, 0:1], lrow, AX.X), deps=[t1])
        t2 = fw.op('vector', lambda e: e.tensor_scalar(cols[:, 1:2], cols[:, 0:1], -1.0, None, ALU.mult), deps=[t2])
        t3 = fw.op('scalar', lambda e: e.activation(prob, lrow, AF.Exp, bias=cols[:, 1:2]), deps=[t2, prev_v])
        t4 = fw.op('vector', lambda e: e.reduce_sum(cols[:, 2:3], prob, AX.X), deps=[t3])
        t4 = fw.op('vector', lambda e: e.reciprocal(cols[:, 3:4], cols[:, 2:3]), deps=[t4])
        t5 = fw.op('vector', lambda e: e.tensor_scalar(mtm, prob, 1.0, None, ALU.is_ge), deps=[t3])
        t5 = fw.op('vector', lambda e: e.tensor_tensor(mtm, mtm, prob, ALU.mult), deps=[t5])
        t5 = fw.op('vector', lambda e: e.tensor_scalar(mtm, mtm, cols[:, 3:4], None, ALU.mult), deps=[t5, t4])
        prev_v = t5
        it, pst, relt = pp.get()
        tt_ = fw.op('tensor', lambda e: e.transpose(pst[0:8, 0:128], mtm, C.idn), deps=[t5] + relt)
        tev = fw.op('scalar', lambda e, o=Mfm[0:8, tc * 128:(tc + 1) * 128]: e.activation(o, pst[0:8, 0:128], AF.Identity),
                    deps=[tt_, prev_tr])
        pp.done(it, [tev])
        prev_tr = tev
    tch = fw.dma('sync', dr['coefh'][0:8, :], Mfm[0:8, :], deps=[prev_tr])
    fw.barrier()
    sb.pop()
    sb.push()
    w1s = [sb.alloc(8 * 512, BF16)]
    w2s = [sb.alloc(8 * 512, BF16)]
    hm = sb.alloc(8 * 1024, BF16)
    hmv = hm.rearrange("p (c t) -> p c t", c=8)
    mb = sb.alloc(2048)
    b1c = sb.alloc(8)
    b2c = sb.alloc(8)
    crow = sb.alloc(512)
    crow_rel = []
    tmpf = sb.alloc(512)
    wrel = [[], []]
    w2rel = []
    mb_rel = []
    hm_rel = []
    prev_gelu = None
    for e_ in range(8):
        s = 0
        w1v = w1s[0].rearrange("p (c n) -> p c n", c=8)
        w2v = w2s[0].rearrange("p (c n) -> p c n", c=8)
        tw1 = fw.dma('gpsimd', w1v, dr['L3_W1'][e_].rearrange("(c p) n -> p c n", p=128), deps=wrel[s], ch=0)
        tw2 = fw.dma('gpsimd', w2v, dr['L3_W2'][e_].rearrange("(c p) n -> p c n", p=128), deps=w2rel, ch=2)
        tsc = fw.tok('scalar')
        tvv = fw.tok('vector')
        tb1 = tb2 = None
        for ic in range(NCH):
            tb1 = fw.dma('sync', b1c[:, ic:ic + 1], dr['L3_b1'][e_, ic], deps=[tsc])
            tb2 = fw.dma('sync', b2c[:, ic:ic + 1], dr['L3_b2'][e_, ic], deps=[tvv])
        mbts = []
        for tt in range(NTT):
            sl = slice(tt * 512, (tt + 1) * 512)
            tl = fw.dma('sync', crow[0:1, :], dr['coefh'][e_, sl], deps=crow_rel)
            bi, bp, tb = bcast_row(C, crow[0:1, :], 512, mb_rel + [tl])
            crow_rel = [tb]
            t = fw.op('scalar', lambda e, o=mb[:, sl], p=bp: e.activation(o, p[:], AF.Identity), deps=[tb] + mb_rel)
            pp.done(bi, [t])
            mbts.append(t)
        mb_rel = []
        for ic in range(NCH):
            for tt in range(NTT):
                sl = slice(tt * 512, (tt + 1) * 512)
                ip, ps, rel = pp.get()
                last = None
                for mc in range(NCH):
                    last = fw.op('tensor', lambda e, m=mc, i=ic, s0=(mc == 0): e.matmul(
                        ps[:], w1v[:, m, i * 128:(i + 1) * 128], xnbv[:, m, sl],
                        start=s0, stop=(m == NCH - 1)), deps=[tw1] + rel)
                t1 = fw.op('scalar', lambda e, i=ic: e.activation(tmpf, ps[:], AF.Gelu, bias=b1c[:, i:i + 1]),
                           deps=[last, tb1, prev_gelu])
                t2 = fw.op('vector', lambda e, o=hmv[:, ic, sl], c=mb[:, sl]: e.tensor_tensor(o, tmpf, c, ALU.mult),
                           deps=[t1, mbts[tt]] + hm_rel)
                pp.done(ip, [t1])
                prev_gelu = t2
                mb_rel.append(t2)
        hm_rel = []
        for mc in range(NCH):
            for tt in range(NTT):
                sl = slice(tt * 512, (tt + 1) * 512)
                ip, ps, rel = pp.get()
                last = None
                for ic in range(NCH):
                    last = fw.op('tensor', lambda e, i=ic, m=mc, s0=(ic == 0): e.matmul(
                        ps[:], w2v[:, i, m * 128:(m + 1) * 128], hmv[:, i, sl],
                        start=s0, stop=(ic == NCH - 1)), deps=[tw2] + rel + ([prev_gelu] if ic == 0 else []))
                t4 = fw.op('vector', lambda e, o=accv[:, mc, sl]: e.tensor_tensor(o, o, ps[:], ALU.add),
                           deps=[last, acc_tok[(mc, tt)]])
                t4 = fw.op('vector', lambda e, o=accv[:, mc, sl], m=mc: e.scalar_tensor_tensor(
                    o, mb[:, sl], b2c[:, m:m + 1], o, ALU.mult, ALU.add), deps=[t4, tb2, mbts[tt]])
                pp.done(ip, [t4])
                acc_tok[(mc, tt)] = t4
                mb_rel.append(t4)
            hm_rel = [fw.tok('tensor')]
        wrel[s] = [fw.tok('tensor')]
        w2rel = [fw.tok('tensor')]
    for mc in range(NCH):
        fw.dma('sync', dr['dlt'][seq, mc], accv[:, mc, :],
               deps=[acc_tok[(mc, tt)] for tt in range(NTT)])
    fw.barrier()
    sb.pop()
    sb.pop()


def kernel(upto=None, skip=None, **inputs):
    _upto = 4 if upto is None else upto
    _skip = set() if skip is None else set(skip)
    inputs = {k: np.asarray(v, dtype=np.float32) for k, v in inputs.items()}
    weights = prep_weights(inputs)
    consts = host_consts()
    x = inputs['L0_attn_x']
    nc = build_program(weights, consts, upto=_upto, skip=_skip)
    shared = {}
    for k, v in list(weights.items()) + list(consts.items()):
        shared[k] = np.ascontiguousarray(v, dtype=np.float32)
    in_maps = []
    for c in range(8):
        m = dict(shared)
        xs = x[2 * c:2 * c + 2]
        xs = np.ascontiguousarray(xs.transpose(0, 2, 1)).reshape(NSEQ, NCH, 128, T)
        m['XIN'] = xs
        in_maps.append(m)
    res = run_bass_kernel_spmd(nc, in_maps, core_ids=list(range(8)))
    outs = []
    for c in range(8):
        o = res.results[c]['OUT'].reshape(NSEQ, D, T).transpose(0, 2, 1)
        outs.append(o)
    return np.ascontiguousarray(np.concatenate(outs, axis=0), dtype=np.float32)


# version 03
# speedup vs baseline: 1.2574x; 1.2574x over previous
import math
import types
import numpy as np
from contextlib import ExitStack

import concourse.bass as bass
import concourse.mybir as mybir
from concourse.bass_utils import run_bass_kernel_spmd

F32 = mybir.dt.float32
BF16 = mybir.dt.bfloat16
AF = mybir.ActivationFunctionType
ALU = mybir.AluOpType
AX = mybir.AxisListType

ENGS = ('tensor', 'vector', 'scalar', 'gpsimd', 'sync')
QCH = [f'{q}{i}' for q in ('sync', 'gpsimd') for i in range(4)]
ALLSEM = list(ENGS) + QCH
NSEQ = 2
T = 2048
D = 1024
NCH = 8
NTC = 16
NTT = 4


class FW:
    def __init__(self, nc):
        self.nc = nc
        self.streams = {e: [] for e in ENGS}
        self.cnt = {e: 0 for e in ALLSEM}
        self.sems = {}
        self.lastwait = {}

    def wait(self, eng, deps):
        for d in deps:
            if d is None:
                continue
            se, v = d
            if v <= 0:
                continue
            assert v <= self.cnt[se], f"dep on future {se}#{v} (have {self.cnt[se]})"
            if se not in ENGS:
                v = self.cnt[se]
            k = (eng, se)
            if self.lastwait.get(k, 0) >= v:
                continue
            self.lastwait[k] = v
            sem = self.sems[se]
            self.streams[eng].append(lambda e, sem=sem, v=v: e.wait_ge(sem, v))

    @staticmethod
    def _freeze(f):
        if not f.__closure__:
            return f
        cells = tuple(types.CellType(c.cell_contents) for c in f.__closure__)
        return types.FunctionType(f.__code__, f.__globals__, f.__name__,
                                  f.__defaults__, cells)

    def op(self, eng, f, deps=(), inc=1, sname=None):
        f = self._freeze(f)
        self.wait(eng, deps)
        sname = sname or eng
        self.cnt[sname] += inc
        sem = self.sems[sname]

        def closure(e, f=f, sem=sem, inc=inc):
            f(e).then_inc(sem, inc)
        self.streams[eng].append(closure)
        return (sname, self.cnt[sname])

    def memset(self, eng, ap, val, deps=()):
        return self.op(eng, lambda e, a=ap, v=val: e.memset(a, v), deps=deps)

    def dma(self, q, out, in_, deps=(), ch=0):
        sname = f'{q}{ch}'
        deps = list(deps) + [(sname, self.cnt[sname])]
        return self.op(q, lambda e, o=out, i=in_: e.dma_start(out=o, in_=i),
                       deps=deps, inc=16, sname=sname)

    def tok(self, eng):
        return (eng, self.cnt[eng])

    def all_toks(self):
        return [(e, self.cnt[e]) for e in ALLSEM]

    def barrier(self):
        toks = self.all_toks()
        for e in ENGS:
            self.wait(e, toks)

    def finalize(self, block, final_deps=()):
        self.wait('sync', final_deps)
        for e in ENGS:
            stream = self.streams[e]
            getattr(block, e)(lambda eng, stream=stream: [f(eng) for f in stream])


class SB:
    def __init__(self, handle, words):
        self.h = handle
        self.off = 0
        self.words = words
        self.stack = []
        self.hiwater = 0

    def alloc(self, words, dt=F32):
        off = self.off
        self.off += words
        self.hiwater = max(self.hiwater, self.off)
        assert self.off <= self.words, f"arena overflow {self.off} > {self.words}"
        ap = self.h[:, off:off + words]
        return ap if dt == F32 else ap.bitcast(dt)

    def push(self):
        self.stack.append(self.off)

    def pop(self):
        self.off = self.stack.pop()


class PP:
    def __init__(self, tiles):
        self.t = tiles
        self.rel = [[] for _ in tiles]
        self.held = [False] * len(tiles)
        self.i = 0

    def get(self, hold=False):
        for _ in range(len(self.t)):
            i = self.i % len(self.t)
            self.i += 1
            if not self.held[i]:
                self.held[i] = hold
                return i, self.t[i], list(self.rel[i])
        raise RuntimeError("all psum banks held")

    def done(self, i, toks):
        self.rel[i] = [t for t in toks if t is not None]
        self.held[i] = False


def host_consts():
    c = {}
    c['IDN'] = np.eye(128, dtype=np.float32)
    c['ONES'] = np.ones((128, 128), np.float32)
    a = np.arange(128)
    c['TRIG'] = np.where(a[:, None] > a[None, :], -1e30, 0.0).astype(np.float32)
    c['TRIL'] = np.where(a[:, None] < a[None, :], -1e30, 0.0).astype(np.float32)
    c['SGN'] = np.sign(a[:, None] - a[None, :]).astype(np.float32)
    half = 64
    freqs = np.power(10000.0, -np.arange(half, dtype=np.float64) / half)
    rad = np.arange(T, dtype=np.float64)[:, None] * freqs[None, :]
    s = 1.0 / math.sqrt(128.0)
    c['COS0'] = (np.cos(rad) * s).astype(np.float32)
    c['SIN0'] = (np.sin(rad) * s).astype(np.float32)
    DH = 64
    hi = np.arange(0, DH, 2, dtype=np.float64)
    inv_freq = 1.0 / 10000.0 ** (hi / DH)
    tt_ = np.arange(T, dtype=np.float64)
    fr = tt_[:, None] * inv_freq[None, :]
    fr = np.concatenate([fr, fr], axis=-1)
    base = (hi + 0.4 * DH) / (1.4 * DH)
    power = (tt_ - T // 2) / 256.0
    scale = base[None, :] ** power[:, None]
    scale = np.concatenate([scale, scale], axis=-1)
    c['XCQ'] = (np.cos(fr) * scale * 8.0).astype(np.float32)
    c['XSQ'] = (np.sin(fr) * scale * 8.0).astype(np.float32)
    c['XCK'] = (np.cos(fr) / scale).astype(np.float32)
    c['XSK'] = (np.sin(fr) / scale).astype(np.float32)
    c['POSN'] = np.ascontiguousarray((np.arange(T, dtype=np.float64) / T).astype(np.float32).reshape(16, 128).T)
    n = 4096
    NF = 2176
    f = np.arange(NF, dtype=np.float64)
    tfull = np.arange(n, dtype=np.float64)
    ang = 2.0 * np.pi * tfull[:, None] * f[None, :] / n
    cre = np.cos(ang)
    cim = -np.sin(ang)
    cre[:, 2049:] = 0.0
    cim[:, 2049:] = 0.0
    c['C4R'] = cre.astype(np.float32)
    c['C4I'] = cim.astype(np.float32)
    w = np.full(NF, 2.0 / n, np.float64)
    w[0] = 1.0 / n
    w[2048] = 1.0 / n
    w[2049:] = 0.0
    ii = np.arange(T, dtype=np.float64)
    ang2 = 2.0 * np.pi * f[:, None] * ii[None, :] / n
    c['AMAT'] = (w[:, None] * np.cos(ang2)).astype(np.float32)
    c['BMAT'] = (-w[:, None] * np.sin(ang2)).astype(np.float32)
    tlin = np.linspace(0.0, 1.0, n)
    deltas = np.linspace(math.log(0.01) / 1.5, math.log(0.01) / 0.3, D)
    c['DECAY'] = (np.exp(-tlin[:, None] * np.abs(deltas)[None, :]) + 0.05).astype(np.float32)
    t01 = np.linspace(0.0, 1.0, n)
    tr = np.linspace(0.0, n - 1.0, n)
    wang = 2.0 * np.pi * tr / n
    fband = np.array([0.0001])
    fw_ = fband[None, :] * wang[:, None]
    c['ZAUG'] = np.stack([t01, np.cos(fw_)[:, 0], -np.sin(fw_)[:, 0], np.ones(n)], 0).astype(np.float32)
    c['NEGIDX'] = (-np.arange(64, dtype=np.float64) * 1e-8).astype(np.float32).reshape(1, 64)
    c['ONECOL'] = np.ones((128, 1), np.float32)
    return c


def prep_weights(inp):
    w = {}
    cp = np.ascontiguousarray
    w['L0_wq'] = inp['L0_attn_wq'].reshape(1024, 1024)
    w['L0_wk'] = inp['L0_attn_wk'].reshape(1024, 1024)
    w['L0_wv'] = inp['L0_attn_wv'].reshape(1024, 1024)
    w['L0_wo'] = inp['L0_attn_wo'].reshape(1024, 1024)
    w['L0_guT'] = cp(inp['L0_ffn_experts_gate_up'].transpose(0, 2, 1))
    w['L0_dnT'] = cp(inp['L0_ffn_experts_down'].transpose(0, 2, 1))
    w['L0_gwT'] = cp(inp['L0_ffn_gate_weight'].T)
    w['L0_ebias'] = inp['L0_ffn_e_score_correction_bias'].reshape(1, 32)
    w['L0_sgT'] = cp(inp['L0_ffn_shared_gate_w'].T)
    w['L0_suT'] = cp(inp['L0_ffn_shared_up_w'].T)
    w['L0_sdT'] = cp(inp['L0_ffn_shared_down_w'].T)
    w['L1_WqT'] = cp(inp['L1_attn_Wq'].T)
    w['L1_WkT'] = cp(inp['L1_attn_Wk'].T)
    w['L1_WvT'] = cp(inp['L1_attn_Wv'].T)
    w['L1_lng'] = inp['L1_attn_ln_gamma'].reshape(8, 128, 1)
    w['L1_lnb'] = inp['L1_attn_ln_beta'].reshape(8, 128, 1)
    w['L1_rwT'] = cp(inp['L1_ffn_router_w'].T)
    w['L1_ebias'] = inp['L1_ffn_e_bias'].reshape(1, 64)
    w['L1_gu'] = cp(np.concatenate([inp['L1_ffn_Wg'], inp['L1_ffn_Wu']], axis=2))
    w['L1_dn'] = inp['L1_ffn_Wd']
    w['L1_sgT'] = inp['L1_ffn_SWg']
    w['L1_suT'] = inp['L1_ffn_SWu']
    w['L1_sdT'] = inp['L1_ffn_SWd']
    w['L2_w_in'] = inp['L2_attn_w_in']
    w['L2_b_in'] = inp['L2_attn_b_in'].reshape(24, 128, 1)
    w['L2_sf_w'] = inp['L2_attn_sf_w']
    w['L2_sf_b'] = inp['L2_attn_sf_b'].reshape(24, 128, 1)
    for k in ('f_w1', 'f_w2', 'f_w3', 'f_wout', 'w_out'):
        w['L2_' + k] = inp['L2_attn_' + k]
    for k in ('f_b1', 'f_b2', 'f_b3'):
        w['L2_' + k] = inp['L2_attn_' + k].reshape(1, 64)
    for k in ('freq1', 'freq2', 'freq3'):
        w['L2_' + k] = inp['L2_attn_' + k].reshape(64, 1)
    w['L2_sfw_cols'] = np.ascontiguousarray(inp['L2_attn_sf_w'].T).reshape(3, 24, 128, 1)
    w['L2_filt_bias'] = inp['L2_attn_filt_bias'].reshape(8, 128, 1)
    w['L2_b_out'] = inp['L2_attn_b_out'].reshape(8, 128, 1)
    HID, HP = 2730, 2816
    w1 = inp['L2_ffn_w1']
    w1p = np.zeros((2, 1024, 2 * HP), np.float32)
    w1p[:, :, :HID] = w1[:, :, :HID]
    w1p[:, :, HP:HP + HID] = w1[:, :, HID:]
    w['L2f_w1p'] = w1p
    w2p = np.zeros((2, HP, 1024), np.float32)
    w2p[:, :HID, :] = inp['L2_ffn_w2']
    w['L2f_w2p'] = w2p
    b1 = inp['L2_ffn_b1']
    b1p = np.zeros((2, 2 * HP), np.float32)
    b1p[:, :HID] = b1[:, :HID]
    b1p[:, HP:HP + HID] = b1[:, HID:]
    w['L2f_b1p'] = b1p.reshape(2, 44, 128, 1)
    w['L2f_b2'] = inp['L2_ffn_b2'].reshape(2, 8, 128, 1)
    w['L2f_wqkv'] = inp['L2_ffn_wqkv']
    w['L2f_wo'] = inp['L2_ffn_wo']
    w['L2f_qs'] = inp['L2_ffn_q_scale'].reshape(2, 1, 64)
    w['L2f_ks'] = inp['L2_ffn_k_scale'].reshape(2, 1, 64)
    w['L2f_lng'] = inp['L2_ffn_ln_g'].reshape(2, 8, 128, 1)
    w['L2f_lnb'] = inp['L2_ffn_ln_b'].reshape(2, 8, 128, 1)
    w['L3_winT'] = cp(inp['L3_attn_w_in'].T)
    w['L3_b_in'] = inp['L3_attn_b_in'].reshape(1, 3072)
    w['L3_woutT'] = cp(inp['L3_attn_w_out'].T)
    w['L3_b_out'] = inp['L3_attn_b_out'].reshape(8, 128, 1)
    w['L3_Wg'] = inp['L3_ffn_Wg']
    w['L3_bg'] = inp['L3_ffn_bg'].reshape(1, 8)
    w['L3_W1'] = inp['L3_ffn_W1']
    w['L3_b1'] = inp['L3_ffn_b1'].reshape(8, 8, 128, 1)
    w['L3_W2'] = inp['L3_ffn_W2']
    w['L3_b2'] = inp['L3_ffn_b2'].reshape(8, 8, 128, 1)
    return w


class Ctx:
    pass


def mk_dram(nc, weights, consts):
    d = {}
    for k, v in list(weights.items()) + list(consts.items()):
        d[k] = nc.dram_tensor(k, list(v.shape), F32, kind="ExternalInput").ap()
    d['XIN'] = nc.dram_tensor('XIN', [NSEQ, NCH, 128, T], F32, kind="ExternalInput").ap()
    d['OUT'] = nc.dram_tensor('OUT', [NSEQ, NCH, 128, T], F32, kind="ExternalOutput").ap()
    for nm, shp in [('xres', [NSEQ, NCH, 128, T]), ('xnh', [NSEQ, NCH, 128, T]),
                    ('dlt', [NSEQ, NCH, 128, T]), ('x0h', [NCH, 128, T]),
                    ('vh', [NCH, 128, T]), ('KRE', [17, 128, 1024]),
                    ('KIM', [17, 128, 1024]), ('coefh', [64, T])]:
        d[nm] = nc.dram_tensor(nm, shp, F32, kind="Internal").ap()
    return d


def emit_statics(C):
    fw, dr, sb = C.fw, C.dr, C.sb
    C.idn = sb.alloc(128)
    C.idnb = sb.alloc(64, BF16)
    C.trig = sb.alloc(128)
    C.tril = sb.alloc(128)
    C.sgn = sb.alloc(128)
    C.onesc = sb.alloc(1)
    C.onescb = sb.alloc(1, BF16)[:, 0:1]
    C.onesr = sb.alloc(128)
    C.onesrb = sb.alloc(64, BF16)
    C.posn = sb.alloc(16)
    C.negidx = sb.alloc(64)
    C.negc = sb.alloc(1)
    fw.memset('vector', C.negc, -1.0)
    fw.dma('gpsimd', C.idn, dr['IDN'])
    fw.dma('gpsimd', C.idnb, dr['IDN'])
    fw.dma('gpsimd', C.trig, dr['TRIG'])
    fw.dma('gpsimd', C.tril, dr['TRIL'])
    fw.dma('gpsimd', C.sgn, dr['SGN'])
    fw.dma('gpsimd', C.onesc, dr['ONECOL'])
    fw.dma('gpsimd', C.onescb, dr['ONECOL'])
    fw.dma('gpsimd', C.onesr[0:1, :], dr['ONES'][0:1, :])
    fw.dma('gpsimd', C.onesrb[0:1, :], dr['ONES'][0:1, :])
    fw.dma('gpsimd', C.posn, dr['POSN'])
    fw.dma('gpsimd', C.negidx[0:1, :], dr['NEGIDX'])
    fw.barrier()


def bcast_row(C, row_ap, n, deps, bf=False):
    i, ps, rel = C.pp.get()
    lhs = C.onesrb[0:1, :] if bf else C.onesr[0:1, :]
    t = C.fw.op('tensor', lambda e, r=row_ap, p=ps: e.matmul(
        p[:, 0:n], lhs, r, start=True, stop=True), deps=list(deps) + rel)
    return i, ps, t


def ln_norm_tiles(C, tiles, ttoks, deps0, eps=1e-5):
    fw, pp, sb = C.fw, C.pp, C.sb
    mub = sb.alloc(2048)
    rstdb = sb.alloc(2048)
    xsq = sb.alloc(512)
    r_mu = sb.alloc(512)
    r_v = sb.alloc(512)
    r_t = sb.alloc(512)
    toks = []
    prev_sq_reader = None
    for tt in range(NTT):
        i1, ps1, r1 = pp.get()
        i2, ps2, r2 = pp.get()
        for mc in range(NCH):
            fw.op('tensor', lambda e, a=tiles[(mc, tt)], s=(mc == 0): e.matmul(
                ps1[0:1, :], C.onesc, a, start=s, stop=(mc == NCH - 1)),
                deps=list(deps0) + r1 + [ttoks[(mc, tt)]])
        for mc in range(NCH):
            tsq = fw.op('scalar', lambda e, a=tiles[(mc, tt)]: e.activation(xsq, a, AF.Square),
                        deps=[ttoks[(mc, tt)], prev_sq_reader])
            prev_sq_reader = fw.op('tensor', lambda e, s=(mc == 0): e.matmul(
                ps2[0:1, :], C.onesc, xsq, start=s, stop=(mc == NCH - 1)),
                deps=[tsq] + r2)
        tmm = fw.tok('tensor')
        sl = slice(tt * 512, (tt + 1) * 512)
        t1 = fw.op('vector', lambda e: e.tensor_scalar(r_mu[0:1, :], ps1[0:1, :], 1.0 / D, None, ALU.mult), deps=[tmm])
        t2 = fw.op('vector', lambda e: e.tensor_scalar(r_v[0:1, :], ps2[0:1, :], 1.0 / D, None, ALU.mult), deps=[tmm])
        t3 = fw.op('vector', lambda e: e.tensor_tensor(r_t[0:1, :], r_mu[0:1, :], r_mu[0:1, :], ALU.mult), deps=[t1])
        t4 = fw.op('vector', lambda e: e.tensor_tensor(r_v[0:1, :], r_v[0:1, :], r_t[0:1, :], ALU.subtract), deps=[t2, t3])
        t5 = fw.op('vector', lambda e: e.tensor_scalar(r_v[0:1, :], r_v[0:1, :], eps, None, ALU.add), deps=[t4])
        t6 = fw.op('scalar', lambda e: e.activation(r_v[0:1, :], r_v[0:1, :], AF.Sqrt), deps=[t5])
        t7 = fw.op('vector', lambda e: e.reciprocal(r_t[0:1, :], r_v[0:1, :]), deps=[t6])
        pp.done(i1, [t1])
        pp.done(i2, [t2])
        bi1, bp1, tb1 = bcast_row(C, r_mu[0:1, :], 512, [t1, t3])
        bi2, bp2, tb2 = bcast_row(C, r_t[0:1, :], 512, [t7])
        e1 = fw.op('scalar', lambda e, o=mub[:, sl], p=bp1: e.activation(o, p[:], AF.Identity), deps=[tb1])
        e2 = fw.op('scalar', lambda e, o=rstdb[:, sl], p=bp2: e.activation(o, p[:], AF.Identity), deps=[tb2])
        pp.done(bi1, [e1])
        pp.done(bi2, [e2])
        prev_sq_reader = prev_sq_reader
        toks += [e1, e2]
    return mub, rstdb, toks


def out_phase(C, L, seq, have_delta):
    fw, pp, sb, dr = C.fw, C.pp, C.sb, C.dr
    first = (L == 0)
    last = (L == C.upto - 1)
    sb.push()
    xnew = sb.alloc(8 * 2048)
    xv = xnew.rearrange("p (c t) -> p c t", c=8)
    xsrc = dr['XIN'] if first else dr['xres']
    xdst = dr['OUT'] if last else dr['xres']
    xa = sb.alloc(512)
    da = sb.alloc(512)
    tiles, ttoks = {}, {}
    prev = {}
    for tt in range(NTT):
        sl = slice(tt * 512, (tt + 1) * 512)
        for mc in range(NCH):
            t1 = fw.dma('sync', xa, xsrc[seq, mc, :, sl], deps=[prev.get('xa')])
            if have_delta:
                t2 = fw.dma('sync', da, dr['dlt'][seq, mc, :, sl], deps=[prev.get('da')], ch=1)
                t3 = fw.op('vector', lambda e, o=xv[:, mc, sl]: e.tensor_tensor(o, xa, da, ALU.add), deps=[t1, t2])
                prev['da'] = t3
            else:
                t3 = fw.op('vector', lambda e, o=xv[:, mc, sl]: e.tensor_copy(o, xa), deps=[t1])
            prev['xa'] = t3
            tiles[(mc, tt)] = xv[:, mc, sl]
            ttoks[(mc, tt)] = t3
            fw.dma('sync', xdst[seq, mc, :, sl], xv[:, mc, sl], deps=[t3], ch=2)
    if not last:
        mub, rstdb, stoks = ln_norm_tiles(C, tiles, ttoks, [])
        tmp = sb.alloc(512)
        ptmp = None
        for tt in range(NTT):
            sl = slice(tt * 512, (tt + 1) * 512)
            for mc in range(NCH):
                t1 = fw.op('vector', lambda e, a=tiles[(mc, tt)], m=mub[:, sl]: e.tensor_tensor(tmp, a, m, ALU.subtract),
                           deps=stoks + [ttoks[(mc, tt)], ptmp])
                t2 = fw.op('vector', lambda e, r=rstdb[:, sl]: e.tensor_tensor(tmp, tmp, r, ALU.mult), deps=[t1])
                ptmp = fw.dma('sync', dr['xnh'][seq, mc, :, sl], tmp, deps=[t2], ch=2)
    fw.barrier()
    sb.pop()


def load_wblock(C, dst, dram2d, r0, r1, c0, c1, deps=()):
    nr = r1 - r0
    src = dram2d[r0:r1, c0:c1].rearrange("(c p) n -> p c n", p=128)
    return C.fw.dma('gpsimd', dst, src, deps=deps)


def moe_phase(C, L, seq):
    import os
    dbg = os.environ.get('MOE_DBG', 'rse')
    fw, pp, sb, dr = C.fw, C.pp, C.sb, C.dr
    E = 32 if L == 0 else 64
    NG = 8
    GS = E // NG
    TOPK = 6 if L == 0 else 8
    gwT = dr['L0_gwT'] if L == 0 else dr['L1_rwT']
    ebias = dr['L0_ebias'] if L == 0 else dr['L1_ebias']
    guD = dr['L0_guT'] if L == 0 else dr['L1_gu']
    dnD = dr['L0_dnT'] if L == 0 else dr['L1_dn']
    sg, su, sd = (dr[f'L{L}_sgT'], dr[f'L{L}_suT'], dr[f'L{L}_sdT'])
    sb.push()
    acc = sb.alloc(8 * 2048)
    accv = acc.rearrange("p (c t) -> p c t", c=8)
    xnb = sb.alloc(8 * 1024, BF16)
    xnbv = xnb.rearrange("p (c t) -> p c t", c=8)
    coef = sb.alloc(2048)
    ldx = [fw.dma('gpsimd', xnbv[:, mc, :], dr['xnh'][seq, mc]) for mc in range(NCH)]

    if 'r' not in dbg:
        tzz = fw.memset('vector', coef[0:E, :], 0.01, deps=[fw.tok('scalar')])
        fw.dma('sync', dr['coefh'][0:E, :], coef[0:E, :], deps=[tzz])
        fw.barrier()
    else:
        moe_routing(C, L, seq, E, NG, GS, TOPK, gwT, ebias, coef)
    if 's' in dbg:
        moe_shared(C, seq, sg, su, sd, acc, accv, xnbv, ldx)
    else:
        tzz = fw.memset('vector', acc, 0.0)
        fw.barrier()
    if 'e' in dbg:
        moe_experts(C, L, seq, E, guD, dnD, acc, accv, xnbv)
    for mc in range(NCH):
        fw.dma('sync', dr['dlt'][seq, mc], accv[:, mc, :], deps=[fw.tok('vector'), fw.tok('scalar')])
    fw.barrier()
    sb.pop()
    return


def moe_routing(C, L, seq, E, NG, GS, TOPK, gwT, ebias, coef):
    fw, pp, sb, dr = C.fw, C.pp, C.sb, C.dr
    sb.push()
    rw = sb.alloc(8 * E)
    rwv = rw.rearrange("p (c e) -> p c e", c=8)
    trw = fw.dma('sync', rwv, gwT.rearrange("(c p) e -> p c e", p=128))
    ebrow = sb.alloc(E)
    teb = fw.dma('sync', ebrow[0:1, :], ebias)
    ebb = sb.alloc(E)
    nib = sb.alloc(E)
    bi, bp, tb = bcast_row(C, ebrow[0:1, :], E, [teb])
    t = fw.op('scalar', lambda e: e.activation(ebb, bp[:, 0:E], AF.Identity), deps=[tb])
    pp.done(bi, [t])
    bi, bp, tb = bcast_row(C, C.negidx[0:1, 0:E], E, [t])
    t = fw.op('scalar', lambda e: e.activation(nib, bp[:, 0:E], AF.Identity), deps=[tb])
    pp.done(bi, [t])
    xf = [sb.alloc(128) for _ in range(2)]
    score = sb.alloc(E)
    sfc = sb.alloc(E)
    gsc = sb.alloc(NG)
    gwork = sb.alloc(NG)
    gmask = sb.alloc(NG)
    gtmp = sb.alloc(E)
    msk = sb.alloc(E)
    sel = sb.alloc(E)
    wrow = sb.alloc(E)
    colp = sb.alloc(4)
    prevx = [None, None]
    prev_tr = None
    for tc in range(NTC):
        ip, ps, rel = pp.get()
        lastmm = None
        for mc in range(NCH):
            xa = xf[mc % 2]
            tl = fw.dma('sync', xa, dr['xnh'][seq, mc, :, tc * 128:(tc + 1) * 128],
                        deps=[prevx[mc % 2]], ch=mc % 2)
            lastmm = fw.op('tensor', lambda e, a=xa, s=(mc == 0): e.matmul(
                ps[:, 0:E], a, rwv[:, mc, :], start=s, stop=(mc == NCH - 1)),
                deps=[tl, trw] + rel)
            prevx[mc % 2] = lastmm
        t0 = fw.op('scalar', lambda e: e.activation(score, ps[:, 0:E], AF.Sigmoid), deps=[lastmm, fw.tok('vector')])
        pp.done(ip, [t0])
        t1 = fw.op('vector', lambda e: e.tensor_tensor(sfc, score, ebb, ALU.add), deps=[t0])
        sfc3 = sfc.rearrange("p (g s) -> p g s", g=NG)
        if L == 0:
            tg = fw.op('vector', lambda e: e.reduce_max(gsc.rearrange("p (g o) -> p g o", g=NG), sfc3, AX.X), deps=[t1])
        else:
            gt3 = gtmp.rearrange("p (g s) -> p g s", g=NG)
            tm1 = fw.op('vector', lambda e: e.reduce_max(gwork.rearrange("p (g o) -> p g o", g=NG), sfc3, AX.X), deps=[t1])
            to = fw.op('vector', lambda e: e.tensor_tensor(
                gt3, sfc3, gwork.rearrange("p (g o) -> p g o", g=NG).broadcast_to([128, NG, GS]), ALU.is_ge), deps=[tm1])
            tv = fw.op('vector', lambda e: e.scalar_tensor_tensor(gt3, gt3, -1e30, sfc3, ALU.mult, ALU.add), deps=[to])
            tm2 = fw.op('vector', lambda e: e.reduce_max(gsc.rearrange("p (g o) -> p g o", g=NG), gt3, AX.X), deps=[tv])
            tg = fw.op('vector', lambda e: e.tensor_tensor(gsc, gsc, gwork, ALU.add), deps=[tm2])
        tg = fw.op('vector', lambda e: e.tensor_copy(gwork, gsc), deps=[tg])
        tg = fw.memset('vector', gmask, 0.0, deps=[tg])
        for r in range(4):
            tg = fw.op('vector', lambda e: e.reduce_max(colp[:, 0:1], gwork, AX.X), deps=[tg])
            tg = fw.op('vector', lambda e: e.tensor_scalar(gtmp[:, 0:NG], gwork, colp[:, 0:1], None, ALU.is_ge), deps=[tg])
            tg = fw.op('vector', lambda e: e.tensor_tensor(gmask, gmask, gtmp[:, 0:NG], ALU.add), deps=[tg])
            tg = fw.op('vector', lambda e: e.scalar_tensor_tensor(gwork, gtmp[:, 0:NG], -1e30, gwork, ALU.mult, ALU.add), deps=[tg])
        g3 = gmask.rearrange("p (g o) -> p g o", g=NG)
        m3 = msk.rearrange("p (g s) -> p g s", g=NG)
        tg = fw.op('vector', lambda e: e.tensor_tensor(m3, sfc3, g3.broadcast_to([128, NG, GS]), ALU.mult), deps=[tg, t1])
        tg2 = fw.op('vector', lambda e: e.tensor_scalar(gwork, gmask, -1.0, 1.0, ALU.mult, ALU.add), deps=[tg])
        tg2 = fw.op('vector', lambda e: e.tensor_tensor(
            gtmp.rearrange("p (g s) -> p g s", g=NG), nib.rearrange("p (g s) -> p g s", g=NG),
            gwork.rearrange("p (g o) -> p g o", g=NG).broadcast_to([128, NG, GS]), ALU.mult), deps=[tg2])
        tg = fw.op('vector', lambda e: e.tensor_tensor(msk, msk, gtmp, ALU.add), deps=[tg2])
        ts_ = fw.memset('vector', sel, 0.0, deps=[tg])
        for r in range(TOPK):
            ts_ = fw.op('vector', lambda e: e.reduce_max(colp[:, 0:1], msk, AX.X), deps=[ts_])
            ts_ = fw.op('vector', lambda e: e.tensor_scalar(gtmp[:, 0:E] if False else wrow, msk, colp[:, 0:1], None, ALU.is_ge), deps=[ts_])
            ts_ = fw.op('vector', lambda e: e.tensor_tensor(sel, sel, wrow, ALU.add), deps=[ts_])
            ts_ = fw.op('vector', lambda e: e.scalar_tensor_tensor(msk, wrow, -1e30, msk, ALU.mult, ALU.add), deps=[ts_])
        ts_ = fw.op('vector', lambda e: e.tensor_tensor(wrow, sel, score, ALU.mult), deps=[ts_, t0])
        ts_ = fw.op('vector', lambda e: e.reduce_sum(colp[:, 1:2], wrow, AX.X), deps=[ts_])
        ts_ = fw.op('vector', lambda e: e.tensor_scalar(colp[:, 1:2], colp[:, 1:2], 1e-20, None, ALU.add), deps=[ts_])
        ts_ = fw.op('vector', lambda e: e.reciprocal(colp[:, 2:3], colp[:, 1:2]), deps=[ts_])
        ts_ = fw.op('vector', lambda e: e.tensor_scalar(wrow, wrow, colp[:, 2:3], 2.5, ALU.mult, ALU.mult), deps=[ts_])
        it, pst, relt = pp.get()
        tt_ = fw.op('tensor', lambda e: e.transpose(pst[0:E, 0:128], wrow, C.idn), deps=[ts_] + relt)
        tev = fw.op('scalar', lambda e, o=coef[0:E, tc * 128:(tc + 1) * 128]: e.activation(o, pst[0:E, 0:128], AF.Identity),
                    deps=[tt_, prev_tr])
        pp.done(it, [tev])
        prev_tr = tev
    tch = fw.dma('sync', dr['coefh'][0:E, :], coef[0:E, :], deps=[prev_tr])
    fw.barrier()
    sb.pop()


def moe_shared(C, seq, sg, su, sd, acc, accv, xnbv, ldx):
    fw, pp, sb, dr = C.fw, C.pp, C.sb, C.dr
    sb.push()
    hsh = sb.alloc(8 * 1024, BF16)
    hshv = hsh.rearrange("p (c t) -> p c t", c=8)
    wga = [sb.alloc(8 * 64, BF16) for _ in range(2)]
    wua = [sb.alloc(8 * 64, BF16) for _ in range(2)]
    tmpf = sb.alloc(512)
    relg = [None, None]
    for ic in range(NCH):
        s = ic % 2
        wg = wga[s].rearrange("p (c n) -> p c n", c=8)
        wu = wua[s].rearrange("p (c n) -> p c n", c=8)
        tg = fw.dma('gpsimd', wg, sg[0:1024, ic * 128:(ic + 1) * 128].rearrange("(c p) n -> p c n", p=128), deps=[relg[s]], ch=s)
        tu = fw.dma('gpsimd', wu, su[0:1024, ic * 128:(ic + 1) * 128].rearrange("(c p) n -> p c n", p=128), deps=[relg[s]], ch=2 + s)
        lastu = None
        for tt in range(NTT):
            sl = slice(tt * 512, (tt + 1) * 512)
            ig, pg, rg = pp.get()
            iu, pu, ru = pp.get()
            for mc in range(NCH):
                fw.op('tensor', lambda e, m=mc, s0=(mc == 0): e.matmul(
                    pg[:], wg[:, m, :], xnbv[:, m, sl], start=s0, stop=(m == NCH - 1)),
                    deps=[tg] + ldx + rg)
            for mc in range(NCH):
                lastu = fw.op('tensor', lambda e, m=mc, s0=(mc == 0): e.matmul(
                    pu[:], wu[:, m, :], xnbv[:, m, sl], start=s0, stop=(m == NCH - 1)),
                    deps=[tu] + ru)
            t1 = fw.op('scalar', lambda e: e.activation(tmpf, pg[:], AF.Silu), deps=[lastu, fw.tok('vector')])
            t2 = fw.op('vector', lambda e, o=hshv[:, ic, sl]: e.tensor_tensor(o, tmpf, pu[:], ALU.mult), deps=[t1])
            pp.done(ig, [t1])
            pp.done(iu, [t2])
        relg[s] = fw.tok('tensor')
    wda = [sb.alloc(8 * 64, BF16) for _ in range(2)]
    reld = [None, None]
    hold = fw.tok('vector')
    accw = []
    for mc in range(NCH):
        s = mc % 2
        wd = wda[s].rearrange("p (c n) -> p c n", c=8)
        td = fw.dma('gpsimd', wd, sd[0:1024, mc * 128:(mc + 1) * 128].rearrange("(c p) n -> p c n", p=128), deps=[reld[s]], ch=s)
        for tt in range(NTT):
            sl = slice(tt * 512, (tt + 1) * 512)
            idx, pd, rd = pp.get()
            last = None
            for ic in range(NCH):
                last = fw.op('tensor', lambda e, i=ic, s0=(ic == 0): e.matmul(
                    pd[:], wd[:, i, :], hshv[:, i, sl], start=s0, stop=(i == NCH - 1)),
                    deps=[td, hold] + rd)
            t1 = fw.op('scalar', lambda e, o=accv[:, mc, sl]: e.activation(o, pd[:], AF.Identity), deps=[last])
            pp.done(idx, [t1])
            accw.append(t1)
        reld[s] = fw.tok('tensor')
    fw.barrier()
    sb.pop()


def moe_experts(C, L, seq, E, guD, dnD, acc, accv, xnbv):
    fw, pp, sb, dr = C.fw, C.pp, C.sb, C.dr
    sb.push()
    gus = [sb.alloc(8 * 512, BF16) for _ in range(2)]
    dns = [sb.alloc(4 * 512, BF16) for _ in range(2)]
    hbuf = sb.alloc(4 * 1024, BF16)
    hv = hbuf.rearrange("p (c t) -> p c t", c=4)
    cbf = sb.alloc(2048)
    crow = sb.alloc(512)
    crow_rel = []
    tmps = sb.alloc(512)
    tmp2 = sb.alloc(512)
    wrel = [[], []]
    t0_ = fw.tok('scalar')
    acc_tok = {(mc, tt): t0_ for mc in range(NCH) for tt in range(NTT)}
    cbf_rel = []
    h_rel = []
    prev_silu = None
    prev_tmp2 = None
    prev_t3 = None
    for e_ in range(E):
        s = e_ % 2
        gv = gus[s].rearrange("p (c n) -> p c n", c=8)
        dv = dns[s].rearrange("p (c n) -> p c n", c=4)
        tw1 = fw.dma('gpsimd', gv, guD[e_].rearrange("(c p) n -> p c n", p=128), deps=wrel[s], ch=s)
        tw2 = fw.dma('gpsimd', dv, dnD[e_].rearrange("(c p) n -> p c n", p=128), deps=wrel[s], ch=2 + s)
        cbts = []
        for tt in range(NTT):
            sl = slice(tt * 512, (tt + 1) * 512)
            tl = fw.dma('sync', crow[0:1, :], dr['coefh'][e_, sl], deps=crow_rel)
            bi, bp, tb = bcast_row(C, crow[0:1, :], 512, cbf_rel + [tl])
            crow_rel = [tb]
            t = fw.op('scalar', lambda e, o=cbf[:, sl], p=bp: e.activation(o, p[:], AF.Identity),
                      deps=[tb] + cbf_rel)
            pp.done(bi, [t])
            cbts.append(t)
        cbf_rel = []
        htoks = {}
        for icg in range(4):
            for tt in range(NTT):
                sl = slice(tt * 512, (tt + 1) * 512)
                ig, pg, rg = pp.get()
                iu, pu, ru = pp.get()
                lastg = lastu = None
                for mc in range(NCH):
                    lastg = fw.op('tensor', lambda e, m=mc, i=icg, pg=pg, sl=sl, s0=(mc == 0): e.matmul(
                        pg[:], gv[:, m, i * 128:(i + 1) * 128], xnbv[:, m, sl],
                        start=s0, stop=(m == NCH - 1)), deps=[tw1] + rg)
                for mc in range(NCH):
                    lastu = fw.op('tensor', lambda e, m=mc, i=icg, pu=pu, sl=sl, s0=(mc == 0): e.matmul(
                        pu[:], gv[:, m, 512 + i * 128:512 + (i + 1) * 128], xnbv[:, m, sl],
                        start=s0, stop=(m == NCH - 1)), deps=ru)
                t1 = fw.op('scalar', lambda e, pg=pg: e.activation(tmps, pg[:], AF.Silu),
                           deps=[lastg, prev_tmp2])
                t2 = fw.op('vector', lambda e, pu=pu: e.tensor_tensor(tmp2, tmps, pu[:], ALU.mult),
                           deps=[t1, lastu, prev_t3])
                t3 = fw.op('vector', lambda e, o=hv[:, icg, sl], c=cbf[:, sl]: e.tensor_tensor(o, tmp2, c, ALU.mult),
                           deps=[t2, cbts[tt]] + h_rel)
                pp.done(ig, [t1])
                pp.done(iu, [t2])
                prev_t3 = t3
                prev_tmp2 = t2
                cbf_rel.append(t3)
                htoks[(icg, tt)] = t3
        for tt in range(NTT):
            sl = slice(tt * 512, (tt + 1) * 512)
            hts = [htoks[(i, tt)] for i in range(4)]
            for mc in range(NCH):
                idx2, pd, rd = pp.get()
                last = None
                for ic in range(4):
                    last = fw.op('tensor', lambda e, i=ic, m=mc, pd=pd, sl=sl, s0=(ic == 0): e.matmul(
                        pd[:], dv[:, i, m * 128:(m + 1) * 128], hv[:, i, sl],
                        start=s0, stop=(i == 3)), deps=[tw2] + rd + hts)
                t4 = fw.op('vector', lambda e, o=accv[:, mc, sl], pd=pd: e.tensor_tensor(o, o, pd[:], ALU.add),
                           deps=[last, acc_tok[(mc, tt)]])
                pp.done(idx2, [t4])
                acc_tok[(mc, tt)] = t4
        h_rel = [fw.tok('tensor')]
        wrel[s] = [fw.tok('tensor')]
    fw.barrier()
    sb.pop()


def l0_attn_phase(C, seq):
    fw, pp, sb, dr = C.fw, C.pp, C.sb, C.dr
    sb.push()
    o_sb = sb.alloc(8 * 1024, BF16)
    ov = o_sb.rearrange("p (h t) -> p h t", h=8)
    sb.push()
    q_fm = sb.alloc(8 * 1024, BF16)
    qfv = q_fm.rearrange("p (h t) -> p h t", h=8)
    k_fm = sb.alloc(8 * 1024, BF16)
    kfv = k_fm.rearrange("p (h t) -> p h t", h=8)
    v_tm = sb.alloc(16 * 512, BF16)
    vtv = v_tm.rearrange("p (c d) -> p c d", c=16)
    cosv = sb.alloc(16 * 64)
    cv = cosv.rearrange("p (c d) -> p c d", c=16)
    sinv = sb.alloc(16 * 64)
    sv = sinv.rearrange("p (c d) -> p c d", c=16)
    tcs = []
    for tc in range(NTC):
        tcs.append(fw.dma('sync', cv[:, tc, :], dr['COS0'][tc * 128:(tc + 1) * 128, :]))
        tcs.append(fw.dma('sync', sv[:, tc, :], dr['SIN0'][tc * 128:(tc + 1) * 128, :]))
    sb.push()
    xns = [sb.alloc(8 * 64, BF16) for _ in range(2)]
    wslot = sb.alloc(8 * 512, BF16)
    wv = wslot.rearrange("p (c n) -> p c n", c=8)
    qtmp = sb.alloc(1024)
    q3 = qtmp.rearrange("p (h d) -> p h d", h=8)
    qsq = sb.alloc(1024)
    qq3 = qsq.rearrange("p (h d) -> p h d", h=8)
    ssc = sb.alloc(16)
    qrt = sb.alloc(1024)
    r3 = qrt.rearrange("p (h d) -> p h d", h=8)
    qb = sb.alloc(256, BF16)

    def qkv_pass(kind, wdram, dstv):
        tw = fw.dma('gpsimd', wv, wdram.rearrange("(c p) n -> p c n", p=128),
                    deps=[fw.tok('tensor')])
        prevx = [None, None]
        prev_cons = None
        for tc in range(NTC):
            s_ = tc % 2
            xa = xns[s_].rearrange("p (c n) -> p c n", c=8)
            tl = fw.dma('gpsimd', xa, dr['xnh'][seq].rearrange("c p t -> p c t")[:, :, tc * 128:(tc + 1) * 128],
                        deps=[prevx[s_]], ch=s_)
            if kind == 'v':
                ip, ps, rel = pp.get()
                ip2, ps2, rel2 = pp.get()
                last = None
                for mc in range(NCH):
                    fw.op('tensor', lambda e, m=mc, s0=(mc == 0): e.matmul(
                        ps[:], xa[:, m, :], wv[:, m, 0:512], start=s0, stop=(m == NCH - 1)),
                        deps=[tl, tw] + rel)
                for mc in range(NCH):
                    last = fw.op('tensor', lambda e, m=mc, s0=(mc == 0): e.matmul(
                        ps2[:], xa[:, m, :], wv[:, m, 512:1024], start=s0, stop=(m == NCH - 1)),
                        deps=rel2)
                prevx[s_] = last
                t1 = fw.op('scalar', lambda e, o=vtv[:, tc, 0:512]: e.activation(o, ps[:], AF.Identity), deps=[last])
                t2 = fw.op('scalar', lambda e, o=vtv[:, tc, 512:1024]: e.activation(o, ps2[:], AF.Identity), deps=[last])
                pp.done(ip, [t1])
                pp.done(ip2, [t2])
                continue
            ip, ps, rel = pp.get()
            ip2, ps2, rel2 = pp.get()
            last = None
            for mc in range(NCH):
                fw.op('tensor', lambda e, m=mc, s0=(mc == 0): e.matmul(
                    ps[:], xa[:, m, :], wv[:, m, 0:512], start=s0, stop=(m == NCH - 1)),
                    deps=[tl, tw] + rel)
            for mc in range(NCH):
                last = fw.op('tensor', lambda e, m=mc, s0=(mc == 0): e.matmul(
                    ps2[:], xa[:, m, :], wv[:, m, 512:1024], start=s0, stop=(m == NCH - 1)),
                    deps=rel2)
            prevx[s_] = last
            t1 = fw.op('scalar', lambda e: e.activation(qtmp[:, 0:512], ps[:], AF.Identity), deps=[last, prev_cons])
            t2 = fw.op('scalar', lambda e: e.activation(qtmp[:, 512:1024], ps2[:], AF.Identity), deps=[last, prev_cons])
            pp.done(ip, [t1])
            pp.done(ip2, [t2])
            t3 = fw.op('scalar', lambda e: e.activation(qsq, qtmp, AF.Square), deps=[t1, t2])
            t4 = fw.op('vector', lambda e: e.reduce_sum(ssc[:, 0:8].rearrange("p (h o) -> p h o", h=8), qq3, AX.X), deps=[t3])
            t4 = fw.op('vector', lambda e: e.tensor_scalar(ssc[:, 0:8], ssc[:, 0:8], 1.0 / 128.0, 1e-6, ALU.mult, ALU.add), deps=[t4])
            t4 = fw.op('scalar', lambda e: e.activation(ssc[:, 8:16], ssc[:, 0:8], AF.Sqrt), deps=[t4])
            t4 = fw.op('vector', lambda e: e.reciprocal(ssc[:, 0:8], ssc[:, 8:16]), deps=[t4])
            tn = []
            for h in range(8):
                tn.append(fw.op('vector', lambda e, h=h: e.tensor_scalar(
                    q3[:, h, :], q3[:, h, :], ssc[:, h:h + 1], None, ALU.mult), deps=[t4]))
            cb = cv[:, tc:tc + 1, :].broadcast_to([128, 8, 64])
            sbt = sv[:, tc:tc + 1, :].broadcast_to([128, 8, 64])
            ua = fw.op('vector', lambda e: e.tensor_tensor(r3[:, :, 0:64], q3[:, :, 0:64], cb, ALU.mult), deps=tn + tcs)
            ub = fw.op('vector', lambda e: e.tensor_tensor(qq3[:, :, 0:64], q3[:, :, 64:128], sbt, ALU.mult), deps=tn + [t3])
            uc = fw.op('vector', lambda e: e.tensor_tensor(r3[:, :, 0:64], r3[:, :, 0:64], qq3[:, :, 0:64], ALU.subtract), deps=[ua, ub])
            ud = fw.op('vector', lambda e: e.tensor_tensor(r3[:, :, 64:128], q3[:, :, 0:64], sbt, ALU.mult), deps=tn)
            ue = fw.op('vector', lambda e: e.tensor_tensor(qq3[:, :, 64:128], q3[:, :, 64:128], cb, ALU.mult), deps=tn)
            uf = fw.op('vector', lambda e: e.tensor_tensor(r3[:, :, 64:128], r3[:, :, 64:128], qq3[:, :, 64:128], ALU.add), deps=[ud, ue])
            dv = qfv if kind == 'q' else kfv
            qb_rel = None
            for hh in range(2):
                g1 = fw.op('vector', lambda e, hh=hh: e.tensor_copy(
                    qb[:, 0:512], qrt[:, hh * 512:(hh + 1) * 512]), deps=[uc, uf, qb_rel])
                for h2 in range(4):
                    it, pst, relt = pp.get()
                    pb16a = pst[:, 0:64].bitcast(BF16)
                    tt_ = fw.op('tensor', lambda e, h2=h2, pb16a=pb16a: e.transpose(
                        pb16a, qb[:, h2 * 128:(h2 + 1) * 128], C.idnb), deps=[g1] + relt)
                    h = hh * 4 + h2
                    ev = fw.op('scalar', lambda e, h=h, tc=tc, pb16a=pb16a: e.activation(
                        dv[:, h, tc * 128:(tc + 1) * 128], pb16a, AF.Identity), deps=[tt_])
                    pp.done(it, [ev])
                    qb_rel = tt_
            prev_cons = fw.tok('vector')

    qkv_pass('q', dr['L0_wq'], qfv)
    qkv_pass('k', dr['L0_wk'], kfv)
    qkv_pass('v', dr['L0_wv'], None)
    fw.barrier()
    sb.pop()
    sb.push()
    pts = [sb.alloc(256, BF16) for _ in range(4)]
    rrow = sb.alloc(512)
    rrec = sb.alloc(512)
    rb = sb.alloc(512)
    otmp = None
    prel = [[], [], [], []]
    pi = 0
    for h in range(8):
        for qt in range(NTT):
            sl = slice(qt * 512, (qt + 1) * 512)
            iR, psR, relR = pp.get(hold=True)
            iO, psO, relO = pp.get(hold=True)
            nkc = qt * 4 + 4
            lastR = lastO = None
            pend = None
            for kc in range(nkc):
                iS, psS, relS = pp.get()
                tm = fw.op('tensor', lambda e, kc=kc, h=h, psS=psS: e.matmul(
                    psS[:], kfv[:, h, kc * 128:(kc + 1) * 128], qfv[:, h, sl],
                    start=True, stop=True), deps=relS)
                mts = [tm]
                for m in range(4):
                    qc = qt * 4 + m
                    if kc == qc:
                        mts.append(fw.op('vector', lambda e, m=m, psS=psS: e.tensor_tensor(
                            psS[:, m * 128:(m + 1) * 128], psS[:, m * 128:(m + 1) * 128], C.trig, ALU.add), deps=[tm]))
                    elif kc > qc:
                        mts.append(fw.op('vector', lambda e, m=m, psS=psS: e.tensor_scalar(
                            psS[:, m * 128:(m + 1) * 128], psS[:, m * 128:(m + 1) * 128], -1e30, None, ALU.add), deps=[tm]))
                pt = pts[pi % 4]
                texp = fw.op('scalar', lambda e, pt=pt, psS=psS: e.activation(pt, psS[:], AF.Exp), deps=mts + prel[pi % 4])
                pp.done(iS, [texp])
                if pend is not None:
                    (ptx, texpx, kcx, ipx) = pend
                    lastR = fw.op('tensor', lambda e, pt=ptx, s0=(kcx == 0): e.matmul(
                        psR[0:1, :], C.onescb, pt, start=s0, stop=False), deps=[texpx] + (relR if kcx == 0 else []))
                    lastO = fw.op('tensor', lambda e, pt=ptx, kc=kcx, h=h, s0=(kcx == 0): e.matmul(
                        psO[:], vtv[:, kc, h * 128:(h + 1) * 128], pt, start=s0, stop=False), deps=[texpx] + (relO if kcx == 0 else []))
                    prel[ipx] = [lastR, lastO]
                pend = (pt, texp, kc, pi % 4)
                pi += 1
            (ptx, texpx, kcx, ipx) = pend
            lastR = fw.op('tensor', lambda e, pt=ptx, s0=(kcx == 0): e.matmul(
                psR[0:1, :], C.onescb, pt, start=s0, stop=True), deps=[texpx] + (relR if kcx == 0 else []))
            lastO = fw.op('tensor', lambda e, pt=ptx, kc=kcx, h=h, s0=(kcx == 0): e.matmul(
                psO[:], vtv[:, kc, h * 128:(h + 1) * 128], pt, start=s0, stop=True), deps=[texpx] + (relO if kcx == 0 else []))
            prel[ipx] = [lastR, lastO]
            t1 = fw.op('vector', lambda e: e.tensor_copy(rrow[0:1, :], psR[0:1, :]), deps=[lastR])
            pp.done(iR, [t1])
            t2 = fw.op('vector', lambda e: e.reciprocal(rrec[0:1, :], rrow[0:1, :]), deps=[t1])
            bi, bp, tb = bcast_row(C, rrec[0:1, :], 512, [t2])
            t3 = fw.op('scalar', lambda e: e.activation(rb, bp[:], AF.Identity), deps=[tb, otmp])
            pp.done(bi, [t3])
            t4 = fw.op('vector', lambda e, h=h: e.tensor_tensor(ov[:, h, sl], psO[:], rb, ALU.mult), deps=[t3, lastO])
            pp.done(iO, [t4])
            otmp = t4
    fw.barrier()
    sb.pop()
    sb.pop()
    sb.push()
    wslot = sb.alloc(8 * 512, BF16)
    wv2 = wslot.rearrange("p (c n) -> p c n", c=8)
    tw = fw.dma('gpsimd', wv2, dr['L0_wo'].rearrange("(c p) n -> p c n", p=128))
    tmpo = sb.alloc(512)
    dta = sb.alloc(512)
    pv1 = pv2 = None
    for mc in range(NCH):
        for tt in range(NTT):
            sl = slice(tt * 512, (tt + 1) * 512)
            ip, ps, rel = pp.get()
            last = None
            for h in range(8):
                last = fw.op('tensor', lambda e, h=h, m=mc, s0=(h == 0): e.matmul(
                    ps[:], wv2[:, h, m * 128:(m + 1) * 128], ov[:, h, sl],
                    start=s0, stop=(h == NCH - 1)), deps=[tw] + rel)
            t1 = fw.op('scalar', lambda e: e.activation(tmpo, ps[:], AF.Identity), deps=[last, pv1])
            pp.done(ip, [t1])
            t2 = fw.dma('sync', dta, dr['dlt'][seq, mc, :, sl], deps=[pv2], ch=1)
            t3 = fw.op('vector', lambda e: e.tensor_tensor(dta, dta, tmpo, ALU.add), deps=[t1, t2])
            pv1 = t3
            pv2 = fw.dma('sync', dr['dlt'][seq, mc, :, sl], dta, deps=[t3], ch=2)
    fw.barrier()
    sb.pop()
    sb.pop()


def build_program(weights, consts, upto=4, skip=()):
    nc = bass.Bass("TRN2", target_bir_lowering=False)
    C = Ctx()
    C.upto = upto
    C.nc = nc
    C.dr = mk_dram(nc, weights, consts)
    with ExitStack() as stk:
        E = stk.enter_context
        ar = E(nc.sbuf_tensor("arena", [128, 48800], F32))
        psums = [E(nc.psum_tensor(f"ps{i}", [128, 512], F32)) for i in range(8)]
        sems = {e: E(nc.semaphore(f"sem_{e}")) for e in ALLSEM}
        blk = E(nc.Block())
        C.fw = FW(nc)
        C.fw.sems = sems
        C.sb = SB(ar, 48800)
        C.pp = PP([p[:] for p in psums])
        emit_statics(C)
        for L in range(upto):
            if L == 0:
                for seq in range(NSEQ):
                    pre_ln(C, seq)
            if L == 2 and (2, 'attn') not in skip:
                l2_filter_phase(C)
            for seq in range(NSEQ):
                hav_f = (L, 'ffn') not in skip
                hav_a = (L, 'attn') not in skip
                if hav_f:
                    if L in (0, 1):
                        moe_phase(C, L, seq)
                    elif L == 3:
                        l3_ffn_phase(C, seq)
                    else:
                        l2_ffn_phase(C, seq)
                elif hav_a:
                    zero_dlt(C, seq)
                if hav_a:
                    if L == 0:
                        l0_attn_phase(C, seq)
                    elif L == 1:
                        l1_attn_phase(C, seq)
                    elif L == 2:
                        l2_attn_phase(C, seq)
                    else:
                        l3_attn_phase(C, seq)
            for seq in range(NSEQ):
                out_phase(C, L, seq, have_delta=((L, 'ffn') not in skip or (L, 'attn') not in skip))
        ft = C.fw.all_toks()
        C.fw.finalize(blk, final_deps=ft)
    return nc


def zero_dlt(C, seq):
    fw, sb, dr = C.fw, C.sb, C.dr
    sb.push()
    z = sb.alloc(2048)
    t = fw.memset('vector', z, 0.0)
    for mc in range(NCH):
        fw.dma('sync', dr['dlt'][seq, mc], z, deps=[t])
    fw.barrier()
    sb.pop()


def pre_ln(C, seq):
    fw, sb, dr = C.fw, C.sb, C.dr
    sb.push()
    xbuf = sb.alloc(8 * 2048)
    xv = xbuf.rearrange("p (c t) -> p c t", c=8)
    tiles, ttoks = {}, {}
    for tt in range(NTT):
        sl = slice(tt * 512, (tt + 1) * 512)
        for mc in range(NCH):
            t1 = fw.dma('sync', xv[:, mc, sl], dr['XIN'][seq, mc, :, sl])
            tiles[(mc, tt)] = xv[:, mc, sl]
            ttoks[(mc, tt)] = t1
    mub, rstdb, stoks = ln_norm_tiles(C, tiles, ttoks, [])
    tmp = sb.alloc(512)
    ptmp = None
    for tt in range(NTT):
        sl = slice(tt * 512, (tt + 1) * 512)
        for mc in range(NCH):
            t1 = fw.op('vector', lambda e, a=tiles[(mc, tt)], m=mub[:, sl]: e.tensor_tensor(tmp, a, m, ALU.subtract),
                       deps=stoks + [ttoks[(mc, tt)], ptmp])
            t2 = fw.op('vector', lambda e, r=rstdb[:, sl]: e.tensor_tensor(tmp, tmp, r, ALU.mult), deps=[t1])
            ptmp = fw.dma('sync', dr['xnh'][seq, mc, :, sl], tmp, deps=[t2], ch=2)
    fw.barrier()
    sb.pop()


def l1_attn_phase(C, seq):
    fw, pp, sb, dr = C.fw, C.pp, C.sb, C.dr
    sb.push()
    o_sb = sb.alloc(8 * 2048)
    osv = o_sb.rearrange("p (c t) -> p c t", c=8)
    for hg in range(2):
        sb.push()
        qf = sb.alloc(4 * 1024, BF16)
        qfv = qf.rearrange("p (c t) -> p c t", c=4)
        qnf = sb.alloc(4 * 1024, BF16)
        qnv = qnf.rearrange("p (c t) -> p c t", c=4)
        kf = sb.alloc(4 * 1024, BF16)
        kfv = kf.rearrange("p (c t) -> p c t", c=4)
        ksf = sb.alloc(4 * 1024, BF16)
        ksv = ksf.rearrange("p (c t) -> p c t", c=4)
        vtm = sb.alloc(16 * 256, BF16)
        vtv = vtm.rearrange("p (c d) -> p c d", c=16)
        xns = [sb.alloc(8 * 64, BF16) for _ in range(2)]
        wslot = sb.alloc(8 * 256, BF16)
        wv = wslot.rearrange("p (c n) -> p c n", c=8)
        qtmp = sb.alloc(512)
        qstmp = sb.alloc(512)
        qb = sb.alloc(256, BF16)
        qsb = sb.alloc(256, BF16)

        def qkv_pass(kind, wd):
            tw = fw.dma('gpsimd', wv, wd[:, hg * 512:(hg + 1) * 512].rearrange("(c p) n -> p c n", p=128),
                        deps=[fw.tok('tensor')])
            prevx = [None, None]
            prev_cons = None
            for tc in range(NTC):
                s_ = tc % 2
                xa = xns[s_].rearrange("p (c n) -> p c n", c=8)
                tl = fw.dma('gpsimd', xa, dr['xnh'][seq].rearrange("c p t -> p c t")[:, :, tc * 128:(tc + 1) * 128],
                            deps=[prevx[s_]], ch=s_)
                ip, ps, rel = pp.get()
                last = None
                for mc in range(NCH):
                    last = fw.op('tensor', lambda e, m=mc, s0=(mc == 0): e.matmul(
                        ps[:], xa[:, m, :], wv[:, m, :], start=s0, stop=(m == NCH - 1)),
                        deps=[tl, tw] + rel)
                prevx[s_] = last
                if kind == 'v':
                    t1 = fw.op('scalar', lambda e, tc=tc: e.activation(vtv[:, tc, :], ps[:], AF.Identity), deps=[last])
                    pp.done(ip, [t1])
                    continue
                t1 = fw.op('scalar', lambda e: e.activation(qtmp, ps[:], AF.Identity), deps=[last, prev_cons])
                pp.done(ip, [t1])
                sgn = (-1.0 / 32.0) if kind == 'q' else 1.0
                t2 = fw.op('vector', lambda e, tc=tc, sgn=sgn: e.tensor_scalar(
                    qstmp, qtmp, C.posn[:, tc:tc + 1], sgn, ALU.mult, ALU.mult), deps=[t1])
                if kind == 'q':
                    t3 = fw.op('vector', lambda e: e.tensor_scalar(qb, qtmp, 1.0 / 32.0, None, ALU.mult), deps=[t1])
                else:
                    t3 = fw.op('vector', lambda e: e.tensor_copy(qb, qtmp), deps=[t1])
                t4 = fw.op('vector', lambda e: e.tensor_copy(qsb, qstmp), deps=[t2])
                dv1 = qfv if kind == 'q' else kfv
                dv2 = qnv if kind == 'q' else ksv
                qb_rel = None
                for h2 in range(4):
                    it, pst, relt = pp.get()
                    pb16a = pst[:, 0:64].bitcast(BF16)
                    tt_ = fw.op('tensor', lambda e, h2=h2, pb16a=pb16a: e.transpose(
                        pb16a, qb[:, h2 * 128:(h2 + 1) * 128], C.idnb), deps=[t3] + relt)
                    ev = fw.op('scalar', lambda e, h2=h2, tc=tc, pb16a=pb16a: e.activation(
                        dv1[:, h2, tc * 128:(tc + 1) * 128], pb16a, AF.Identity), deps=[tt_])
                    pp.done(it, [ev])
                    it2, pst2, relt2 = pp.get()
                    pb16b = pst2[:, 0:64].bitcast(BF16)
                    tt2 = fw.op('tensor', lambda e, h2=h2, pb16b=pb16b: e.transpose(
                        pb16b, qsb[:, h2 * 128:(h2 + 1) * 128], C.idnb), deps=[t4] + relt2)
                    ev2 = fw.op('scalar', lambda e, h2=h2, tc=tc, pb16b=pb16b: e.activation(
                        dv2[:, h2, tc * 128:(tc + 1) * 128], pb16b, AF.Identity), deps=[tt2])
                    pp.done(it2, [ev2])
                    qb_rel = tt2
                prev_cons = fw.tok('tensor')
        qkv_pass('q', dr['L1_WqT'])
        qkv_pass('k', dr['L1_WkT'])
        qkv_pass('v', dr['L1_WvT'])
        fw.barrier()
        pts = [sb.alloc(256, BF16) for _ in range(4)]
        rrow = sb.alloc(512)
        rrec = sb.alloc(512)
        rbv = sb.alloc(512)
        otmp = None
        prel = [[], [], [], []]
        pi = 0
        for h in range(8):
            hp, ho = h // 2, (h % 2) * 64
            gh = hg * 8 + h
            for qt in range(NTT):
                sl = slice(qt * 512, (qt + 1) * 512)
                iR, psR, relR = pp.get(hold=True)
                iO, psO, relO = pp.get(hold=True)
                lastR = lastO = None
                pend = None
                for kc in range(NTC):
                    iS, psS, relS = pp.get()
                    fw.op('tensor', lambda e, kc=kc, psS=psS: e.matmul(
                        psS[:], ksv[ho:ho + 64, hp, kc * 128:(kc + 1) * 128],
                        qfv[ho:ho + 64, hp, sl], start=True, stop=False), deps=relS)
                    tm = fw.op('tensor', lambda e, kc=kc, psS=psS: e.matmul(
                        psS[:], kfv[ho:ho + 64, hp, kc * 128:(kc + 1) * 128],
                        qnv[ho:ho + 64, hp, sl], start=False, stop=True), deps=[])
                    kcm = kc - qt * 4
                    mts = [tm]
                    pt = pts[pi % 4]
                    if kcm < 0:
                        texp = fw.op('scalar', lambda e, pt=pt: e.activation(pt, psS[:], AF.Exp, scale=C.negc[:, 0:1]),
                                     deps=mts + prel[pi % 4])
                    elif kcm > 3:
                        texp = fw.op('scalar', lambda e, pt=pt: e.activation(pt, psS[:], AF.Exp),
                                     deps=mts + prel[pi % 4])
                    else:
                        tsg = fw.op('vector', lambda e, m=kcm: e.tensor_tensor(
                            psS[:, m * 128:(m + 1) * 128], psS[:, m * 128:(m + 1) * 128], C.sgn, ALU.mult), deps=[tm])
                        mts.append(tsg)
                        texp = None
                        for (a, b, neg) in [(0, kcm + 1, False), (kcm + 1, 4, True)]:
                            if a >= b:
                                continue
                            if neg:
                                texp = fw.op('scalar', lambda e, pt=pt, a=a, b=b: e.activation(
                                    pt[:, a * 128:b * 128], psS[:, a * 128:b * 128], AF.Exp, scale=C.negc[:, 0:1]),
                                    deps=mts + prel[pi % 4])
                            else:
                                texp = fw.op('scalar', lambda e, pt=pt, a=a, b=b: e.activation(
                                    pt[:, a * 128:b * 128], psS[:, a * 128:b * 128], AF.Exp),
                                    deps=mts + prel[pi % 4])
                    pp.done(iS, [texp])
                    if pend is not None:
                        (ptx, texpx, kcx, ipx) = pend
                        lastR = fw.op('tensor', lambda e, pt=ptx, s0=(kcx == 0): e.matmul(
                            psR[0:1, :], C.onescb, pt, start=s0, stop=False),
                            deps=[texpx] + (relR if kcx == 0 else []))
                        lastO = fw.op('tensor', lambda e, pt=ptx, kc=kcx, h=h, s0=(kcx == 0): e.matmul(
                            psO[0:64, :], vtv[:, kc, h * 64:(h + 1) * 64], pt, start=s0, stop=False),
                            deps=[texpx] + (relO if kcx == 0 else []))
                        prel[ipx] = [lastR, lastO]
                    pend = (pt, texp, kc, pi % 4)
                    pi += 1
                (ptx, texpx, kcx, ipx) = pend
                lastR = fw.op('tensor', lambda e, pt=ptx, s0=(kcx == 0): e.matmul(
                    psR[0:1, :], C.onescb, pt, start=s0, stop=True),
                    deps=[texpx] + (relR if kcx == 0 else []))
                lastO = fw.op('tensor', lambda e, pt=ptx, kc=kcx, h=h, s0=(kcx == 0): e.matmul(
                    psO[0:64, :], vtv[:, kc, h * 64:(h + 1) * 64], pt, start=s0, stop=True),
                    deps=[texpx] + (relO if kcx == 0 else []))
                prel[ipx] = [lastR, lastO]
                t1 = fw.op('vector', lambda e: e.tensor_copy(rrow[0:1, :], psR[0:1, :]), deps=[lastR])
                pp.done(iR, [t1])
                t2 = fw.op('vector', lambda e: e.reciprocal(rrec[0:1, :], rrow[0:1, :]), deps=[t1])
                bi, bp, tb = bcast_row(C, rrec[0:1, :], 512, [t2])
                t3 = fw.op('scalar', lambda e: e.activation(rbv[0:64, :], bp[0:64, :], AF.Identity), deps=[tb, otmp])
                pp.done(bi, [t3])
                t4 = fw.op('vector', lambda e, gh=gh: e.tensor_tensor(
                    osv[ho:ho + 64, gh // 2, sl], psO[0:64, :], rbv[0:64, :], ALU.mult), deps=[t3, lastO])
                pp.done(iO, [t4])
                otmp = t4
        fw.barrier()
        sb.pop()
    sb.push()
    vt = fw.tok('vector')
    tiles = {(mc, tt): osv[:, mc, tt * 512:(tt + 1) * 512] for mc in range(NCH) for tt in range(NTT)}
    ttoks = {k: vt for k in tiles}
    mub, rstdb, stoks = ln_norm_tiles(C, tiles, ttoks, [])
    gcol = sb.alloc(8)
    bcol = sb.alloc(8)
    tgb = None
    for mc in range(NCH):
        fw.dma('sync', gcol[:, mc:mc + 1], dr['L1_lng'][mc])
        tgb = fw.dma('sync', bcol[:, mc:mc + 1], dr['L1_lnb'][mc])
    tmp = sb.alloc(512)
    tmpo = sb.alloc(512)
    dta = sb.alloc(512)
    pv1 = pv2 = pv0 = None
    for mc in range(NCH):
        for tt in range(NTT):
            sl = slice(tt * 512, (tt + 1) * 512)
            t1 = fw.op('vector', lambda e, a=tiles[(mc, tt)], m=mub[:, sl]: e.tensor_tensor(tmp, a, m, ALU.subtract),
                       deps=stoks + [pv0])
            t2 = fw.op('vector', lambda e, r=rstdb[:, sl]: e.tensor_tensor(tmp, tmp, r, ALU.mult), deps=[t1])
            t3 = fw.op('scalar', lambda e, m=mc: e.activation(tmpo, tmp, AF.Identity, scale=gcol[:, m:m + 1], bias=bcol[:, m:m + 1]),
                       deps=[t2, tgb, pv1])
            t4 = fw.dma('sync', dta, dr['dlt'][seq, mc, :, sl], deps=[pv2], ch=1)
            t5 = fw.op('vector', lambda e: e.tensor_tensor(dta, dta, tmpo, ALU.add), deps=[t3, t4])
            pv0 = t3
            pv1 = t5
            pv2 = fw.dma('sync', dr['dlt'][seq, mc, :, sl], dta, deps=[t5], ch=2)
    fw.barrier()
    sb.pop()
    sb.pop()


def l2_attn_phase(C, seq):
    fw, pp, sb, dr = C.fw, C.pp, C.sb, C.dr
    sb.push()
    vtm = sb.alloc(16 * 512, BF16)
    vtv = vtm.rearrange("p (c d) -> p c d", c=16)
    Pre = sb.alloc(17 * 512, BF16)
    Prv = Pre.rearrange("p (c d) -> p c d", c=17)
    Pim = sb.alloc(17 * 512, BF16)
    Piv = Pim.rearrange("p (c d) -> p c d", c=17)
    sb.push()
    xnb = sb.alloc(8 * 1024, BF16)
    xnbv = xnb.rearrange("p (c t) -> p c t", c=8)
    ldx = [fw.dma('gpsimd', xnbv[:, mc, :], dr['xnh'][seq, mc], ch=2) for mc in range(NCH)]
    wslot = [sb.alloc(8 * 64, BF16) for _ in range(2)]
    upc = sb.alloc(1026, BF16)
    dwt = [sb.alloc(64 * 6, BF16) for _ in range(2)]
    wcol = sb.alloc(4)
    x1h = sb.alloc(2048)
    vtmp = sb.alloc(2048)
    ucres = sb.alloc(2048)
    bic = sb.alloc(1)
    sfc_ = sb.alloc(1)
    wrel = [[], []]
    drel = [[], []]
    pup = None
    pdi = None
    puc = {}

    def upconv(c, dst, ddeps):
        nonlocal pup, pdi
        s_ = c % 2
        wv = wslot[s_].rearrange("p (c n) -> p c n", c=8)
        tw = fw.dma('gpsimd', wv, dr['L2_w_in'][:, c * 128:(c + 1) * 128].rearrange("(c p) n -> p c n", p=128),
                    deps=wrel[s_], ch=s_)
        tbi = fw.dma('sync', bic, dr['L2_b_in'][c], deps=[pup], ch=3)
        tsf = fw.dma('sync', sfc_, dr['L2_sf_b'][c], deps=[pup], ch=3)
        dv6 = dwt[s_].rearrange("p (j n) -> p j n", j=6)
        dts = []
        pdil = drel[s_]
        for j in range(3):
            twc = fw.dma('sync', wcol[:, j:j + 1], dr['L2_sfw_cols'][j, c], deps=pdil + [pdi], ch=3)
            t1 = fw.op('vector', lambda e, j=j: e.tensor_copy(wcol[:, 3:4].bitcast(BF16)[:, 0:1], wcol[:, j:j + 1]), deps=[twc])
            t2 = fw.op('vector', lambda e, j=j: e.tensor_copy(wcol[:, 3:4], wcol[:, 3:4].bitcast(BF16)[:, 0:1]), deps=[t1])
            t3 = fw.op('vector', lambda e, j=j: e.tensor_scalar(dv6[:, 2 * j, :], C.idn, wcol[:, 3:4], None, ALU.mult), deps=[t2] + pdil)
            t4 = fw.op('vector', lambda e, j=j: e.tensor_tensor(wcol[:, 3:4], wcol[:, j:j + 1], wcol[:, 3:4], ALU.subtract), deps=[t3])
            t5 = fw.op('vector', lambda e, j=j: e.tensor_scalar(dv6[:, 2 * j + 1, :], C.idn, wcol[:, 3:4], None, ALU.mult), deps=[t4] + pdil)
            dts.append(t5)
            pdi = t5
        tz = fw.memset('vector', upc[:, 0:2], 0.0, deps=[pup])
        evs = []
        for tt in range(NTT):
            sl = slice(tt * 512, (tt + 1) * 512)
            ip, ps, rel = pp.get()
            last = None
            for mc in range(NCH):
                last = fw.op('tensor', lambda e, m=mc, sl=sl, ps=ps, wv=wv, s0=(mc == 0): e.matmul(
                    ps[:], wv[:, m, :], xnbv[:, m, sl], start=s0, stop=(m == NCH - 1)),
                    deps=[tw] + rel + ldx)
            ev = fw.op('scalar', lambda e, tt=tt, ps=ps: e.activation(
                upc[:, 2 + tt * 512:2 + (tt + 1) * 512], ps[:], AF.Identity, bias=bic[:, 0:1]),
                deps=[last, tbi, pup, tz])
            pp.done(ip, [ev])
            evs.append(ev)
        cts = []
        for tt in range(NTT):
            ip, ps, rel = pp.get()
            last = None
            for j in range(3):
                for k in range(2):
                    last = fw.op('tensor', lambda e, j=j, k=k, tt=tt, ps=ps, dv6=dv6: e.matmul(
                        ps[:], dv6[:, 2 * j + k, :],
                        upc[:, tt * 512 + j:tt * 512 + j + 512],
                        start=(j == 0 and k == 0), stop=(j == 2 and k == 1)),
                        deps=dts + evs + rel)
            ev = fw.op('scalar', lambda e, tt=tt, ps=ps, dst=dst: e.activation(
                dst[:, tt * 512:(tt + 1) * 512], ps[:], AF.Identity, bias=sfc_[:, 0:1]),
                deps=[last, tsf] + list(ddeps))
            pp.done(ip, [ev])
            cts.append(ev)
        pup = cts[-1]
        wrel[s_] = [fw.tok('tensor')]
        drel[s_] = [fw.tok('tensor')]
        return cts

    pvw = None
    ptr = None
    ptm = None
    for i in range(8):
        t_x0 = upconv(i, ucres, [pvw])
        pvw = fw.dma('sync', dr['x0h'][i], ucres, deps=t_x0, ch=2)
        t_x1 = upconv(8 + i, x1h, [ptm])
        t_v = upconv(16 + i, vtmp, [ptr, ptm])
        tm = fw.op('vector', lambda e: e.tensor_tensor(vtmp, vtmp, x1h, ALU.mult), deps=t_x1 + t_v)
        ptm = tm
        tvw = fw.dma('sync', dr['vh'][i], vtmp, deps=[tm], ch=2)
        for tc in range(NTC):
            it, pst, relt = pp.get()
            tt_ = fw.op('tensor', lambda e, tc=tc, pst=pst: e.transpose(
                pst[:, 0:128], vtmp[:, tc * 128:(tc + 1) * 128], C.idn), deps=[tm] + relt)
            ev = fw.op('scalar', lambda e, tc=tc, i=i, pst=pst: e.activation(
                vtv[:, tc, i * 128:(i + 1) * 128], pst[:, 0:128], AF.Identity), deps=[tt_, ptr])
            pp.done(it, [ev])
            ptr = ev
        pvw = tvw
    fw.barrier()
    sb.pop()
    sb.push()
    crs = [sb.alloc(64, BF16) for _ in range(2)]
    cis = [sb.alloc(64, BF16) for _ in range(2)]
    vre = sb.alloc(1024)
    vim = sb.alloc(1024)
    kre = sb.alloc(1024)
    kim = sb.alloc(1024)
    ptmp = sb.alloc(1024)
    pcr = [None, None]
    pev = None
    pkl = None
    for fc in range(17):
        ire0, pre0, r0 = pp.get(hold=True)
        ire1, pre1, r1 = pp.get(hold=True)
        iim0, pim0, r2 = pp.get(hold=True)
        iim1, pim1, r3 = pp.get(hold=True)
        lasts = []
        for tc in range(NTC):
            s_ = tc % 2
            tcr = fw.dma('gpsimd', crs[s_], dr['C4R'][tc * 128:(tc + 1) * 128, fc * 128:(fc + 1) * 128],
                         deps=[pcr[s_]], ch=s_)
            tci = fw.dma('gpsimd', cis[s_], dr['C4I'][tc * 128:(tc + 1) * 128, fc * 128:(fc + 1) * 128],
                         deps=[pcr[s_]], ch=s_)
            lasts = []
            lst = None
            for (pp_, cc, hh, rr) in ((pre0, crs[s_], 0, r0), (pre1, crs[s_], 1, r1),
                                      (pim0, cis[s_], 0, r2), (pim1, cis[s_], 1, r3)):
                lst = fw.op('tensor', lambda e, p=pp_, c=cc, hh=hh, tc=tc: e.matmul(
                    p[:], c, vtv[:, tc, hh * 512:(hh + 1) * 512], start=(tc == 0), stop=(tc == NTC - 1)),
                    deps=[tcr, tci] + (list(rr) if tc == 0 else []))
                lasts.append(lst)
            pcr[s_] = lst
        e1 = fw.op('scalar', lambda e: e.activation(vre[:, 0:512], pre0[:], AF.Identity), deps=[lasts[0], pev])
        e2 = fw.op('scalar', lambda e: e.activation(vre[:, 512:1024], pre1[:], AF.Identity), deps=[lasts[1], pev])
        e3 = fw.op('scalar', lambda e: e.activation(vim[:, 0:512], pim0[:], AF.Identity), deps=[lasts[2], pev])
        e4 = fw.op('scalar', lambda e: e.activation(vim[:, 512:1024], pim1[:], AF.Identity), deps=[lasts[3], pev])
        pp.done(ire0, [e1])
        pp.done(ire1, [e2])
        pp.done(iim0, [e3])
        pp.done(iim1, [e4])
        tk1 = fw.dma('sync', kre, dr['KRE'][fc], deps=[pkl], ch=1)
        tk2 = fw.dma('sync', kim, dr['KIM'][fc], deps=[pkl], ch=1)
        t1 = fw.op('vector', lambda e: e.tensor_tensor(ptmp, vre, kre, ALU.mult), deps=[e1, e2, tk1])
        t2 = fw.op('vector', lambda e, fc=fc: e.tensor_tensor(Prv[:, fc, :], vim, kim, ALU.mult), deps=[e3, e4, tk2])
        t3 = fw.op('vector', lambda e, fc=fc: e.tensor_tensor(ptmp, ptmp, Prv[:, fc, :], ALU.subtract), deps=[t1, t2])
        t3b = fw.op('vector', lambda e, fc=fc: e.tensor_copy(Prv[:, fc, :], ptmp), deps=[t3])
        t4 = fw.op('vector', lambda e: e.tensor_tensor(ptmp, vre, kim, ALU.mult), deps=[t3b, tk2])
        t5 = fw.op('vector', lambda e, fc=fc: e.tensor_tensor(Piv[:, fc, :], vim, kre, ALU.mult), deps=[t4, tk1])
        t6 = fw.op('vector', lambda e, fc=fc: e.tensor_tensor(ptmp, ptmp, Piv[:, fc, :], ALU.add), deps=[t5])
        t6b = fw.op('vector', lambda e, fc=fc: e.tensor_copy(Piv[:, fc, :], ptmp), deps=[t6])
        pkl = t6b
        pev = t6b
    fw.barrier()
    sb.pop()
    sb.push()
    obf = sb.alloc(8 * 1024, BF16)
    obv = obf.rearrange("p (c t) -> p c t", c=8)
    sb.push()
    abs_ = sb.alloc(17 * 256, BF16)
    abv = abs_.rearrange("p (c n) -> p c n", c=17)
    bbs = sb.alloc(17 * 256, BF16)
    bbv = bbs.rearrange("p (c n) -> p c n", c=17)
    vt_ = sb.alloc(512)
    x0t = sb.alloc(512)
    fbc = sb.alloc(8)
    ytmp = sb.alloc(512)
    tfb = None
    for mc in range(NCH):
        tfb = fw.dma('sync', fbc[:, mc:mc + 1], dr['L2_filt_bias'][mc], ch=3)
    pab = None
    pvl = None
    pgl = None
    for it in range(NTT):
        tas = []
        for fc in range(17):
            tas.append(fw.dma('gpsimd', abv[:, fc, :], dr['AMAT'][fc * 128:(fc + 1) * 128, it * 512:(it + 1) * 512],
                              deps=[pab], ch=0))
            tas.append(fw.dma('gpsimd', bbv[:, fc, :], dr['BMAT'][fc * 128:(fc + 1) * 128, it * 512:(it + 1) * 512],
                              deps=[pab], ch=1))
        for ch in range(NCH):
            ip, ps, rel = pp.get()
            last = None
            for fc in range(17):
                last = fw.op('tensor', lambda e, fc=fc, ch=ch, ps=ps, s0=(fc == 0): e.matmul(
                    ps[:], Prv[:, fc, ch * 128:(ch + 1) * 128], abv[:, fc, :], start=s0, stop=False),
                    deps=tas + rel)
                last = fw.op('tensor', lambda e, fc=fc, ch=ch, ps=ps, st=(fc == 16): e.matmul(
                    ps[:], Piv[:, fc, ch * 128:(ch + 1) * 128], bbv[:, fc, :], start=False, stop=st),
                    deps=[])
            tvl = fw.dma('sync', vt_, dr['vh'][ch, :, it * 512:(it + 1) * 512], deps=[pvl], ch=1)
            txl = fw.dma('sync', x0t, dr['x0h'][ch, :, it * 512:(it + 1) * 512], deps=[pvl], ch=2)
            t1 = fw.op('vector', lambda e, ch=ch: e.tensor_scalar(ytmp, vt_, fbc[:, ch:ch + 1], None, ALU.mult),
                       deps=[tvl, tfb, pgl])
            t2 = fw.op('vector', lambda e, ps=ps: e.tensor_tensor(ytmp, ytmp, ps[:], ALU.add), deps=[t1, last])
            t3 = fw.op('vector', lambda e, ch=ch, it=it: e.tensor_tensor(
                obv[:, ch, it * 512:(it + 1) * 512], ytmp, x0t, ALU.mult), deps=[t2, txl])
            pp.done(ip, [t2])
            pvl = t3
            pgl = t3
        pab = fw.tok('tensor')
    fw.barrier()
    sb.pop()
    wslot2 = sb.alloc(8 * 512, BF16)
    wv2 = wslot2.rearrange("p (c n) -> p c n", c=8)
    tw = fw.dma('gpsimd', wv2, dr['L2_w_out'].rearrange("(c p) n -> p c n", p=128), ch=0)
    boc = sb.alloc(8)
    tbo = None
    for mc in range(NCH):
        tbo = fw.dma('sync', boc[:, mc:mc + 1], dr['L2_b_out'][mc], ch=3)
    tmpo = sb.alloc(512)
    dta = sb.alloc(512)
    pv1 = pv2 = None
    for mc in range(NCH):
        for tt in range(NTT):
            sl = slice(tt * 512, (tt + 1) * 512)
            ip, ps, rel = pp.get()
            last = None
            for c in range(NCH):
                last = fw.op('tensor', lambda e, c=c, m=mc, sl=sl, ps=ps, s0=(c == 0): e.matmul(
                    ps[:], wv2[:, c, m * 128:(m + 1) * 128], obv[:, c, sl],
                    start=s0, stop=(c == NCH - 1)), deps=[tw] + rel)
            t1 = fw.op('scalar', lambda e, m=mc, ps=ps: e.activation(tmpo, ps[:], AF.Identity, bias=boc[:, m:m + 1]),
                       deps=[last, tbo, pv1])
            pp.done(ip, [t1])
            t2 = fw.dma('sync', dta, dr['dlt'][seq, mc, :, sl], deps=[pv2], ch=1)
            t3 = fw.op('vector', lambda e: e.tensor_tensor(dta, dta, tmpo, ALU.add), deps=[t1, t2])
            pv1 = t3
            pv2 = fw.dma('sync', dr['dlt'][seq, mc, :, sl], dta, deps=[t3], ch=2)
    fw.barrier()
    sb.pop()
    sb.pop()



def l2_filter_phase(C):
    fw, pp, sb, dr = C.fw, C.pp, C.sb, C.dr
    sb.push()
    zaug = sb.alloc(4096)
    tz = fw.dma('sync', zaug[0:4, :], dr['ZAUG'])
    h1 = sb.alloc(4096)
    h2 = sb.alloc(4096)
    tmpa = sb.alloc(512)
    tmpb = sb.alloc(512)
    wa = sb.alloc(64)
    fq = sb.alloc(1)
    t1_ = fw.memset('vector', h1[64:65, :], 1.0)
    t2_ = fw.memset('vector', h2[64:65, :], 1.0)
    PI = math.pi

    def mlp_layer(wname, bname, fname, K, hsrc, hdst, deps):
        nw = 3 if K == 4 else 64
        tw = fw.dma('sync', wa[0:nw, 0:64], dr[wname], deps=[fw.tok('tensor'), fw.tok('scalar')])
        tb = fw.dma('sync', wa[nw:nw + 1, 0:64], dr[bname], deps=[])
        tf = fw.dma('sync', fq[0:64, :], dr[fname], deps=[fw.tok('vector')])
        toks = []
        for nt in range(8):
            sl = slice(nt * 512, (nt + 1) * 512)
            ip, ps, rel = pp.get()
            tm = fw.op('tensor', lambda e, sl=sl, K=K: e.matmul(
                ps[0:64, :], wa[0:K, 0:64], hsrc[0:K, sl], start=True, stop=True),
                deps=[tw, tb] + rel + list(deps))
            tv = fw.op('vector', lambda e: e.tensor_scalar(tmpa[0:64, :], ps[0:64, :], fq[0:64, 0:1], None, ALU.mult),
                       deps=[tm, tf, fw.tok('scalar')])
            pp.done(ip, [tv])
            for _ in range(3):
                tv = fw.op('vector', lambda e: e.tensor_scalar(tmpb[0:64, :], tmpa[0:64, :], PI, None, ALU.is_gt), deps=[tv])
                tv = fw.op('vector', lambda e: e.scalar_tensor_tensor(tmpa[0:64, :], tmpb[0:64, :], -2 * PI, tmpa[0:64, :], ALU.mult, ALU.add), deps=[tv])
                tv = fw.op('vector', lambda e: e.tensor_scalar(tmpb[0:64, :], tmpa[0:64, :], -PI, None, ALU.is_lt), deps=[tv])
                tv = fw.op('vector', lambda e: e.scalar_tensor_tensor(tmpa[0:64, :], tmpb[0:64, :], 2 * PI, tmpa[0:64, :], ALU.mult, ALU.add), deps=[tv])
            ts_ = fw.op('scalar', lambda e, sl=sl: e.activation(hdst[0:64, sl], tmpa[0:64, :], AF.Sin), deps=[tv])
            toks.append(ts_)
        return toks

    tk = mlp_layer('L2_f_w1', 'L2_f_b1', 'L2_freq1', 4, zaug, h1, [tz, t1_])
    tk = mlp_layer('L2_f_w2', 'L2_f_b2', 'L2_freq2', 65, h1, h2, tk + [t2_])
    tk = mlp_layer('L2_f_w3', 'L2_f_b3', 'L2_freq3', 65, h2, h1, tk + [t1_])
    fw.barrier()
    ktm = sb.alloc(32 * 512, BF16)
    kv = ktm.rearrange("p (c n) -> p c n", c=32)
    fwout = sb.alloc(1024)
    tfo = fw.dma('sync', fwout[0:64, :], dr['L2_f_wout'])
    dec = [sb.alloc(512) for _ in range(2)]
    pdec = [None, None]
    for tc in range(32):
        for chh in range(2):
            s_ = chh
            td = fw.dma('sync', dec[s_], dr['DECAY'][tc * 128:(tc + 1) * 128, chh * 512:(chh + 1) * 512],
                        deps=[pdec[s_]], ch=1)
            ip, ps, rel = pp.get()
            tm = fw.op('tensor', lambda e, tc=tc, chh=chh: e.matmul(
                ps[:], h1[0:64, tc * 128:(tc + 1) * 128], fwout[0:64, chh * 512:(chh + 1) * 512],
                start=True, stop=True), deps=[tfo] + rel)
            tv = fw.op('vector', lambda e, tc=tc, chh=chh, s_=s_: e.tensor_tensor(
                kv[:, tc, chh * 512:(chh + 1) * 512], ps[:], dec[s_], ALU.mult), deps=[tm, td])
            pp.done(ip, [tv])
            pdec[s_] = tv
    crs = [sb.alloc(64, BF16) for _ in range(2)]
    cis = [sb.alloc(64, BF16) for _ in range(2)]
    kreb = sb.alloc(1024)
    kimb = sb.alloc(1024)
    pcr = [None, None]
    pev = None
    ktv = fw.tok('vector')
    for fc in range(17):
        ire0, pre0, r0 = pp.get(hold=True)
        ire1, pre1, r1 = pp.get(hold=True)
        iim0, pim0, r2 = pp.get(hold=True)
        iim1, pim1, r3 = pp.get(hold=True)
        lasts = []
        for tc in range(32):
            s_ = tc % 2
            tcr = fw.dma('gpsimd', crs[s_], dr['C4R'][tc * 128:(tc + 1) * 128, fc * 128:(fc + 1) * 128],
                         deps=[pcr[s_]], ch=s_)
            tci = fw.dma('gpsimd', cis[s_], dr['C4I'][tc * 128:(tc + 1) * 128, fc * 128:(fc + 1) * 128],
                         deps=[pcr[s_]], ch=s_)
            lst = None
            lasts = []
            for (pp_, cc, hh, rr) in ((pre0, crs[s_], 0, r0), (pre1, crs[s_], 1, r1),
                                      (pim0, cis[s_], 0, r2), (pim1, cis[s_], 1, r3)):
                lst = fw.op('tensor', lambda e, p=pp_, c=cc, hh=hh, tc=tc: e.matmul(
                    p[:], c, kv[:, tc, hh * 512:(hh + 1) * 512], start=(tc == 0), stop=(tc == 31)),
                    deps=[tcr, tci, ktv] + (list(rr) if tc == 0 else []))
                lasts.append(lst)
            pcr[s_] = lst
        e1 = fw.op('scalar', lambda e: e.activation(kreb[:, 0:512], pre0[:], AF.Identity), deps=[lasts[0], pev])
        e2 = fw.op('scalar', lambda e: e.activation(kreb[:, 512:1024], pre1[:], AF.Identity), deps=[lasts[1]])
        e3 = fw.op('scalar', lambda e: e.activation(kimb[:, 0:512], pim0[:], AF.Identity), deps=[lasts[2]])
        e4 = fw.op('scalar', lambda e: e.activation(kimb[:, 512:1024], pim1[:], AF.Identity), deps=[lasts[3]])
        pp.done(ire0, [e1])
        pp.done(ire1, [e2])
        pp.done(iim0, [e3])
        pp.done(iim1, [e4])
        s1 = fw.dma('sync', dr['KRE'][fc], kreb, deps=[e1, e2], ch=2)
        s2 = fw.dma('sync', dr['KIM'][fc], kimb, deps=[e3, e4], ch=2)
        pev = s2
    fw.barrier()
    sb.pop()


def l2_ffn_phase(C, seq):
    fw, pp, sb, dr = C.fw, C.pp, C.sb, C.dr
    sb.push()
    xi = sb.alloc(8 * 2052)
    xiv = xi.rearrange("p (c t) -> p c t", c=8)
    tz = fw.memset('vector', xiv[:, :, 0:1], 0.0)
    tz2 = fw.memset('vector', xiv[:, :, 2049:2052], 0.0)
    ldx = [fw.dma('sync', xiv[:, mc, 1:2049], dr['xnh'][seq, mc], deps=[tz, tz2]) for mc in range(NCH)]
    fw.barrier()
    for li in range(2):
        sb.push()
        xb = sb.alloc(8 * 1026, BF16)
        xbv = xb.rearrange("p (c t) -> p c t", c=8)
        cts = []
        pcv = None
        for mc in range(NCH):
            pcv = fw.op('vector', lambda e, mc=mc: e.tensor_copy(xbv[:, mc, :], xiv[:, mc, 0:2052]), deps=[pcv])
            cts.append(pcv)
        qfm = sb.alloc(4 * 1024, BF16)
        qfv = qfm.rearrange("p (c t) -> p c t", c=4)
        kfm = sb.alloc(4 * 1024, BF16)
        kfv = kfm.rearrange("p (c t) -> p c t", c=4)
        vtm = sb.alloc(16 * 256, BF16)
        vtv = vtm.rearrange("p (c d) -> p c d", c=16)
        osb = sb.alloc(4 * 1024, BF16)
        ov = osb.rearrange("p (c t) -> p c t", c=4)
        wslot = sb.alloc(8 * 256, BF16)
        wv = wslot.rearrange("p (c n) -> p c n", c=8)
        xpos = [sb.alloc(64) for _ in range(4)]
        qsb = sb.alloc(64)
        ksb = sb.alloc(64)
        qrow = sb.alloc(64)
        qtmp = sb.alloc(512)
        qsq = sb.alloc(512)
        qrt = sb.alloc(512)
        qb = sb.alloc(256, BF16)
        sscol = sb.alloc(16)
        t = fw.dma('sync', qrow[0:1, 0:64], dr['L2f_qs'][li], ch=3)
        bi, bp, tb = bcast_row(C, qrow[0:1, 0:64], 64, [t])
        tqs = fw.op('scalar', lambda e: e.activation(qsb, bp[:, 0:64], AF.Identity), deps=[tb])
        pp.done(bi, [tqs])
        t2 = fw.dma('sync', qrow[0:1, 0:64], dr['L2f_ks'][li], deps=[tb], ch=3)
        bi, bp, tb = bcast_row(C, qrow[0:1, 0:64], 64, [t2])
        tks = fw.op('scalar', lambda e: e.activation(ksb, bp[:, 0:64], AF.Identity), deps=[tb])
        pp.done(bi, [tks])

        def shslice(mc, a, b):
            off = 1 if mc < 4 else 0
            return xbv[:, mc, off + a: off + b]

        def qkv_pass(kind, c0):
            tw = fw.dma('gpsimd', wv, dr['L2f_wqkv'][li][:, c0:c0 + 512].rearrange("(c p) n -> p c n", p=128),
                        deps=[fw.tok('tensor')])
            prev_cons = None
            for tc in range(NTC):
                ip, ps, rel = pp.get()
                last = None
                for mc in range(NCH):
                    last = fw.op('tensor', lambda e, m=mc, tc=tc, s0=(mc == 0): e.matmul(
                        ps[:], shslice(m, tc * 128, (tc + 1) * 128), wv[:, m, :],
                        start=s0, stop=(m == NCH - 1)), deps=[tw] + rel + cts)
                if kind == 'v':
                    t1 = fw.op('scalar', lambda e, tc=tc: e.activation(vtv[:, tc, :], ps[:], AF.Identity), deps=[last])
                    pp.done(ip, [t1])
                    continue
                s_ = tc % 2
                ct, st_ = xpos[2 * s_], xpos[2 * s_ + 1]
                cn = 'XCQ' if kind == 'q' else 'XCK'
                sn = 'XSQ' if kind == 'q' else 'XSK'
                tct = fw.dma('sync', ct, dr[cn][tc * 128:(tc + 1) * 128, :], deps=[prev_cons], ch=1)
                tst = fw.dma('sync', st_, dr[sn][tc * 128:(tc + 1) * 128, :], deps=[prev_cons], ch=1)
                t1 = fw.op('scalar', lambda e: e.activation(qtmp, ps[:], AF.Identity), deps=[last, prev_cons])
                pp.done(ip, [t1])
                t3 = fw.op('scalar', lambda e: e.activation(qsq, qtmp, AF.Square), deps=[t1, prev_cons])
                q3 = qtmp.rearrange("p (h d) -> p h d", h=8)
                t4 = fw.op('vector', lambda e: e.reduce_sum(sscol[:, 0:8].rearrange("p (h o) -> p h o", h=8),
                                                            qsq.rearrange("p (h d) -> p h d", h=8), AX.X), deps=[t3])
                t4 = fw.op('scalar', lambda e: e.activation(sscol[:, 8:16], sscol[:, 0:8], AF.Sqrt), deps=[t4])
                t4 = fw.op('vector', lambda e: e.tensor_scalar(sscol[:, 8:16], sscol[:, 8:16], 1e-12, None, ALU.max), deps=[t4])
                t4 = fw.op('vector', lambda e: e.reciprocal(sscol[:, 0:8], sscol[:, 8:16]), deps=[t4])
                tn = []
                for h in range(8):
                    tn.append(fw.op('vector', lambda e, h=h: e.tensor_scalar(
                        q3[:, h, :], q3[:, h, :], sscol[:, h:h + 1], None, ALU.mult), deps=[t4]))
                scb = (qsb if kind == 'q' else ksb).rearrange("p (o d) -> p o d", o=1).broadcast_to([128, 8, 64])
                tqn = fw.op('vector', lambda e, scb=scb: e.tensor_tensor(q3, q3, scb, ALU.mult), deps=tn + [tqs, tks])
                cb = ct.rearrange("p (o d) -> p o d", o=1).broadcast_to([128, 8, 64])
                sb_ = st_.rearrange("p (o d) -> p o d", o=1).broadcast_to([128, 8, 64])
                r3 = qrt.rearrange("p (h d) -> p h d", h=8)
                s3 = qsq.rearrange("p (h d) -> p h d", h=8)
                u1 = fw.op('vector', lambda e, cb=cb: e.tensor_tensor(r3, q3, cb, ALU.mult), deps=[tqn, tct])
                u2 = fw.op('vector', lambda e, sb_=sb_: e.tensor_tensor(s3[:, :, 0:32], q3[:, :, 32:64], sb_[:, :, 0:32], ALU.mult), deps=[tqn, tst, t4])
                u3 = fw.op('vector', lambda e: e.tensor_tensor(r3[:, :, 0:32], r3[:, :, 0:32], s3[:, :, 0:32], ALU.subtract), deps=[u1, u2])
                u4 = fw.op('vector', lambda e, sb_=sb_: e.tensor_tensor(s3[:, :, 32:64], q3[:, :, 0:32], sb_[:, :, 32:64], ALU.mult), deps=[u1, u2])
                u5 = fw.op('vector', lambda e: e.tensor_tensor(r3[:, :, 32:64], r3[:, :, 32:64], s3[:, :, 32:64], ALU.add), deps=[u3, u4])
                dv = qfv if kind == 'q' else kfv
                qb_rel = None
                t6 = fw.op('vector', lambda e: e.tensor_copy(qb, qrt), deps=[u3, u5])
                for h2 in range(4):
                    it, pst, relt = pp.get()
                    pb16 = pst[:, 0:64].bitcast(BF16)
                    tt_ = fw.op('tensor', lambda e, h2=h2, pb16=pb16: e.transpose(
                        pb16, qb[:, h2 * 128:(h2 + 1) * 128], C.idnb), deps=[t6] + relt)
                    ev = fw.op('scalar', lambda e, h2=h2, tc=tc, pb16=pb16, dv=dv: e.activation(
                        dv[:, h2, tc * 128:(tc + 1) * 128], pb16, AF.Identity), deps=[tt_])
                    pp.done(it, [ev])
                    qb_rel = tt_
                prev_cons = fw.tok('vector')
        qkv_pass('q', 0)
        qkv_pass('k', 512)
        qkv_pass('v', 1024)
        fw.barrier()
        pts = [sb.alloc(256, BF16) for _ in range(4)]
        rrow = qtmp
        rrec = qsq
        rbv = qrt
        otmp = None
        prel = [[], [], [], []]
        pi = 0
        for h in range(8):
            hp, ho = h // 2, (h % 2) * 64
            for qw in range(NTT):
                sl = slice(qw * 512, (qw + 1) * 512)
                iR, psR, relR = pp.get(hold=True)
                iO, psO, relO = pp.get(hold=True)
                kcs = [kc for kc in range(qw * 4 - 4, qw * 4 + 4) if kc >= 0]
                lastR = lastO = None
                pend = None
                for ik, kc in enumerate(kcs):
                    iS, psS, relS = pp.get()
                    tm = fw.op('tensor', lambda e, kc=kc, sl=sl, hp=hp, ho=ho, psS=psS: e.matmul(
                        psS[:], kfv[ho:ho + 64, hp, kc * 128:(kc + 1) * 128],
                        qfv[ho:ho + 64, hp, sl], start=True, stop=True), deps=relS)
                    mts = [tm]
                    for m in range(4):
                        qc = qw * 4 + m
                        if kc == qc:
                            mts.append(fw.op('vector', lambda e, m=m, psS=psS: e.tensor_tensor(
                                psS[:, m * 128:(m + 1) * 128], psS[:, m * 128:(m + 1) * 128], C.trig, ALU.add), deps=[tm]))
                        elif kc == qc - 4:
                            mts.append(fw.op('vector', lambda e, m=m, psS=psS: e.tensor_tensor(
                                psS[:, m * 128:(m + 1) * 128], psS[:, m * 128:(m + 1) * 128], C.tril, ALU.add), deps=[tm]))
                        elif kc > qc or kc < qc - 4:
                            mts.append(fw.op('vector', lambda e, m=m, psS=psS: e.tensor_scalar(
                                psS[:, m * 128:(m + 1) * 128], psS[:, m * 128:(m + 1) * 128], -1e30, None, ALU.add), deps=[tm]))
                    pt = pts[pi % 4]
                    texp = fw.op('scalar', lambda e, pt=pt, psS=psS: e.activation(pt, psS[:], AF.Exp), deps=mts + prel[pi % 4])
                    pp.done(iS, [texp])
                    if pend is not None:
                        (ptx, texpx, kcx, ikx, ipx) = pend
                        lastR = fw.op('tensor', lambda e, pt=ptx, psR=psR, s0=(ikx == 0): e.matmul(
                            psR[0:1, :], C.onescb, pt, start=s0, stop=False),
                            deps=[texpx] + (relR if ikx == 0 else []))
                        lastO = fw.op('tensor', lambda e, pt=ptx, kc=kcx, h=h, psO=psO, s0=(ikx == 0): e.matmul(
                            psO[0:64, :], vtv[:, kc, h * 64:(h + 1) * 64], pt, start=s0, stop=False),
                            deps=[texpx] + (relO if ikx == 0 else []))
                        prel[ipx] = [lastR, lastO]
                    pend = (pt, texp, kc, ik, pi % 4)
                    pi += 1
                (ptx, texpx, kcx, ikx, ipx) = pend
                lastR = fw.op('tensor', lambda e, pt=ptx, psR=psR, s0=(ikx == 0): e.matmul(
                    psR[0:1, :], C.onescb, pt, start=s0, stop=True),
                    deps=[texpx] + (relR if ikx == 0 else []))
                lastO = fw.op('tensor', lambda e, pt=ptx, kc=kcx, h=h, psO=psO, s0=(ikx == 0): e.matmul(
                    psO[0:64, :], vtv[:, kc, h * 64:(h + 1) * 64], pt, start=s0, stop=True),
                    deps=[texpx] + (relO if ikx == 0 else []))
                prel[ipx] = [lastR, lastO]
                t1 = fw.op('vector', lambda e, psR=psR: e.tensor_copy(rrow[0:1, :], psR[0:1, :]), deps=[lastR])
                pp.done(iR, [t1])
                t2 = fw.op('vector', lambda e: e.reciprocal(rrec[0:1, :], rrow[0:1, :]), deps=[t1])
                bi, bp, tb = bcast_row(C, rrec[0:1, :], 512, [t2])
                t3 = fw.op('scalar', lambda e, bp=bp: e.activation(rbv[0:64, :], bp[0:64, :], AF.Identity), deps=[tb, otmp])
                pp.done(bi, [t3])
                t4 = fw.op('vector', lambda e, h=h, sl=sl, hp=hp, ho=ho, psO=psO: e.tensor_tensor(
                    ov[ho:ho + 64, hp, sl], psO[0:64, :], rbv[0:64, :], ALU.mult), deps=[t3, lastO])
                pp.done(iO, [t4])
                otmp = t4
        fw.barrier()
        wv4 = wslot.rearrange("p (c n) -> p c n", c=4)
        tw = fw.dma('gpsimd', wv4, dr['L2f_wo'][li].rearrange("(c p) n -> p c n", p=128), deps=[fw.tok('tensor')])
        tmpo = qtmp
        pv1 = None
        for mc in range(NCH):
            for tt in range(NTT):
                sl = slice(tt * 512, (tt + 1) * 512)
                ip, ps, rel = pp.get()
                last = None
                for c in range(4):
                    last = fw.op('tensor', lambda e, c=c, m=mc, sl=sl, ps=ps, s0=(c == 0): e.matmul(
                        ps[:], wv4[:, c, m * 128:(m + 1) * 128], ov[:, c, sl],
                        start=s0, stop=(c == 3)), deps=[tw] + rel)
                t1 = fw.op('scalar', lambda e, ps=ps: e.activation(tmpo, ps[:], AF.Identity), deps=[last, pv1])
                pp.done(ip, [t1])
                t2 = fw.op('vector', lambda e, mc=mc, tt=tt: e.tensor_tensor(
                    xiv[:, mc, 1 + tt * 512:1 + (tt + 1) * 512], xiv[:, mc, 1 + tt * 512:1 + (tt + 1) * 512],
                    tmpo, ALU.add), deps=[t1])
                pv1 = t2
        fw.barrier()
        sb.pop()
        sb.push()
        xnin = sb.alloc(8 * 1024, BF16)
        xnv = xnin.rearrange("p (c t) -> p c t", c=8)
        vt0 = fw.tok('vector')

        def shx(mc, tt):
            off = 1 if mc < 4 else 0
            return xiv[:, mc, off + tt * 512: off + (tt + 1) * 512]
        tiles = {(mc, tt): shx(mc, tt) for mc in range(NCH) for tt in range(NTT)}
        ttoks = {k: vt0 for k in tiles}
        sb.push()
        mub, rstdb, stoks = ln_norm_tiles(C, tiles, ttoks, [])
        gcol = sb.alloc(8)
        bcol = sb.alloc(8)
        tgb = None
        for mc in range(NCH):
            fw.dma('sync', gcol[:, mc:mc + 1], dr['L2f_lng'][li, mc], ch=3)
            tgb = fw.dma('sync', bcol[:, mc:mc + 1], dr['L2f_lnb'][li, mc], ch=3)
        tmp = sb.alloc(512)
        pvx = None
        for mc in range(NCH):
            for tt in range(NTT):
                sl = slice(tt * 512, (tt + 1) * 512)
                t1 = fw.op('vector', lambda e, a=tiles[(mc, tt)], m=mub[:, sl]: e.tensor_tensor(tmp, a, m, ALU.subtract),
                           deps=stoks + [pvx])
                t2 = fw.op('vector', lambda e, r=rstdb[:, sl]: e.tensor_tensor(tmp, tmp, r, ALU.mult), deps=[t1])
                pvx = fw.op('scalar', lambda e, m=mc, o=xnv[:, mc, sl]: e.activation(
                    o, tmp, AF.Identity, scale=gcol[:, m:m + 1], bias=bcol[:, m:m + 1]), deps=[t2, tgb])
        fw.barrier()
        sb.pop()
        NI = 22
        hsb = sb.alloc(NI * 512, BF16)
        hv = hsb.rearrange("p (c t) -> p c t", c=NI)
        w1a = [sb.alloc(8 * 64, BF16) for _ in range(2)]
        w1g = [sb.alloc(8 * 64, BF16) for _ in range(2)]
        w2s = [sb.alloc(NI * 64, BF16) for _ in range(2)]
        b1ac = sb.alloc(NI)
        b1gc = sb.alloc(NI)
        b2c = sb.alloc(8)
        tmpf = sb.alloc(512)
        tb1 = None
        for i in range(NI):
            fw.dma('sync', b1ac[:, i:i + 1], dr['L2f_b1p'][li, i], ch=3)
            tb1 = fw.dma('sync', b1gc[:, i:i + 1], dr['L2f_b1p'][li, 22 + i], ch=3)
        tb2 = None
        for mc in range(NCH):
            tb2 = fw.dma('sync', b2c[:, mc:mc + 1], dr['L2f_b2'][li, mc], ch=3)
        HP = 2816
        for th in range(2):
            wrel1 = [[], []]
            hrel = []
            prev_ga = None
            prev_h = None
            for i in range(NI):
                s_ = i % 2
                wva = w1a[s_].rearrange("p (c n) -> p c n", c=8)
                wvg = w1g[s_].rearrange("p (c n) -> p c n", c=8)
                ta = fw.dma('gpsimd', wva, dr['L2f_w1p'][li][:, i * 128:(i + 1) * 128].rearrange("(c p) n -> p c n", p=128), deps=wrel1[s_], ch=s_)
                tg = fw.dma('gpsimd', wvg, dr['L2f_w1p'][li][:, HP + i * 128:HP + (i + 1) * 128].rearrange("(c p) n -> p c n", p=128), deps=wrel1[s_], ch=2 + s_)
                for tt2 in range(2):
                    sl = slice(th * 1024 + tt2 * 512, th * 1024 + (tt2 + 1) * 512)
                    ia, pa, ra = pp.get()
                    ig, pg, rg = pp.get()
                    lasta = lastg = None
                    for mc in range(NCH):
                        lasta = fw.op('tensor', lambda e, m=mc, sl=sl, pa=pa, wva=wva, s0=(mc == 0): e.matmul(
                            pa[:], wva[:, m, :], xnv[:, m, sl], start=s0, stop=(m == NCH - 1)),
                            deps=[ta] + ra)
                    for mc in range(NCH):
                        lastg = fw.op('tensor', lambda e, m=mc, sl=sl, pg=pg, wvg=wvg, s0=(mc == 0): e.matmul(
                            pg[:], wvg[:, m, :], xnv[:, m, sl], start=s0, stop=(m == NCH - 1)),
                            deps=[tg] + rg)
                    t1 = fw.op('scalar', lambda e, i=i, pg=pg: e.activation(tmpf, pg[:], AF.Gelu, bias=b1gc[:, i:i + 1]),
                               deps=[lastg, tb1, prev_h])
                    t2 = fw.op('vector', lambda e, i=i, tt2=tt2, pa=pa: e.scalar_tensor_tensor(
                        hv[:, i, tt2 * 512:(tt2 + 1) * 512], pa[:], b1ac[:, i:i + 1], tmpf, ALU.add, ALU.mult),
                        deps=[t1, lasta] + hrel)
                    pp.done(ia, [t2])
                    pp.done(ig, [t1])
                    prev_ga = t2
                    prev_h = t2
                wrel1[s_] = [fw.tok('tensor')]
            hrel = []
            wrel2 = [[], []]
            for mc in range(NCH):
                s_ = mc % 2
                wv2 = w2s[s_].rearrange("p (c n) -> p c n", c=NI)
                t2w = fw.dma('gpsimd', wv2, dr['L2f_w2p'][li][:, mc * 128:(mc + 1) * 128].rearrange("(c p) n -> p c n", p=128), deps=wrel2[s_], ch=s_)
                for tt2 in range(2):
                    sl = slice(th * 1024 + tt2 * 512, th * 1024 + (tt2 + 1) * 512)
                    ip, ps, rel = pp.get()
                    last = None
                    for i in range(NI):
                        last = fw.op('tensor', lambda e, i=i, tt2=tt2, ps=ps, wv2=wv2, s0=(i == 0): e.matmul(
                            ps[:], wv2[:, i, :], hv[:, i, tt2 * 512:(tt2 + 1) * 512],
                            start=s0, stop=(i == NI - 1)), deps=[t2w, prev_ga] + rel)
                    t1 = fw.op('scalar', lambda e, m=mc, ps=ps: e.activation(tmpf, ps[:], AF.Identity, bias=b2c[:, m:m + 1]),
                               deps=[last, tb2, fw.tok('vector')])
                    t2 = fw.op('vector', lambda e, mc=mc, sl2=(1 + th * 1024 + tt2 * 512): e.tensor_tensor(
                        xiv[:, mc, sl2:sl2 + 512], xiv[:, mc, sl2:sl2 + 512], tmpf, ALU.add), deps=[t1])
                    pp.done(ip, [t2])
                wrel2[s_] = [fw.tok('tensor')]
            fw.barrier()
        sb.pop()
    for mc in range(NCH):
        fw.dma('sync', dr['dlt'][seq, mc], xiv[:, mc, 1:2049], deps=[fw.tok('vector')], ch=2)
    fw.barrier()
    sb.pop()






def l3_attn_phase(C, seq):
    fw, pp, sb, dr = C.fw, C.pp, C.sb, C.dr
    NH = 16
    sb.push()
    o_sb = sb.alloc(8 * 1024, BF16)
    ov = o_sb.rearrange("p (c t) -> p c t", c=8)
    q_fm = sb.alloc(8 * 1024, BF16)
    qfv = q_fm.rearrange("p (c t) -> p c t", c=8)
    k_fm = sb.alloc(8 * 1024, BF16)
    kfv = k_fm.rearrange("p (c t) -> p c t", c=8)
    v_tm = sb.alloc(16 * 512, BF16)
    vtv = v_tm.rearrange("p (c d) -> p c d", c=16)
    sb.push()
    binr = sb.alloc(1536, BF16)
    tbin = fw.dma('gpsimd', binr[0:1, :], dr['L3_b_in'])
    xns = [sb.alloc(8 * 64, BF16) for _ in range(2)]
    wslot = sb.alloc(8 * 512, BF16)
    wv = wslot.rearrange("p (c n) -> p c n", c=8)
    qb = sb.alloc(256, BF16)

    def qkv_pass(kind, c0):
        tw = fw.dma('gpsimd', wv, dr['L3_winT'][:, c0:c0 + 1024].rearrange("(c p) n -> p c n", p=128),
                    deps=[fw.tok('tensor')])
        prevx = [None, None]
        prev_cons = None
        for tc in range(NTC):
            s_ = tc % 2
            xa = xns[s_].rearrange("p (c n) -> p c n", c=8)
            tl = fw.dma('gpsimd', xa, dr['xnh'][seq].rearrange("c p t -> p c t")[:, :, tc * 128:(tc + 1) * 128],
                        deps=[prevx[s_]], ch=s_)
            evs = []
            for half in range(2):
                ip, ps, rel = pp.get()
                last = None
                for mc in range(NCH):
                    last = fw.op('tensor', lambda e, m=mc, h=half, s0=(mc == 0): e.matmul(
                        ps[:], xa[:, m, :], wv[:, m, h * 512:(h + 1) * 512], start=s0, stop=False),
                        deps=[tl, tw] + rel)
                last = fw.op('tensor', lambda e, h=half: e.matmul(
                    ps[:], C.onesrb[0:1, :], binr[0:1, c0 + half * 512:c0 + (half + 1) * 512],
                    start=False, stop=True), deps=[tbin])
                prevx[s_] = last
                if kind == 'v':
                    t1 = fw.op('scalar', lambda e, h=half, tc=tc: e.activation(
                        vtv[:, tc, h * 512:(h + 1) * 512], ps[:], AF.Identity), deps=[last])
                    pp.done(ip, [t1])
                    continue
                if kind == 'q':
                    t1 = fw.op('vector', lambda e: e.tensor_scalar(qb, ps[:], 0.125, None, ALU.mult),
                               deps=[last, prev_cons])
                else:
                    t1 = fw.op('vector', lambda e: e.tensor_copy(qb, ps[:]), deps=[last, prev_cons])
                pp.done(ip, [t1])
                dv = qfv if kind == 'q' else kfv
                qb_rel = None
                for h2 in range(4):
                    it, pst, relt = pp.get()
                    pb16a = pst[:, 0:64].bitcast(BF16)
                    tt_ = fw.op('tensor', lambda e, h2=h2, pb16a=pb16a: e.transpose(
                        pb16a, qb[:, h2 * 128:(h2 + 1) * 128], C.idnb), deps=[t1] + relt)
                    c = half * 4 + h2
                    ev = fw.op('scalar', lambda e, c=c, tc=tc, pb16a=pb16a: e.activation(
                        dv[:, c, tc * 128:(tc + 1) * 128], pb16a, AF.Identity), deps=[tt_])
                    pp.done(it, [ev])
                    qb_rel = tt_
                prev_cons = qb_rel
    qkv_pass('q', 0)
    qkv_pass('k', 1024)
    qkv_pass('v', 2048)
    fw.barrier()
    import os
    if os.environ.get('L3DBG'):
        for c in range(NCH):
            fw.dma('gpsimd', dr['OUT'][0, c], qfv[:, c, :])
            fw.dma('gpsimd', dr['OUT'][1, c], kfv[:, c, :])
        fw.barrier()
    sb.pop()
    sb.push()
    pts = [sb.alloc(256, BF16) for _ in range(4)]
    rrow = sb.alloc(512)
    rrec = sb.alloc(512)
    rbv = sb.alloc(512)
    otmp = None
    prel = [[], [], [], []]
    pi = 0
    for h in range(NH):
        hp, ho = h // 2, (h % 2) * 64
        for qt in range(NTT):
            sl = slice(qt * 512, (qt + 1) * 512)
            iR, psR, relR = pp.get(hold=True)
            iO, psO, relO = pp.get(hold=True)
            started = [False] * 4
            lastR = lastO = None
            pieces = []
            for m in range(-1, 4):
                kc = qt * 4 + m
                if kc < 0:
                    continue
                qlo, qhi = max(m, 0), min(m + 1, 3)
                pieces.append((kc, m, qlo, qhi))
            for (kc, m, qlo, qhi) in pieces:
                N = (qhi - qlo + 1) * 128
                iS, psS, relS = pp.get()
                tm = fw.op('tensor', lambda e, kc=kc, qlo=qlo, N=N: e.matmul(
                    psS[:, 0:N], kfv[ho:ho + 64, hp, kc * 128:(kc + 1) * 128],
                    qfv[ho:ho + 64, hp, qt * 512 + qlo * 128:qt * 512 + qlo * 128 + N],
                    start=True, stop=True), deps=relS)
                mts = [tm]
                for qc in range(qlo, qhi + 1):
                    off = (qc - qlo) * 128
                    tri = C.trig if (qt * 4 + qc) == kc else C.tril
                    mts.append(fw.op('vector', lambda e, off=off, tri=tri: e.tensor_tensor(
                        psS[:, off:off + 128], psS[:, off:off + 128], tri, ALU.add), deps=[tm]))
                pt = pts[pi % 4]
                texp = fw.op('scalar', lambda e, pt=pt, N=N: e.activation(pt[:, 0:N], psS[:, 0:N], AF.Exp),
                             deps=mts + prel[pi % 4])
                pp.done(iS, [texp])
                for qc in range(qlo, qhi + 1):
                    off = (qc - qlo) * 128
                    s0 = not started[qc]
                    started[qc] = True
                    stp = (m == qc)
                    lastR = fw.op('tensor', lambda e, pt=pt, off=off, qc=qc, s0=s0, stp=stp: e.matmul(
                        psR[0:1, qc * 128:(qc + 1) * 128], C.onescb, pt[:, off:off + 128],
                        start=s0, stop=stp), deps=[texp] + (relR if s0 else []))
                    lastO = fw.op('tensor', lambda e, pt=pt, off=off, qc=qc, kc=kc, s0=s0, stp=stp: e.matmul(
                        psO[0:64, qc * 128:(qc + 1) * 128], vtv[:, kc, h * 64:(h + 1) * 64],
                        pt[:, off:off + 128], start=s0, stop=stp), deps=[texp] + (relO if s0 else []))
                prel[pi % 4] = [lastR, lastO]
                pi += 1
            t1 = fw.op('vector', lambda e: e.tensor_copy(rrow[0:1, :], psR[0:1, :]), deps=[lastR])
            pp.done(iR, [t1])
            t2 = fw.op('vector', lambda e: e.reciprocal(rrec[0:1, :], rrow[0:1, :]), deps=[t1])
            bi, bp, tb = bcast_row(C, rrec[0:1, :], 512, [t2])
            t3 = fw.op('scalar', lambda e: e.activation(rbv[0:64, :], bp[0:64, :], AF.Identity), deps=[tb, otmp])
            pp.done(bi, [t3])
            t4 = fw.op('vector', lambda e: e.tensor_tensor(
                ov[ho:ho + 64, hp, sl], psO[0:64, :], rbv[0:64, :], ALU.mult), deps=[t3, lastO])
            pp.done(iO, [t4])
            otmp = t4
    fw.barrier()
    sb.pop()
    sb.push()
    wslot = sb.alloc(8 * 512, BF16)
    wv2 = wslot.rearrange("p (c n) -> p c n", c=8)
    tw = fw.dma('gpsimd', wv2, dr['L3_woutT'].rearrange("(c p) n -> p c n", p=128))
    boc = sb.alloc(8)
    tbo = None
    for mc in range(NCH):
        tbo = fw.dma('sync', boc[:, mc:mc + 1], dr['L3_b_out'][mc])
    tmpo = sb.alloc(512)
    dta = sb.alloc(512)
    pv1 = pv2 = None
    for mc in range(NCH):
        for tt in range(NTT):
            sl = slice(tt * 512, (tt + 1) * 512)
            ip, ps, rel = pp.get()
            last = None
            for c in range(NCH):
                last = fw.op('tensor', lambda e, c=c, m=mc, s0=(c == 0): e.matmul(
                    ps[:], wv2[:, c, m * 128:(m + 1) * 128], ov[:, c, sl],
                    start=s0, stop=(c == NCH - 1)), deps=[tw] + rel)
            t1 = fw.op('scalar', lambda e, m=mc: e.activation(tmpo, ps[:], AF.Identity, bias=boc[:, m:m + 1]),
                       deps=[last, tbo, pv1])
            pp.done(ip, [t1])
            t2 = fw.dma('sync', dta, dr['dlt'][seq, mc, :, sl], deps=[pv2], ch=1)
            t3 = fw.op('vector', lambda e: e.tensor_tensor(dta, dta, tmpo, ALU.add), deps=[t1, t2])
            pv1 = t3
            pv2 = fw.dma('sync', dr['dlt'][seq, mc, :, sl], dta, deps=[t3], ch=2)
    fw.barrier()
    sb.pop()
    sb.pop()


def l3_ffn_phase(C, seq):
    fw, pp, sb, dr = C.fw, C.pp, C.sb, C.dr
    sb.push()
    acc = sb.alloc(8 * 2048)
    accv = acc.rearrange("p (c t) -> p c t", c=8)
    xnb = sb.alloc(8 * 1024, BF16)
    xnbv = xnb.rearrange("p (c t) -> p c t", c=8)
    Mfm = sb.alloc(2048)
    ldx = [fw.dma('gpsimd', xnbv[:, mc, :], dr['xnh'][seq, mc]) for mc in range(NCH)]
    tz = fw.memset('vector', acc, 0.0)
    acc_tok = {(mc, tt): tz for mc in range(NCH) for tt in range(NTT)}
    sb.push()
    wgr = sb.alloc(64)
    wgv = wgr.rearrange("p (c e) -> p c e", c=8)
    t_w = fw.dma('sync', wgv, dr['L3_Wg'].rearrange("(c p) e -> p c e", p=128))
    bgrow = sb.alloc(8)
    t_b = fw.dma('sync', bgrow[0:1, :], dr['L3_bg'])
    bgb = sb.alloc(8)
    bi, bp, tb = bcast_row(C, bgrow[0:1, 0:8], 8, [t_b])
    t = fw.op('scalar', lambda e: e.activation(bgb, bp[:, 0:8], AF.Identity), deps=[tb])
    pp.done(bi, [t])
    xf = [sb.alloc(128) for _ in range(2)]
    lrow = sb.alloc(8)
    prob = sb.alloc(8)
    cols = sb.alloc(4)
    mtm = sb.alloc(8)
    prevx = [None, None]
    prev_tr = None
    prev_v = None
    for tc in range(NTC):
        ip, ps, rel = pp.get()
        lastmm = None
        for mc in range(NCH):
            xa = xf[mc % 2]
            tl = fw.dma('sync', xa, dr['xnh'][seq, mc, :, tc * 128:(tc + 1) * 128],
                        deps=[prevx[mc % 2]], ch=mc % 2)
            lastmm = fw.op('tensor', lambda e, a=xa, m=mc, s0=(mc == 0): e.matmul(
                ps[:, 0:8], a, wgv[:, m, :], start=s0, stop=(m == NCH - 1)),
                deps=[tl, t_w] + rel)
            prevx[mc % 2] = lastmm
        t1 = fw.op('vector', lambda e: e.tensor_tensor(lrow, ps[:, 0:8], bgb, ALU.add), deps=[lastmm, t, prev_v])
        pp.done(ip, [t1])
        t2 = fw.op('vector', lambda e: e.reduce_max(cols[:, 0:1], lrow, AX.X), deps=[t1])
        t2 = fw.op('vector', lambda e: e.tensor_scalar(cols[:, 1:2], cols[:, 0:1], -1.0, None, ALU.mult), deps=[t2])
        t3 = fw.op('scalar', lambda e: e.activation(prob, lrow, AF.Exp, bias=cols[:, 1:2]), deps=[t2, prev_v])
        t4 = fw.op('vector', lambda e: e.reduce_sum(cols[:, 2:3], prob, AX.X), deps=[t3])
        t4 = fw.op('vector', lambda e: e.reciprocal(cols[:, 3:4], cols[:, 2:3]), deps=[t4])
        t5 = fw.op('vector', lambda e: e.tensor_scalar(mtm, prob, 1.0, None, ALU.is_ge), deps=[t3])
        t5 = fw.op('vector', lambda e: e.tensor_tensor(mtm, mtm, prob, ALU.mult), deps=[t5])
        t5 = fw.op('vector', lambda e: e.tensor_scalar(mtm, mtm, cols[:, 3:4], None, ALU.mult), deps=[t5, t4])
        prev_v = t5
        it, pst, relt = pp.get()
        tt_ = fw.op('tensor', lambda e: e.transpose(pst[0:8, 0:128], mtm, C.idn), deps=[t5] + relt)
        tev = fw.op('scalar', lambda e, o=Mfm[0:8, tc * 128:(tc + 1) * 128]: e.activation(o, pst[0:8, 0:128], AF.Identity),
                    deps=[tt_, prev_tr])
        pp.done(it, [tev])
        prev_tr = tev
    tch = fw.dma('sync', dr['coefh'][0:8, :], Mfm[0:8, :], deps=[prev_tr])
    fw.barrier()
    sb.pop()
    sb.push()
    w1s = [sb.alloc(8 * 512, BF16)]
    w2s = [sb.alloc(8 * 512, BF16)]
    hm = sb.alloc(8 * 1024, BF16)
    hmv = hm.rearrange("p (c t) -> p c t", c=8)
    mb = sb.alloc(2048)
    b1c = sb.alloc(8)
    b2c = sb.alloc(8)
    crow = sb.alloc(512)
    crow_rel = []
    tmpf = sb.alloc(512)
    wrel = [[], []]
    w2rel = []
    mb_rel = []
    hm_rel = []
    prev_gelu = None
    for e_ in range(8):
        s = 0
        w1v = w1s[0].rearrange("p (c n) -> p c n", c=8)
        w2v = w2s[0].rearrange("p (c n) -> p c n", c=8)
        tw1 = fw.dma('gpsimd', w1v, dr['L3_W1'][e_].rearrange("(c p) n -> p c n", p=128), deps=wrel[s], ch=0)
        tw2 = fw.dma('gpsimd', w2v, dr['L3_W2'][e_].rearrange("(c p) n -> p c n", p=128), deps=w2rel, ch=2)
        tsc = fw.tok('scalar')
        tvv = fw.tok('vector')
        tb1 = tb2 = None
        for ic in range(NCH):
            tb1 = fw.dma('sync', b1c[:, ic:ic + 1], dr['L3_b1'][e_, ic], deps=[tsc])
            tb2 = fw.dma('sync', b2c[:, ic:ic + 1], dr['L3_b2'][e_, ic], deps=[tvv])
        mbts = []
        for tt in range(NTT):
            sl = slice(tt * 512, (tt + 1) * 512)
            tl = fw.dma('sync', crow[0:1, :], dr['coefh'][e_, sl], deps=crow_rel)
            bi, bp, tb = bcast_row(C, crow[0:1, :], 512, mb_rel + [tl])
            crow_rel = [tb]
            t = fw.op('scalar', lambda e, o=mb[:, sl], p=bp: e.activation(o, p[:], AF.Identity), deps=[tb] + mb_rel)
            pp.done(bi, [t])
            mbts.append(t)
        mb_rel = []
        for ic in range(NCH):
            for tt in range(NTT):
                sl = slice(tt * 512, (tt + 1) * 512)
                ip, ps, rel = pp.get()
                last = None
                for mc in range(NCH):
                    last = fw.op('tensor', lambda e, m=mc, i=ic, s0=(mc == 0): e.matmul(
                        ps[:], w1v[:, m, i * 128:(i + 1) * 128], xnbv[:, m, sl],
                        start=s0, stop=(m == NCH - 1)), deps=[tw1] + rel)
                t1 = fw.op('scalar', lambda e, i=ic: e.activation(tmpf, ps[:], AF.Gelu, bias=b1c[:, i:i + 1]),
                           deps=[last, tb1, prev_gelu])
                t2 = fw.op('vector', lambda e, o=hmv[:, ic, sl], c=mb[:, sl]: e.tensor_tensor(o, tmpf, c, ALU.mult),
                           deps=[t1, mbts[tt]] + hm_rel)
                pp.done(ip, [t1])
                prev_gelu = t2
                mb_rel.append(t2)
        hm_rel = []
        for mc in range(NCH):
            for tt in range(NTT):
                sl = slice(tt * 512, (tt + 1) * 512)
                ip, ps, rel = pp.get()
                last = None
                for ic in range(NCH):
                    last = fw.op('tensor', lambda e, i=ic, m=mc, s0=(ic == 0): e.matmul(
                        ps[:], w2v[:, i, m * 128:(m + 1) * 128], hmv[:, i, sl],
                        start=s0, stop=(ic == NCH - 1)), deps=[tw2] + rel + ([prev_gelu] if ic == 0 else []))
                t4 = fw.op('vector', lambda e, o=accv[:, mc, sl]: e.tensor_tensor(o, o, ps[:], ALU.add),
                           deps=[last, acc_tok[(mc, tt)]])
                t4 = fw.op('vector', lambda e, o=accv[:, mc, sl], m=mc: e.scalar_tensor_tensor(
                    o, mb[:, sl], b2c[:, m:m + 1], o, ALU.mult, ALU.add), deps=[t4, tb2, mbts[tt]])
                pp.done(ip, [t4])
                acc_tok[(mc, tt)] = t4
                mb_rel.append(t4)
            hm_rel = [fw.tok('tensor')]
        wrel[s] = [fw.tok('tensor')]
        w2rel = [fw.tok('tensor')]
    for mc in range(NCH):
        fw.dma('sync', dr['dlt'][seq, mc], accv[:, mc, :],
               deps=[acc_tok[(mc, tt)] for tt in range(NTT)])
    fw.barrier()
    sb.pop()
    sb.pop()


def kernel(upto=None, skip=None, **inputs):
    _upto = 4 if upto is None else upto
    _skip = set() if skip is None else set(skip)
    inputs = {k: np.asarray(v, dtype=np.float32) for k, v in inputs.items()}
    weights = prep_weights(inputs)
    consts = host_consts()
    x = inputs['L0_attn_x']
    nc = build_program(weights, consts, upto=_upto, skip=_skip)
    shared = {}
    for k, v in list(weights.items()) + list(consts.items()):
        shared[k] = np.ascontiguousarray(v, dtype=np.float32)
    in_maps = []
    for c in range(8):
        m = dict(shared)
        xs = x[2 * c:2 * c + 2]
        xs = np.ascontiguousarray(xs.transpose(0, 2, 1)).reshape(NSEQ, NCH, 128, T)
        m['XIN'] = xs
        in_maps.append(m)
    res = run_bass_kernel_spmd(nc, in_maps, core_ids=list(range(8)))
    outs = []
    for c in range(8):
        o = res.results[c]['OUT'].reshape(NSEQ, D, T).transpose(0, 2, 1)
        outs.append(o)
    return np.ascontiguousarray(np.concatenate(outs, axis=0), dtype=np.float32)


# version 04 (final)
# speedup vs baseline: 1.2630x; 1.0045x over previous
import math
import types
import numpy as np
from contextlib import ExitStack

import concourse.bass as bass
import concourse.mybir as mybir
from concourse.bass_utils import run_bass_kernel_spmd

F32 = mybir.dt.float32
BF16 = mybir.dt.bfloat16
AF = mybir.ActivationFunctionType
ALU = mybir.AluOpType
AX = mybir.AxisListType

ENGS = ('tensor', 'vector', 'scalar', 'gpsimd', 'sync')
QCH = [f'{q}{i}' for q in ('sync', 'gpsimd') for i in range(4)]
ALLSEM = list(ENGS) + QCH
NSEQ = 2
T = 2048
D = 1024
NCH = 8
NTC = 16
NTT = 4


class FW:
    def __init__(self, nc):
        self.nc = nc
        self.streams = {e: [] for e in ENGS}
        self.cnt = {e: 0 for e in ALLSEM}
        self.sems = {}
        self.lastwait = {}

    def wait(self, eng, deps):
        for d in deps:
            if d is None:
                continue
            se, v = d
            if v <= 0:
                continue
            assert v <= self.cnt[se], f"dep on future {se}#{v} (have {self.cnt[se]})"
            if se not in ENGS:
                v = self.cnt[se]
            k = (eng, se)
            if self.lastwait.get(k, 0) >= v:
                continue
            self.lastwait[k] = v
            sem = self.sems[se]
            self.streams[eng].append(lambda e, sem=sem, v=v: e.wait_ge(sem, v))

    @staticmethod
    def _freeze(f):
        if not f.__closure__:
            return f
        cells = tuple(types.CellType(c.cell_contents) for c in f.__closure__)
        return types.FunctionType(f.__code__, f.__globals__, f.__name__,
                                  f.__defaults__, cells)

    def op(self, eng, f, deps=(), inc=1, sname=None):
        f = self._freeze(f)
        self.wait(eng, deps)
        sname = sname or eng
        self.cnt[sname] += inc
        sem = self.sems[sname]

        def closure(e, f=f, sem=sem, inc=inc):
            f(e).then_inc(sem, inc)
        self.streams[eng].append(closure)
        return (sname, self.cnt[sname])

    def memset(self, eng, ap, val, deps=()):
        return self.op(eng, lambda e, a=ap, v=val: e.memset(a, v), deps=deps)

    def dma(self, q, out, in_, deps=(), ch=0):
        sname = f'{q}{ch}'
        deps = list(deps) + [(sname, self.cnt[sname])]
        return self.op(q, lambda e, o=out, i=in_: e.dma_start(out=o, in_=i),
                       deps=deps, inc=16, sname=sname)

    def tok(self, eng):
        return (eng, self.cnt[eng])

    def all_toks(self):
        return [(e, self.cnt[e]) for e in ALLSEM]

    def barrier(self):
        toks = self.all_toks()
        for e in ENGS:
            self.wait(e, toks)

    def finalize(self, block, final_deps=()):
        self.wait('sync', final_deps)
        for e in ENGS:
            stream = self.streams[e]
            getattr(block, e)(lambda eng, stream=stream: [f(eng) for f in stream])


class SB:
    def __init__(self, handle, words):
        self.h = handle
        self.off = 0
        self.words = words
        self.stack = []
        self.hiwater = 0

    def alloc(self, words, dt=F32):
        off = self.off
        self.off += words
        self.hiwater = max(self.hiwater, self.off)
        assert self.off <= self.words, f"arena overflow {self.off} > {self.words}"
        ap = self.h[:, off:off + words]
        return ap if dt == F32 else ap.bitcast(dt)

    def push(self):
        self.stack.append(self.off)

    def pop(self):
        self.off = self.stack.pop()


class PP:
    def __init__(self, tiles):
        self.t = tiles
        self.rel = [[] for _ in tiles]
        self.held = [False] * len(tiles)
        self.i = 0

    def get(self, hold=False):
        for _ in range(len(self.t)):
            i = self.i % len(self.t)
            self.i += 1
            if not self.held[i]:
                self.held[i] = hold
                return i, self.t[i], list(self.rel[i])
        raise RuntimeError("all psum banks held")

    def done(self, i, toks):
        self.rel[i] = [t for t in toks if t is not None]
        self.held[i] = False


def host_consts():
    c = {}
    c['IDN'] = np.eye(128, dtype=np.float32)
    c['ONES'] = np.ones((128, 128), np.float32)
    a = np.arange(128)
    c['TRIG'] = np.where(a[:, None] > a[None, :], -1e30, 0.0).astype(np.float32)
    c['TRIL'] = np.where(a[:, None] < a[None, :], -1e30, 0.0).astype(np.float32)
    c['SGN'] = np.sign(a[:, None] - a[None, :]).astype(np.float32)
    half = 64
    freqs = np.power(10000.0, -np.arange(half, dtype=np.float64) / half)
    rad = np.arange(T, dtype=np.float64)[:, None] * freqs[None, :]
    s = 1.0 / math.sqrt(128.0)
    c['COS0'] = (np.cos(rad) * s).astype(np.float32)
    c['SIN0'] = (np.sin(rad) * s).astype(np.float32)
    DH = 64
    hi = np.arange(0, DH, 2, dtype=np.float64)
    inv_freq = 1.0 / 10000.0 ** (hi / DH)
    tt_ = np.arange(T, dtype=np.float64)
    fr = tt_[:, None] * inv_freq[None, :]
    fr = np.concatenate([fr, fr], axis=-1)
    base = (hi + 0.4 * DH) / (1.4 * DH)
    power = (tt_ - T // 2) / 256.0
    scale = base[None, :] ** power[:, None]
    scale = np.concatenate([scale, scale], axis=-1)
    c['XCQ'] = (np.cos(fr) * scale * 8.0).astype(np.float32)
    c['XSQ'] = (np.sin(fr) * scale * 8.0).astype(np.float32)
    c['XCK'] = (np.cos(fr) / scale).astype(np.float32)
    c['XSK'] = (np.sin(fr) / scale).astype(np.float32)
    c['POSN'] = np.ascontiguousarray((np.arange(T, dtype=np.float64) / T).astype(np.float32).reshape(16, 128).T)
    n = 4096
    NF = 2176
    f = np.arange(NF, dtype=np.float64)
    tfull = np.arange(n, dtype=np.float64)
    ang = 2.0 * np.pi * tfull[:, None] * f[None, :] / n
    cre = np.cos(ang)
    cim = -np.sin(ang)
    cre[:, 2049:] = 0.0
    cim[:, 2049:] = 0.0
    c['C4R'] = cre.astype(np.float32)
    c['C4I'] = cim.astype(np.float32)
    w = np.full(NF, 2.0 / n, np.float64)
    w[0] = 1.0 / n
    w[2048] = 1.0 / n
    w[2049:] = 0.0
    ii = np.arange(T, dtype=np.float64)
    ang2 = 2.0 * np.pi * f[:, None] * ii[None, :] / n
    c['AMAT'] = (w[:, None] * np.cos(ang2)).astype(np.float32)
    c['BMAT'] = (-w[:, None] * np.sin(ang2)).astype(np.float32)
    tlin = np.linspace(0.0, 1.0, n)
    deltas = np.linspace(math.log(0.01) / 1.5, math.log(0.01) / 0.3, D)
    c['DECAY'] = (np.exp(-tlin[:, None] * np.abs(deltas)[None, :]) + 0.05).astype(np.float32)
    t01 = np.linspace(0.0, 1.0, n)
    tr = np.linspace(0.0, n - 1.0, n)
    wang = 2.0 * np.pi * tr / n
    fband = np.array([0.0001])
    fw_ = fband[None, :] * wang[:, None]
    c['ZAUG'] = np.stack([t01, np.cos(fw_)[:, 0], -np.sin(fw_)[:, 0], np.ones(n)], 0).astype(np.float32)
    c['NEGIDX'] = (-np.arange(64, dtype=np.float64) * 1e-8).astype(np.float32).reshape(1, 64)
    c['ONECOL'] = np.ones((128, 1), np.float32)
    return c


def prep_weights(inp):
    w = {}
    cp = np.ascontiguousarray
    w['L0_wq'] = inp['L0_attn_wq'].reshape(1024, 1024)
    w['L0_wk'] = inp['L0_attn_wk'].reshape(1024, 1024)
    w['L0_wv'] = inp['L0_attn_wv'].reshape(1024, 1024)
    w['L0_wo'] = inp['L0_attn_wo'].reshape(1024, 1024)
    w['L0_guT'] = cp(inp['L0_ffn_experts_gate_up'].transpose(0, 2, 1))
    w['L0_dnT'] = cp(inp['L0_ffn_experts_down'].transpose(0, 2, 1))
    w['L0_gwT'] = cp(inp['L0_ffn_gate_weight'].T)
    w['L0_ebias'] = inp['L0_ffn_e_score_correction_bias'].reshape(1, 32)
    w['L0_sgT'] = cp(inp['L0_ffn_shared_gate_w'].T)
    w['L0_suT'] = cp(inp['L0_ffn_shared_up_w'].T)
    w['L0_sdT'] = cp(inp['L0_ffn_shared_down_w'].T)
    w['L1_WqT'] = cp(inp['L1_attn_Wq'].T)
    w['L1_WkT'] = cp(inp['L1_attn_Wk'].T)
    w['L1_WvT'] = cp(inp['L1_attn_Wv'].T)
    w['L1_lng'] = inp['L1_attn_ln_gamma'].reshape(8, 128, 1)
    w['L1_lnb'] = inp['L1_attn_ln_beta'].reshape(8, 128, 1)
    w['L1_rwT'] = cp(inp['L1_ffn_router_w'].T)
    w['L1_ebias'] = inp['L1_ffn_e_bias'].reshape(1, 64)
    w['L1_gu'] = cp(np.concatenate([inp['L1_ffn_Wg'], inp['L1_ffn_Wu']], axis=2))
    w['L1_dn'] = inp['L1_ffn_Wd']
    w['L1_sgT'] = inp['L1_ffn_SWg']
    w['L1_suT'] = inp['L1_ffn_SWu']
    w['L1_sdT'] = inp['L1_ffn_SWd']
    w['L2_w_in'] = inp['L2_attn_w_in']
    w['L2_b_in'] = inp['L2_attn_b_in'].reshape(24, 128, 1)
    w['L2_sf_w'] = inp['L2_attn_sf_w']
    w['L2_sf_b'] = inp['L2_attn_sf_b'].reshape(24, 128, 1)
    for k in ('f_w1', 'f_w2', 'f_w3', 'f_wout', 'w_out'):
        w['L2_' + k] = inp['L2_attn_' + k]
    for k in ('f_b1', 'f_b2', 'f_b3'):
        w['L2_' + k] = inp['L2_attn_' + k].reshape(1, 64)
    for k in ('freq1', 'freq2', 'freq3'):
        w['L2_' + k] = inp['L2_attn_' + k].reshape(64, 1)
    w['L2_sfw_cols'] = np.ascontiguousarray(inp['L2_attn_sf_w'].T).reshape(3, 24, 128, 1)
    w['L2_filt_bias'] = inp['L2_attn_filt_bias'].reshape(8, 128, 1)
    w['L2_b_out'] = inp['L2_attn_b_out'].reshape(8, 128, 1)
    HID, HP = 2730, 2816
    w1 = inp['L2_ffn_w1']
    w1p = np.zeros((2, 1024, 2 * HP), np.float32)
    w1p[:, :, :HID] = w1[:, :, :HID]
    w1p[:, :, HP:HP + HID] = w1[:, :, HID:]
    w['L2f_w1p'] = w1p
    w2p = np.zeros((2, HP, 1024), np.float32)
    w2p[:, :HID, :] = inp['L2_ffn_w2']
    w['L2f_w2p'] = w2p
    b1 = inp['L2_ffn_b1']
    b1p = np.zeros((2, 2 * HP), np.float32)
    b1p[:, :HID] = b1[:, :HID]
    b1p[:, HP:HP + HID] = b1[:, HID:]
    w['L2f_b1p'] = b1p.reshape(2, 44, 128, 1)
    w['L2f_b2'] = inp['L2_ffn_b2'].reshape(2, 8, 128, 1)
    w['L2f_wqkv'] = inp['L2_ffn_wqkv']
    w['L2f_wo'] = inp['L2_ffn_wo']
    w['L2f_qs'] = inp['L2_ffn_q_scale'].reshape(2, 1, 64)
    w['L2f_ks'] = inp['L2_ffn_k_scale'].reshape(2, 1, 64)
    w['L2f_lng'] = inp['L2_ffn_ln_g'].reshape(2, 8, 128, 1)
    w['L2f_lnb'] = inp['L2_ffn_ln_b'].reshape(2, 8, 128, 1)
    w['L3_winT'] = cp(inp['L3_attn_w_in'].T)
    w['L3_b_in'] = inp['L3_attn_b_in'].reshape(1, 3072)
    w['L3_woutT'] = cp(inp['L3_attn_w_out'].T)
    w['L3_b_out'] = inp['L3_attn_b_out'].reshape(8, 128, 1)
    w['L3_Wg'] = inp['L3_ffn_Wg']
    w['L3_bg'] = inp['L3_ffn_bg'].reshape(1, 8)
    w['L3_W1'] = inp['L3_ffn_W1']
    w['L3_b1'] = inp['L3_ffn_b1'].reshape(8, 8, 128, 1)
    w['L3_W2'] = inp['L3_ffn_W2']
    w['L3_b2'] = inp['L3_ffn_b2'].reshape(8, 8, 128, 1)
    return w


class Ctx:
    pass


def mk_dram(nc, weights, consts):
    d = {}
    for k, v in list(weights.items()) + list(consts.items()):
        d[k] = nc.dram_tensor(k, list(v.shape), F32, kind="ExternalInput").ap()
    d['XIN'] = nc.dram_tensor('XIN', [NSEQ, NCH, 128, T], F32, kind="ExternalInput").ap()
    d['OUT'] = nc.dram_tensor('OUT', [NSEQ, NCH, 128, T], F32, kind="ExternalOutput").ap()
    for nm, shp in [('xres', [NSEQ, NCH, 128, T]), ('xnh', [NSEQ, NCH, 128, T]),
                    ('dlt', [NSEQ, NCH, 128, T]), ('x0h', [NCH, 128, T]),
                    ('vh', [NCH, 128, T]), ('KRE', [17, 128, 1024]),
                    ('KIM', [17, 128, 1024]), ('coefh', [64, T])]:
        d[nm] = nc.dram_tensor(nm, shp, F32, kind="Internal").ap()
    return d


def emit_statics(C):
    fw, dr, sb = C.fw, C.dr, C.sb
    C.idn = sb.alloc(128)
    C.idnb = sb.alloc(64, BF16)
    C.trig = sb.alloc(128)
    C.tril = sb.alloc(128)
    C.sgn = sb.alloc(128)
    C.onesc = sb.alloc(1)
    C.onescb = sb.alloc(1, BF16)[:, 0:1]
    C.onesr = sb.alloc(128)
    C.onesrb = sb.alloc(64, BF16)
    C.posn = sb.alloc(16)
    C.negidx = sb.alloc(64)
    C.negc = sb.alloc(1)
    fw.memset('vector', C.negc, -1.0)
    fw.dma('gpsimd', C.idn, dr['IDN'])
    fw.dma('gpsimd', C.idnb, dr['IDN'])
    fw.dma('gpsimd', C.trig, dr['TRIG'])
    fw.dma('gpsimd', C.tril, dr['TRIL'])
    fw.dma('gpsimd', C.sgn, dr['SGN'])
    fw.dma('gpsimd', C.onesc, dr['ONECOL'])
    fw.dma('gpsimd', C.onescb, dr['ONECOL'])
    fw.dma('gpsimd', C.onesr[0:1, :], dr['ONES'][0:1, :])
    fw.dma('gpsimd', C.onesrb[0:1, :], dr['ONES'][0:1, :])
    fw.dma('gpsimd', C.posn, dr['POSN'])
    fw.dma('gpsimd', C.negidx[0:1, :], dr['NEGIDX'])
    fw.barrier()


def warm(C, k=2):
    for _ in range(k):
        C.warmtok = C.fw.op('tensor', lambda e: e.matmul(
            C.warmps[:, 0:128], C.idnb, C.idnb, start=True, stop=True), deps=[C.warmtok])


def bcast_row(C, row_ap, n, deps, bf=False):
    i, ps, rel = C.pp.get()
    lhs = C.onesrb[0:1, :] if bf else C.onesr[0:1, :]
    t = C.fw.op('tensor', lambda e, r=row_ap, p=ps: e.matmul(
        p[:, 0:n], lhs, r, start=True, stop=True), deps=list(deps) + rel)
    return i, ps, t


def ln_norm_tiles(C, tiles, ttoks, deps0, eps=1e-5):
    fw, pp, sb = C.fw, C.pp, C.sb
    mub = sb.alloc(2048)
    rstdb = sb.alloc(2048)
    xsq = sb.alloc(512)
    r_mu = sb.alloc(512)
    r_v = sb.alloc(512)
    r_t = sb.alloc(512)
    toks = []
    prev_sq_reader = None
    for tt in range(NTT):
        i1, ps1, r1 = pp.get()
        i2, ps2, r2 = pp.get()
        for mc in range(NCH):
            fw.op('tensor', lambda e, a=tiles[(mc, tt)], s=(mc == 0): e.matmul(
                ps1[0:1, :], C.onesc, a, start=s, stop=(mc == NCH - 1)),
                deps=list(deps0) + r1 + [ttoks[(mc, tt)]])
        for mc in range(NCH):
            tsq = fw.op('scalar', lambda e, a=tiles[(mc, tt)]: e.activation(xsq, a, AF.Square),
                        deps=[ttoks[(mc, tt)], prev_sq_reader])
            prev_sq_reader = fw.op('tensor', lambda e, s=(mc == 0): e.matmul(
                ps2[0:1, :], C.onesc, xsq, start=s, stop=(mc == NCH - 1)),
                deps=[tsq] + r2)
        tmm = fw.tok('tensor')
        sl = slice(tt * 512, (tt + 1) * 512)
        warm(C, 4)
        t1 = fw.op('vector', lambda e: e.tensor_scalar(r_mu[0:1, :], ps1[0:1, :], 1.0 / D, None, ALU.mult), deps=[tmm])
        t2 = fw.op('vector', lambda e: e.tensor_scalar(r_v[0:1, :], ps2[0:1, :], 1.0 / D, None, ALU.mult), deps=[tmm])
        t3 = fw.op('vector', lambda e: e.tensor_tensor(r_t[0:1, :], r_mu[0:1, :], r_mu[0:1, :], ALU.mult), deps=[t1])
        t4 = fw.op('vector', lambda e: e.tensor_tensor(r_v[0:1, :], r_v[0:1, :], r_t[0:1, :], ALU.subtract), deps=[t2, t3])
        t5 = fw.op('vector', lambda e: e.tensor_scalar(r_v[0:1, :], r_v[0:1, :], eps, None, ALU.add), deps=[t4])
        t6 = fw.op('scalar', lambda e: e.activation(r_v[0:1, :], r_v[0:1, :], AF.Sqrt), deps=[t5])
        t7 = fw.op('vector', lambda e: e.reciprocal(r_t[0:1, :], r_v[0:1, :]), deps=[t6])
        pp.done(i1, [t1])
        pp.done(i2, [t2])
        bi1, bp1, tb1 = bcast_row(C, r_mu[0:1, :], 512, [t1, t3])
        bi2, bp2, tb2 = bcast_row(C, r_t[0:1, :], 512, [t7])
        e1 = fw.op('scalar', lambda e, o=mub[:, sl], p=bp1: e.activation(o, p[:], AF.Identity), deps=[tb1])
        e2 = fw.op('scalar', lambda e, o=rstdb[:, sl], p=bp2: e.activation(o, p[:], AF.Identity), deps=[tb2])
        pp.done(bi1, [e1])
        pp.done(bi2, [e2])
        prev_sq_reader = prev_sq_reader
        toks += [e1, e2]
    return mub, rstdb, toks


def out_phase(C, L, seq, have_delta):
    fw, pp, sb, dr = C.fw, C.pp, C.sb, C.dr
    first = (L == 0)
    last = (L == C.upto - 1)
    sb.push()
    xnew = sb.alloc(8 * 2048)
    xv = xnew.rearrange("p (c t) -> p c t", c=8)
    xsrc = dr['XIN'] if first else dr['xres']
    xdst = dr['OUT'] if last else dr['xres']
    xa = sb.alloc(512)
    da = sb.alloc(512)
    tiles, ttoks = {}, {}
    prev = {}
    for tt in range(NTT):
        sl = slice(tt * 512, (tt + 1) * 512)
        for mc in range(NCH):
            t1 = fw.dma('sync', xa, xsrc[seq, mc, :, sl], deps=[prev.get('xa')])
            if have_delta:
                t2 = fw.dma('sync', da, dr['dlt'][seq, mc, :, sl], deps=[prev.get('da')], ch=1)
                warm(C, 2)
                t3 = fw.op('vector', lambda e, o=xv[:, mc, sl]: e.tensor_tensor(o, xa, da, ALU.add), deps=[t1, t2])
                prev['da'] = t3
            else:
                t3 = fw.op('vector', lambda e, o=xv[:, mc, sl]: e.tensor_copy(o, xa), deps=[t1])
            prev['xa'] = t3
            tiles[(mc, tt)] = xv[:, mc, sl]
            ttoks[(mc, tt)] = t3
            fw.dma('sync', xdst[seq, mc, :, sl], xv[:, mc, sl], deps=[t3], ch=2)
    if not last:
        mub, rstdb, stoks = ln_norm_tiles(C, tiles, ttoks, [])
        tmp = sb.alloc(512)
        ptmp = None
        for tt in range(NTT):
            sl = slice(tt * 512, (tt + 1) * 512)
            for mc in range(NCH):
                t1 = fw.op('vector', lambda e, a=tiles[(mc, tt)], m=mub[:, sl]: e.tensor_tensor(tmp, a, m, ALU.subtract),
                           deps=stoks + [ttoks[(mc, tt)], ptmp])
                t2 = fw.op('vector', lambda e, r=rstdb[:, sl]: e.tensor_tensor(tmp, tmp, r, ALU.mult), deps=[t1])
                ptmp = fw.dma('sync', dr['xnh'][seq, mc, :, sl], tmp, deps=[t2], ch=2)
    fw.barrier()
    sb.pop()


def load_wblock(C, dst, dram2d, r0, r1, c0, c1, deps=()):
    nr = r1 - r0
    src = dram2d[r0:r1, c0:c1].rearrange("(c p) n -> p c n", p=128)
    return C.fw.dma('gpsimd', dst, src, deps=deps)


def moe_phase(C, L, seq):
    import os
    dbg = os.environ.get('MOE_DBG', 'rse')
    fw, pp, sb, dr = C.fw, C.pp, C.sb, C.dr
    E = 32 if L == 0 else 64
    NG = 8
    GS = E // NG
    TOPK = 6 if L == 0 else 8
    gwT = dr['L0_gwT'] if L == 0 else dr['L1_rwT']
    ebias = dr['L0_ebias'] if L == 0 else dr['L1_ebias']
    guD = dr['L0_guT'] if L == 0 else dr['L1_gu']
    dnD = dr['L0_dnT'] if L == 0 else dr['L1_dn']
    sg, su, sd = (dr[f'L{L}_sgT'], dr[f'L{L}_suT'], dr[f'L{L}_sdT'])
    sb.push()
    acc = sb.alloc(8 * 2048)
    accv = acc.rearrange("p (c t) -> p c t", c=8)
    xnb = sb.alloc(8 * 1024, BF16)
    xnbv = xnb.rearrange("p (c t) -> p c t", c=8)
    coef = sb.alloc(2048)
    ldx = [fw.dma('gpsimd', xnbv[:, mc, :], dr['xnh'][seq, mc]) for mc in range(NCH)]

    if 'r' not in dbg:
        tzz = fw.memset('vector', coef[0:E, :], 0.01, deps=[fw.tok('scalar')])
        fw.dma('sync', dr['coefh'][0:E, :], coef[0:E, :], deps=[tzz])
        fw.barrier()
    else:
        moe_routing(C, L, seq, E, NG, GS, TOPK, gwT, ebias, coef)
    if 's' in dbg:
        moe_shared(C, seq, sg, su, sd, acc, accv, xnbv, ldx)
    else:
        tzz = fw.memset('vector', acc, 0.0)
        fw.barrier()
    if 'e' in dbg:
        moe_experts(C, L, seq, E, guD, dnD, acc, accv, xnbv)
    for mc in range(NCH):
        fw.dma('sync', dr['dlt'][seq, mc], accv[:, mc, :], deps=[fw.tok('vector'), fw.tok('scalar')])
    fw.barrier()
    sb.pop()
    return


def moe_routing(C, L, seq, E, NG, GS, TOPK, gwT, ebias, coef):
    fw, pp, sb, dr = C.fw, C.pp, C.sb, C.dr
    sb.push()
    rw = sb.alloc(8 * E)
    rwv = rw.rearrange("p (c e) -> p c e", c=8)
    trw = fw.dma('sync', rwv, gwT.rearrange("(c p) e -> p c e", p=128))
    ebrow = sb.alloc(E)
    teb = fw.dma('sync', ebrow[0:1, :], ebias)
    ebb = sb.alloc(E)
    nib = sb.alloc(E)
    bi, bp, tb = bcast_row(C, ebrow[0:1, :], E, [teb])
    t = fw.op('scalar', lambda e: e.activation(ebb, bp[:, 0:E], AF.Identity), deps=[tb])
    pp.done(bi, [t])
    bi, bp, tb = bcast_row(C, C.negidx[0:1, 0:E], E, [t])
    t = fw.op('scalar', lambda e: e.activation(nib, bp[:, 0:E], AF.Identity), deps=[tb])
    pp.done(bi, [t])
    xf = [sb.alloc(128) for _ in range(2)]
    score = sb.alloc(E)
    sfc = sb.alloc(E)
    gsc = sb.alloc(NG)
    gwork = sb.alloc(NG)
    gmask = sb.alloc(NG)
    gtmp = sb.alloc(E)
    msk = sb.alloc(E)
    sel = sb.alloc(E)
    wrow = sb.alloc(E)
    colp = sb.alloc(4)
    prevx = [None, None]
    prev_tr = None
    for tc in range(NTC):
        ip, ps, rel = pp.get()
        lastmm = None
        for mc in range(NCH):
            xa = xf[mc % 2]
            tl = fw.dma('sync', xa, dr['xnh'][seq, mc, :, tc * 128:(tc + 1) * 128],
                        deps=[prevx[mc % 2]], ch=mc % 2)
            lastmm = fw.op('tensor', lambda e, a=xa, s=(mc == 0): e.matmul(
                ps[:, 0:E], a, rwv[:, mc, :], start=s, stop=(mc == NCH - 1)),
                deps=[tl, trw] + rel)
            prevx[mc % 2] = lastmm
        t0 = fw.op('scalar', lambda e: e.activation(score, ps[:, 0:E], AF.Sigmoid), deps=[lastmm, fw.tok('vector')])
        pp.done(ip, [t0])
        warm(C, 6)
        t1 = fw.op('vector', lambda e: e.tensor_tensor(sfc, score, ebb, ALU.add), deps=[t0])
        sfc3 = sfc.rearrange("p (g s) -> p g s", g=NG)
        if L == 0:
            tg = fw.op('vector', lambda e: e.reduce_max(gsc.rearrange("p (g o) -> p g o", g=NG), sfc3, AX.X), deps=[t1])
        else:
            gt3 = gtmp.rearrange("p (g s) -> p g s", g=NG)
            tm1 = fw.op('vector', lambda e: e.reduce_max(gwork.rearrange("p (g o) -> p g o", g=NG), sfc3, AX.X), deps=[t1])
            to = fw.op('vector', lambda e: e.tensor_tensor(
                gt3, sfc3, gwork.rearrange("p (g o) -> p g o", g=NG).broadcast_to([128, NG, GS]), ALU.is_ge), deps=[tm1])
            tv = fw.op('vector', lambda e: e.scalar_tensor_tensor(gt3, gt3, -1e30, sfc3, ALU.mult, ALU.add), deps=[to])
            tm2 = fw.op('vector', lambda e: e.reduce_max(gsc.rearrange("p (g o) -> p g o", g=NG), gt3, AX.X), deps=[tv])
            tg = fw.op('vector', lambda e: e.tensor_tensor(gsc, gsc, gwork, ALU.add), deps=[tm2])
        tg = fw.op('vector', lambda e: e.tensor_copy(gwork, gsc), deps=[tg])
        tg = fw.memset('vector', gmask, 0.0, deps=[tg])
        for r in range(4):
            tg = fw.op('vector', lambda e: e.reduce_max(colp[:, 0:1], gwork, AX.X), deps=[tg])
            tg = fw.op('vector', lambda e: e.tensor_scalar(gtmp[:, 0:NG], gwork, colp[:, 0:1], None, ALU.is_ge), deps=[tg])
            tg = fw.op('vector', lambda e: e.tensor_tensor(gmask, gmask, gtmp[:, 0:NG], ALU.add), deps=[tg])
            tg = fw.op('vector', lambda e: e.scalar_tensor_tensor(gwork, gtmp[:, 0:NG], -1e30, gwork, ALU.mult, ALU.add), deps=[tg])
        g3 = gmask.rearrange("p (g o) -> p g o", g=NG)
        m3 = msk.rearrange("p (g s) -> p g s", g=NG)
        tg = fw.op('vector', lambda e: e.tensor_tensor(m3, sfc3, g3.broadcast_to([128, NG, GS]), ALU.mult), deps=[tg, t1])
        tg2 = fw.op('vector', lambda e: e.tensor_scalar(gwork, gmask, -1.0, 1.0, ALU.mult, ALU.add), deps=[tg])
        tg2 = fw.op('vector', lambda e: e.tensor_tensor(
            gtmp.rearrange("p (g s) -> p g s", g=NG), nib.rearrange("p (g s) -> p g s", g=NG),
            gwork.rearrange("p (g o) -> p g o", g=NG).broadcast_to([128, NG, GS]), ALU.mult), deps=[tg2])
        tg = fw.op('vector', lambda e: e.tensor_tensor(msk, msk, gtmp, ALU.add), deps=[tg2])
        ts_ = fw.memset('vector', sel, 0.0, deps=[tg])
        for r in range(TOPK):
            ts_ = fw.op('vector', lambda e: e.reduce_max(colp[:, 0:1], msk, AX.X), deps=[ts_])
            ts_ = fw.op('vector', lambda e: e.tensor_scalar(gtmp[:, 0:E] if False else wrow, msk, colp[:, 0:1], None, ALU.is_ge), deps=[ts_])
            ts_ = fw.op('vector', lambda e: e.tensor_tensor(sel, sel, wrow, ALU.add), deps=[ts_])
            ts_ = fw.op('vector', lambda e: e.scalar_tensor_tensor(msk, wrow, -1e30, msk, ALU.mult, ALU.add), deps=[ts_])
        ts_ = fw.op('vector', lambda e: e.tensor_tensor(wrow, sel, score, ALU.mult), deps=[ts_, t0])
        ts_ = fw.op('vector', lambda e: e.reduce_sum(colp[:, 1:2], wrow, AX.X), deps=[ts_])
        ts_ = fw.op('vector', lambda e: e.tensor_scalar(colp[:, 1:2], colp[:, 1:2], 1e-20, None, ALU.add), deps=[ts_])
        ts_ = fw.op('vector', lambda e: e.reciprocal(colp[:, 2:3], colp[:, 1:2]), deps=[ts_])
        ts_ = fw.op('vector', lambda e: e.tensor_scalar(wrow, wrow, colp[:, 2:3], 2.5, ALU.mult, ALU.mult), deps=[ts_])
        it, pst, relt = pp.get()
        tt_ = fw.op('tensor', lambda e: e.transpose(pst[0:E, 0:128], wrow, C.idn), deps=[ts_] + relt)
        tev = fw.op('scalar', lambda e, o=coef[0:E, tc * 128:(tc + 1) * 128]: e.activation(o, pst[0:E, 0:128], AF.Identity),
                    deps=[tt_, prev_tr])
        pp.done(it, [tev])
        prev_tr = tev
    tch = fw.dma('sync', dr['coefh'][0:E, :], coef[0:E, :], deps=[prev_tr])
    fw.barrier()
    sb.pop()


def moe_shared(C, seq, sg, su, sd, acc, accv, xnbv, ldx):
    fw, pp, sb, dr = C.fw, C.pp, C.sb, C.dr
    sb.push()
    hsh = sb.alloc(8 * 1024, BF16)
    hshv = hsh.rearrange("p (c t) -> p c t", c=8)
    wga = [sb.alloc(8 * 64, BF16) for _ in range(2)]
    wua = [sb.alloc(8 * 64, BF16) for _ in range(2)]
    tmpf = sb.alloc(512)
    relg = [None, None]
    for ic in range(NCH):
        s = ic % 2
        wg = wga[s].rearrange("p (c n) -> p c n", c=8)
        wu = wua[s].rearrange("p (c n) -> p c n", c=8)
        tg = fw.dma('gpsimd', wg, sg[0:1024, ic * 128:(ic + 1) * 128].rearrange("(c p) n -> p c n", p=128), deps=[relg[s]], ch=s)
        tu = fw.dma('gpsimd', wu, su[0:1024, ic * 128:(ic + 1) * 128].rearrange("(c p) n -> p c n", p=128), deps=[relg[s]], ch=2 + s)
        lastu = None
        for tt in range(NTT):
            sl = slice(tt * 512, (tt + 1) * 512)
            ig, pg, rg = pp.get()
            iu, pu, ru = pp.get()
            for mc in range(NCH):
                fw.op('tensor', lambda e, m=mc, s0=(mc == 0): e.matmul(
                    pg[:], wg[:, m, :], xnbv[:, m, sl], start=s0, stop=(m == NCH - 1)),
                    deps=[tg] + ldx + rg)
            for mc in range(NCH):
                lastu = fw.op('tensor', lambda e, m=mc, s0=(mc == 0): e.matmul(
                    pu[:], wu[:, m, :], xnbv[:, m, sl], start=s0, stop=(m == NCH - 1)),
                    deps=[tu] + ru)
            t1 = fw.op('scalar', lambda e: e.activation(tmpf, pg[:], AF.Silu), deps=[lastu, fw.tok('vector')])
            t2 = fw.op('vector', lambda e, o=hshv[:, ic, sl]: e.tensor_tensor(o, tmpf, pu[:], ALU.mult), deps=[t1])
            pp.done(ig, [t1])
            pp.done(iu, [t2])
        relg[s] = fw.tok('tensor')
    wda = [sb.alloc(8 * 64, BF16) for _ in range(2)]
    reld = [None, None]
    hold = fw.tok('vector')
    accw = []
    for mc in range(NCH):
        s = mc % 2
        wd = wda[s].rearrange("p (c n) -> p c n", c=8)
        td = fw.dma('gpsimd', wd, sd[0:1024, mc * 128:(mc + 1) * 128].rearrange("(c p) n -> p c n", p=128), deps=[reld[s]], ch=s)
        for tt in range(NTT):
            sl = slice(tt * 512, (tt + 1) * 512)
            idx, pd, rd = pp.get()
            last = None
            for ic in range(NCH):
                last = fw.op('tensor', lambda e, i=ic, s0=(ic == 0): e.matmul(
                    pd[:], wd[:, i, :], hshv[:, i, sl], start=s0, stop=(i == NCH - 1)),
                    deps=[td, hold] + rd)
            t1 = fw.op('scalar', lambda e, o=accv[:, mc, sl]: e.activation(o, pd[:], AF.Identity), deps=[last])
            pp.done(idx, [t1])
            accw.append(t1)
        reld[s] = fw.tok('tensor')
    fw.barrier()
    sb.pop()


def moe_experts(C, L, seq, E, guD, dnD, acc, accv, xnbv):
    fw, pp, sb, dr = C.fw, C.pp, C.sb, C.dr
    sb.push()
    gus = [sb.alloc(8 * 512, BF16) for _ in range(2)]
    dns = [sb.alloc(4 * 512, BF16) for _ in range(2)]
    hbuf = sb.alloc(4 * 1024, BF16)
    hv = hbuf.rearrange("p (c t) -> p c t", c=4)
    cbf = sb.alloc(2048)
    crow = sb.alloc(512)
    crow_rel = []
    tmps = sb.alloc(512)
    tmp2 = sb.alloc(512)
    wrel = [[], []]
    t0_ = fw.tok('scalar')
    acc_tok = {(mc, tt): t0_ for mc in range(NCH) for tt in range(NTT)}
    cbf_rel = []
    h_rel = []
    prev_silu = None
    prev_tmp2 = None
    prev_t3 = None
    for e_ in range(E):
        s = e_ % 2
        gv = gus[s].rearrange("p (c n) -> p c n", c=8)
        dv = dns[s].rearrange("p (c n) -> p c n", c=4)
        tw1 = fw.dma('gpsimd', gv, guD[e_].rearrange("(c p) n -> p c n", p=128), deps=wrel[s], ch=s)
        tw2 = fw.dma('gpsimd', dv, dnD[e_].rearrange("(c p) n -> p c n", p=128), deps=wrel[s], ch=2 + s)
        cbts = []
        for tt in range(NTT):
            sl = slice(tt * 512, (tt + 1) * 512)
            tl = fw.dma('sync', crow[0:1, :], dr['coefh'][e_, sl], deps=crow_rel)
            bi, bp, tb = bcast_row(C, crow[0:1, :], 512, cbf_rel + [tl])
            crow_rel = [tb]
            t = fw.op('scalar', lambda e, o=cbf[:, sl], p=bp: e.activation(o, p[:], AF.Identity),
                      deps=[tb] + cbf_rel)
            pp.done(bi, [t])
            cbts.append(t)
        cbf_rel = []
        htoks = {}
        for icg in range(4):
            for tt in range(NTT):
                sl = slice(tt * 512, (tt + 1) * 512)
                ig, pg, rg = pp.get()
                iu, pu, ru = pp.get()
                lastg = lastu = None
                for mc in range(NCH):
                    lastg = fw.op('tensor', lambda e, m=mc, i=icg, pg=pg, sl=sl, s0=(mc == 0): e.matmul(
                        pg[:], gv[:, m, i * 128:(i + 1) * 128], xnbv[:, m, sl],
                        start=s0, stop=(m == NCH - 1)), deps=[tw1] + rg)
                for mc in range(NCH):
                    lastu = fw.op('tensor', lambda e, m=mc, i=icg, pu=pu, sl=sl, s0=(mc == 0): e.matmul(
                        pu[:], gv[:, m, 512 + i * 128:512 + (i + 1) * 128], xnbv[:, m, sl],
                        start=s0, stop=(m == NCH - 1)), deps=ru)
                t1 = fw.op('scalar', lambda e, pg=pg: e.activation(tmps, pg[:], AF.Silu),
                           deps=[lastg, prev_tmp2])
                t2 = fw.op('vector', lambda e, pu=pu: e.tensor_tensor(tmp2, tmps, pu[:], ALU.mult),
                           deps=[t1, lastu, prev_t3])
                t3 = fw.op('vector', lambda e, o=hv[:, icg, sl], c=cbf[:, sl]: e.tensor_tensor(o, tmp2, c, ALU.mult),
                           deps=[t2, cbts[tt]] + h_rel)
                pp.done(ig, [t1])
                pp.done(iu, [t2])
                prev_t3 = t3
                prev_tmp2 = t2
                cbf_rel.append(t3)
                htoks[(icg, tt)] = t3
        for tt in range(NTT):
            sl = slice(tt * 512, (tt + 1) * 512)
            hts = [htoks[(i, tt)] for i in range(4)]
            for mc in range(NCH):
                idx2, pd, rd = pp.get()
                last = None
                for ic in range(4):
                    last = fw.op('tensor', lambda e, i=ic, m=mc, pd=pd, sl=sl, s0=(ic == 0): e.matmul(
                        pd[:], dv[:, i, m * 128:(m + 1) * 128], hv[:, i, sl],
                        start=s0, stop=(i == 3)), deps=[tw2] + rd + hts)
                t4 = fw.op('vector', lambda e, o=accv[:, mc, sl], pd=pd: e.tensor_tensor(o, o, pd[:], ALU.add),
                           deps=[last, acc_tok[(mc, tt)]])
                pp.done(idx2, [t4])
                acc_tok[(mc, tt)] = t4
        h_rel = [fw.tok('tensor')]
        wrel[s] = [fw.tok('tensor')]
    fw.barrier()
    sb.pop()


def l0_attn_phase(C, seq):
    fw, pp, sb, dr = C.fw, C.pp, C.sb, C.dr
    sb.push()
    o_sb = sb.alloc(8 * 1024, BF16)
    ov = o_sb.rearrange("p (h t) -> p h t", h=8)
    sb.push()
    q_fm = sb.alloc(8 * 1024, BF16)
    qfv = q_fm.rearrange("p (h t) -> p h t", h=8)
    k_fm = sb.alloc(8 * 1024, BF16)
    kfv = k_fm.rearrange("p (h t) -> p h t", h=8)
    v_tm = sb.alloc(16 * 512, BF16)
    vtv = v_tm.rearrange("p (c d) -> p c d", c=16)
    cosv = sb.alloc(16 * 64)
    cv = cosv.rearrange("p (c d) -> p c d", c=16)
    sinv = sb.alloc(16 * 64)
    sv = sinv.rearrange("p (c d) -> p c d", c=16)
    tcs = []
    for tc in range(NTC):
        tcs.append(fw.dma('sync', cv[:, tc, :], dr['COS0'][tc * 128:(tc + 1) * 128, :]))
        tcs.append(fw.dma('sync', sv[:, tc, :], dr['SIN0'][tc * 128:(tc + 1) * 128, :]))
    sb.push()
    xns = [sb.alloc(8 * 64, BF16) for _ in range(2)]
    wslot = sb.alloc(8 * 512, BF16)
    wv = wslot.rearrange("p (c n) -> p c n", c=8)
    qtmp = sb.alloc(1024)
    q3 = qtmp.rearrange("p (h d) -> p h d", h=8)
    qsq = sb.alloc(1024)
    qq3 = qsq.rearrange("p (h d) -> p h d", h=8)
    ssc = sb.alloc(16)
    qrt = sb.alloc(1024)
    r3 = qrt.rearrange("p (h d) -> p h d", h=8)
    qb = sb.alloc(256, BF16)

    def qkv_pass(kind, wdram, dstv):
        tw = fw.dma('gpsimd', wv, wdram.rearrange("(c p) n -> p c n", p=128),
                    deps=[fw.tok('tensor')])
        prevx = [None, None]
        prev_cons = None
        for tc in range(NTC):
            s_ = tc % 2
            xa = xns[s_].rearrange("p (c n) -> p c n", c=8)
            tl = fw.dma('gpsimd', xa, dr['xnh'][seq].rearrange("c p t -> p c t")[:, :, tc * 128:(tc + 1) * 128],
                        deps=[prevx[s_]], ch=s_)
            if kind == 'v':
                ip, ps, rel = pp.get()
                ip2, ps2, rel2 = pp.get()
                last = None
                for mc in range(NCH):
                    fw.op('tensor', lambda e, m=mc, s0=(mc == 0): e.matmul(
                        ps[:], xa[:, m, :], wv[:, m, 0:512], start=s0, stop=(m == NCH - 1)),
                        deps=[tl, tw] + rel)
                for mc in range(NCH):
                    last = fw.op('tensor', lambda e, m=mc, s0=(mc == 0): e.matmul(
                        ps2[:], xa[:, m, :], wv[:, m, 512:1024], start=s0, stop=(m == NCH - 1)),
                        deps=rel2)
                prevx[s_] = last
                t1 = fw.op('scalar', lambda e, o=vtv[:, tc, 0:512]: e.activation(o, ps[:], AF.Identity), deps=[last])
                t2 = fw.op('scalar', lambda e, o=vtv[:, tc, 512:1024]: e.activation(o, ps2[:], AF.Identity), deps=[last])
                pp.done(ip, [t1])
                pp.done(ip2, [t2])
                continue
            ip, ps, rel = pp.get()
            ip2, ps2, rel2 = pp.get()
            last = None
            for mc in range(NCH):
                fw.op('tensor', lambda e, m=mc, s0=(mc == 0): e.matmul(
                    ps[:], xa[:, m, :], wv[:, m, 0:512], start=s0, stop=(m == NCH - 1)),
                    deps=[tl, tw] + rel)
            for mc in range(NCH):
                last = fw.op('tensor', lambda e, m=mc, s0=(mc == 0): e.matmul(
                    ps2[:], xa[:, m, :], wv[:, m, 512:1024], start=s0, stop=(m == NCH - 1)),
                    deps=rel2)
            prevx[s_] = last
            t1 = fw.op('scalar', lambda e: e.activation(qtmp[:, 0:512], ps[:], AF.Identity), deps=[last, prev_cons])
            t2 = fw.op('scalar', lambda e: e.activation(qtmp[:, 512:1024], ps2[:], AF.Identity), deps=[last, prev_cons])
            pp.done(ip, [t1])
            pp.done(ip2, [t2])
            t3 = fw.op('scalar', lambda e: e.activation(qsq, qtmp, AF.Square), deps=[t1, t2])
            t4 = fw.op('vector', lambda e: e.reduce_sum(ssc[:, 0:8].rearrange("p (h o) -> p h o", h=8), qq3, AX.X), deps=[t3])
            t4 = fw.op('vector', lambda e: e.tensor_scalar(ssc[:, 0:8], ssc[:, 0:8], 1.0 / 128.0, 1e-6, ALU.mult, ALU.add), deps=[t4])
            t4 = fw.op('scalar', lambda e: e.activation(ssc[:, 8:16], ssc[:, 0:8], AF.Sqrt), deps=[t4])
            t4 = fw.op('vector', lambda e: e.reciprocal(ssc[:, 0:8], ssc[:, 8:16]), deps=[t4])
            tn = []
            for h in range(8):
                tn.append(fw.op('vector', lambda e, h=h: e.tensor_scalar(
                    q3[:, h, :], q3[:, h, :], ssc[:, h:h + 1], None, ALU.mult), deps=[t4]))
            cb = cv[:, tc:tc + 1, :].broadcast_to([128, 8, 64])
            sbt = sv[:, tc:tc + 1, :].broadcast_to([128, 8, 64])
            ua = fw.op('vector', lambda e: e.tensor_tensor(r3[:, :, 0:64], q3[:, :, 0:64], cb, ALU.mult), deps=tn + tcs)
            ub = fw.op('vector', lambda e: e.tensor_tensor(qq3[:, :, 0:64], q3[:, :, 64:128], sbt, ALU.mult), deps=tn + [t3])
            uc = fw.op('vector', lambda e: e.tensor_tensor(r3[:, :, 0:64], r3[:, :, 0:64], qq3[:, :, 0:64], ALU.subtract), deps=[ua, ub])
            ud = fw.op('vector', lambda e: e.tensor_tensor(r3[:, :, 64:128], q3[:, :, 0:64], sbt, ALU.mult), deps=tn)
            ue = fw.op('vector', lambda e: e.tensor_tensor(qq3[:, :, 64:128], q3[:, :, 64:128], cb, ALU.mult), deps=tn)
            uf = fw.op('vector', lambda e: e.tensor_tensor(r3[:, :, 64:128], r3[:, :, 64:128], qq3[:, :, 64:128], ALU.add), deps=[ud, ue])
            dv = qfv if kind == 'q' else kfv
            qb_rel = None
            for hh in range(2):
                g1 = fw.op('vector', lambda e, hh=hh: e.tensor_copy(
                    qb[:, 0:512], qrt[:, hh * 512:(hh + 1) * 512]), deps=[uc, uf, qb_rel])
                for h2 in range(4):
                    it, pst, relt = pp.get()
                    pb16a = pst[:, 0:64].bitcast(BF16)
                    tt_ = fw.op('tensor', lambda e, h2=h2, pb16a=pb16a: e.transpose(
                        pb16a, qb[:, h2 * 128:(h2 + 1) * 128], C.idnb), deps=[g1] + relt)
                    h = hh * 4 + h2
                    ev = fw.op('scalar', lambda e, h=h, tc=tc, pb16a=pb16a: e.activation(
                        dv[:, h, tc * 128:(tc + 1) * 128], pb16a, AF.Identity), deps=[tt_])
                    pp.done(it, [ev])
                    qb_rel = tt_
            prev_cons = fw.tok('vector')

    qkv_pass('q', dr['L0_wq'], qfv)
    qkv_pass('k', dr['L0_wk'], kfv)
    qkv_pass('v', dr['L0_wv'], None)
    fw.barrier()
    sb.pop()
    sb.push()
    pts = [sb.alloc(256, BF16) for _ in range(4)]
    rrow = sb.alloc(512)
    rrec = sb.alloc(512)
    rb = sb.alloc(512)
    otmp = None
    prel = [[], [], [], []]
    pi = 0
    for h in range(8):
        for qt in range(NTT):
            sl = slice(qt * 512, (qt + 1) * 512)
            iR, psR, relR = pp.get(hold=True)
            iO, psO, relO = pp.get(hold=True)
            nkc = qt * 4 + 4
            lastR = lastO = None
            pend = None
            for kc in range(nkc):
                iS, psS, relS = pp.get()
                tm = fw.op('tensor', lambda e, kc=kc, h=h, psS=psS: e.matmul(
                    psS[:], kfv[:, h, kc * 128:(kc + 1) * 128], qfv[:, h, sl],
                    start=True, stop=True), deps=relS)
                mts = [tm]
                for m in range(4):
                    qc = qt * 4 + m
                    if kc == qc:
                        mts.append(fw.op('vector', lambda e, m=m, psS=psS: e.tensor_tensor(
                            psS[:, m * 128:(m + 1) * 128], psS[:, m * 128:(m + 1) * 128], C.trig, ALU.add), deps=[tm]))
                    elif kc > qc:
                        mts.append(fw.op('vector', lambda e, m=m, psS=psS: e.tensor_scalar(
                            psS[:, m * 128:(m + 1) * 128], psS[:, m * 128:(m + 1) * 128], -1e30, None, ALU.add), deps=[tm]))
                pt = pts[pi % 4]
                texp = fw.op('scalar', lambda e, pt=pt, psS=psS: e.activation(pt, psS[:], AF.Exp), deps=mts + prel[pi % 4])
                pp.done(iS, [texp])
                if pend is not None:
                    (ptx, texpx, kcx, ipx) = pend
                    lastR = fw.op('tensor', lambda e, pt=ptx, s0=(kcx == 0): e.matmul(
                        psR[0:1, :], C.onescb, pt, start=s0, stop=False), deps=[texpx] + (relR if kcx == 0 else []))
                    lastO = fw.op('tensor', lambda e, pt=ptx, kc=kcx, h=h, s0=(kcx == 0): e.matmul(
                        psO[:], vtv[:, kc, h * 128:(h + 1) * 128], pt, start=s0, stop=False), deps=[texpx] + (relO if kcx == 0 else []))
                    prel[ipx] = [lastR, lastO]
                pend = (pt, texp, kc, pi % 4)
                pi += 1
            (ptx, texpx, kcx, ipx) = pend
            lastR = fw.op('tensor', lambda e, pt=ptx, s0=(kcx == 0): e.matmul(
                psR[0:1, :], C.onescb, pt, start=s0, stop=True), deps=[texpx] + (relR if kcx == 0 else []))
            lastO = fw.op('tensor', lambda e, pt=ptx, kc=kcx, h=h, s0=(kcx == 0): e.matmul(
                psO[:], vtv[:, kc, h * 128:(h + 1) * 128], pt, start=s0, stop=True), deps=[texpx] + (relO if kcx == 0 else []))
            prel[ipx] = [lastR, lastO]
            t1 = fw.op('vector', lambda e: e.tensor_copy(rrow[0:1, :], psR[0:1, :]), deps=[lastR])
            pp.done(iR, [t1])
            t2 = fw.op('vector', lambda e: e.reciprocal(rrec[0:1, :], rrow[0:1, :]), deps=[t1])
            bi, bp, tb = bcast_row(C, rrec[0:1, :], 512, [t2])
            t3 = fw.op('scalar', lambda e: e.activation(rb, bp[:], AF.Identity), deps=[tb, otmp])
            pp.done(bi, [t3])
            t4 = fw.op('vector', lambda e, h=h: e.tensor_tensor(ov[:, h, sl], psO[:], rb, ALU.mult), deps=[t3, lastO])
            pp.done(iO, [t4])
            otmp = t4
    fw.barrier()
    sb.pop()
    sb.pop()
    sb.push()
    wslot = sb.alloc(8 * 512, BF16)
    wv2 = wslot.rearrange("p (c n) -> p c n", c=8)
    tw = fw.dma('gpsimd', wv2, dr['L0_wo'].rearrange("(c p) n -> p c n", p=128))
    tmpo = sb.alloc(512)
    dta = sb.alloc(512)
    pv1 = pv2 = None
    for mc in range(NCH):
        for tt in range(NTT):
            sl = slice(tt * 512, (tt + 1) * 512)
            ip, ps, rel = pp.get()
            last = None
            for h in range(8):
                last = fw.op('tensor', lambda e, h=h, m=mc, s0=(h == 0): e.matmul(
                    ps[:], wv2[:, h, m * 128:(m + 1) * 128], ov[:, h, sl],
                    start=s0, stop=(h == NCH - 1)), deps=[tw] + rel)
            t1 = fw.op('scalar', lambda e: e.activation(tmpo, ps[:], AF.Identity), deps=[last, pv1])
            pp.done(ip, [t1])
            warm(C, 2)
            t2 = fw.dma('sync', dta, dr['dlt'][seq, mc, :, sl], deps=[pv2], ch=1)
            t3 = fw.op('vector', lambda e: e.tensor_tensor(dta, dta, tmpo, ALU.add), deps=[t1, t2])
            pv1 = t3
            pv2 = fw.dma('sync', dr['dlt'][seq, mc, :, sl], dta, deps=[t3], ch=2)
    fw.barrier()
    sb.pop()
    sb.pop()


def build_program(weights, consts, upto=4, skip=()):
    nc = bass.Bass("TRN2", target_bir_lowering=False)
    C = Ctx()
    C.upto = upto
    C.nc = nc
    C.dr = mk_dram(nc, weights, consts)
    with ExitStack() as stk:
        E = stk.enter_context
        ar = E(nc.sbuf_tensor("arena", [128, 48800], F32))
        psums = [E(nc.psum_tensor(f"ps{i}", [128, 512], F32)) for i in range(8)]
        sems = {e: E(nc.semaphore(f"sem_{e}")) for e in ALLSEM}
        blk = E(nc.Block())
        C.fw = FW(nc)
        C.fw.sems = sems
        C.sb = SB(ar, 48800)
        C.pp = PP([p[:] for p in psums[:7]])
        C.warmps = psums[7]
        C.warmtok = None
        emit_statics(C)
        for L in range(upto):
            if L == 0:
                for seq in range(NSEQ):
                    pre_ln(C, seq)
            if L == 2 and (2, 'attn') not in skip:
                l2_filter_phase(C)
            for seq in range(NSEQ):
                hav_f = (L, 'ffn') not in skip
                hav_a = (L, 'attn') not in skip
                if hav_f:
                    if L in (0, 1):
                        moe_phase(C, L, seq)
                    elif L == 3:
                        l3_ffn_phase(C, seq)
                    else:
                        l2_ffn_phase(C, seq)
                elif hav_a:
                    zero_dlt(C, seq)
                if hav_a:
                    if L == 0:
                        l0_attn_phase(C, seq)
                    elif L == 1:
                        l1_attn_phase(C, seq)
                    elif L == 2:
                        l2_attn_phase(C, seq)
                    else:
                        l3_attn_phase(C, seq)
            for seq in range(NSEQ):
                out_phase(C, L, seq, have_delta=((L, 'ffn') not in skip or (L, 'attn') not in skip))
        ft = C.fw.all_toks()
        C.fw.finalize(blk, final_deps=ft)
    return nc


def zero_dlt(C, seq):
    fw, sb, dr = C.fw, C.sb, C.dr
    sb.push()
    z = sb.alloc(2048)
    t = fw.memset('vector', z, 0.0)
    for mc in range(NCH):
        fw.dma('sync', dr['dlt'][seq, mc], z, deps=[t])
    fw.barrier()
    sb.pop()


def pre_ln(C, seq):
    fw, sb, dr = C.fw, C.sb, C.dr
    sb.push()
    xbuf = sb.alloc(8 * 2048)
    xv = xbuf.rearrange("p (c t) -> p c t", c=8)
    tiles, ttoks = {}, {}
    for tt in range(NTT):
        sl = slice(tt * 512, (tt + 1) * 512)
        for mc in range(NCH):
            t1 = fw.dma('sync', xv[:, mc, sl], dr['XIN'][seq, mc, :, sl])
            tiles[(mc, tt)] = xv[:, mc, sl]
            ttoks[(mc, tt)] = t1
    mub, rstdb, stoks = ln_norm_tiles(C, tiles, ttoks, [])
    tmp = sb.alloc(512)
    ptmp = None
    for tt in range(NTT):
        sl = slice(tt * 512, (tt + 1) * 512)
        for mc in range(NCH):
            t1 = fw.op('vector', lambda e, a=tiles[(mc, tt)], m=mub[:, sl]: e.tensor_tensor(tmp, a, m, ALU.subtract),
                       deps=stoks + [ttoks[(mc, tt)], ptmp])
            t2 = fw.op('vector', lambda e, r=rstdb[:, sl]: e.tensor_tensor(tmp, tmp, r, ALU.mult), deps=[t1])
            ptmp = fw.dma('sync', dr['xnh'][seq, mc, :, sl], tmp, deps=[t2], ch=2)
    fw.barrier()
    sb.pop()


def l1_attn_phase(C, seq):
    fw, pp, sb, dr = C.fw, C.pp, C.sb, C.dr
    sb.push()
    o_sb = sb.alloc(8 * 2048)
    osv = o_sb.rearrange("p (c t) -> p c t", c=8)
    for hg in range(2):
        sb.push()
        qf = sb.alloc(4 * 1024, BF16)
        qfv = qf.rearrange("p (c t) -> p c t", c=4)
        qnf = sb.alloc(4 * 1024, BF16)
        qnv = qnf.rearrange("p (c t) -> p c t", c=4)
        kf = sb.alloc(4 * 1024, BF16)
        kfv = kf.rearrange("p (c t) -> p c t", c=4)
        ksf = sb.alloc(4 * 1024, BF16)
        ksv = ksf.rearrange("p (c t) -> p c t", c=4)
        vtm = sb.alloc(16 * 256, BF16)
        vtv = vtm.rearrange("p (c d) -> p c d", c=16)
        xns = [sb.alloc(8 * 64, BF16) for _ in range(2)]
        wslot = sb.alloc(8 * 256, BF16)
        wv = wslot.rearrange("p (c n) -> p c n", c=8)
        qtmp = sb.alloc(512)
        qstmp = sb.alloc(512)
        qb = sb.alloc(256, BF16)
        qsb = sb.alloc(256, BF16)

        def qkv_pass(kind, wd):
            tw = fw.dma('gpsimd', wv, wd[:, hg * 512:(hg + 1) * 512].rearrange("(c p) n -> p c n", p=128),
                        deps=[fw.tok('tensor')])
            prevx = [None, None]
            prev_cons = None
            for tc in range(NTC):
                s_ = tc % 2
                xa = xns[s_].rearrange("p (c n) -> p c n", c=8)
                tl = fw.dma('gpsimd', xa, dr['xnh'][seq].rearrange("c p t -> p c t")[:, :, tc * 128:(tc + 1) * 128],
                            deps=[prevx[s_]], ch=s_)
                ip, ps, rel = pp.get()
                last = None
                for mc in range(NCH):
                    last = fw.op('tensor', lambda e, m=mc, s0=(mc == 0): e.matmul(
                        ps[:], xa[:, m, :], wv[:, m, :], start=s0, stop=(m == NCH - 1)),
                        deps=[tl, tw] + rel)
                prevx[s_] = last
                if kind == 'v':
                    t1 = fw.op('scalar', lambda e, tc=tc: e.activation(vtv[:, tc, :], ps[:], AF.Identity), deps=[last])
                    pp.done(ip, [t1])
                    continue
                t1 = fw.op('scalar', lambda e: e.activation(qtmp, ps[:], AF.Identity), deps=[last, prev_cons])
                pp.done(ip, [t1])
                sgn = (-1.0 / 32.0) if kind == 'q' else 1.0
                t2 = fw.op('vector', lambda e, tc=tc, sgn=sgn: e.tensor_scalar(
                    qstmp, qtmp, C.posn[:, tc:tc + 1], sgn, ALU.mult, ALU.mult), deps=[t1])
                if kind == 'q':
                    t3 = fw.op('vector', lambda e: e.tensor_scalar(qb, qtmp, 1.0 / 32.0, None, ALU.mult), deps=[t1])
                else:
                    t3 = fw.op('vector', lambda e: e.tensor_copy(qb, qtmp), deps=[t1])
                t4 = fw.op('vector', lambda e: e.tensor_copy(qsb, qstmp), deps=[t2])
                dv1 = qfv if kind == 'q' else kfv
                dv2 = qnv if kind == 'q' else ksv
                qb_rel = None
                for h2 in range(4):
                    it, pst, relt = pp.get()
                    pb16a = pst[:, 0:64].bitcast(BF16)
                    tt_ = fw.op('tensor', lambda e, h2=h2, pb16a=pb16a: e.transpose(
                        pb16a, qb[:, h2 * 128:(h2 + 1) * 128], C.idnb), deps=[t3] + relt)
                    ev = fw.op('scalar', lambda e, h2=h2, tc=tc, pb16a=pb16a: e.activation(
                        dv1[:, h2, tc * 128:(tc + 1) * 128], pb16a, AF.Identity), deps=[tt_])
                    pp.done(it, [ev])
                    it2, pst2, relt2 = pp.get()
                    pb16b = pst2[:, 0:64].bitcast(BF16)
                    tt2 = fw.op('tensor', lambda e, h2=h2, pb16b=pb16b: e.transpose(
                        pb16b, qsb[:, h2 * 128:(h2 + 1) * 128], C.idnb), deps=[t4] + relt2)
                    ev2 = fw.op('scalar', lambda e, h2=h2, tc=tc, pb16b=pb16b: e.activation(
                        dv2[:, h2, tc * 128:(tc + 1) * 128], pb16b, AF.Identity), deps=[tt2])
                    pp.done(it2, [ev2])
                    qb_rel = tt2
                prev_cons = fw.tok('tensor')
        qkv_pass('q', dr['L1_WqT'])
        qkv_pass('k', dr['L1_WkT'])
        qkv_pass('v', dr['L1_WvT'])
        fw.barrier()
        pts = [sb.alloc(256, BF16) for _ in range(4)]
        rrow = sb.alloc(512)
        rrec = sb.alloc(512)
        rbv = sb.alloc(512)
        otmp = None
        prel = [[], [], [], []]
        pi = 0
        for h in range(8):
            hp, ho = h // 2, (h % 2) * 64
            gh = hg * 8 + h
            for qt in range(NTT):
                sl = slice(qt * 512, (qt + 1) * 512)
                iR, psR, relR = pp.get(hold=True)
                iO, psO, relO = pp.get(hold=True)
                lastR = lastO = None
                pend = None
                for kc in range(NTC):
                    iS, psS, relS = pp.get()
                    fw.op('tensor', lambda e, kc=kc, psS=psS: e.matmul(
                        psS[:], ksv[ho:ho + 64, hp, kc * 128:(kc + 1) * 128],
                        qfv[ho:ho + 64, hp, sl], start=True, stop=False), deps=relS)
                    tm = fw.op('tensor', lambda e, kc=kc, psS=psS: e.matmul(
                        psS[:], kfv[ho:ho + 64, hp, kc * 128:(kc + 1) * 128],
                        qnv[ho:ho + 64, hp, sl], start=False, stop=True), deps=[])
                    kcm = kc - qt * 4
                    mts = [tm]
                    pt = pts[pi % 4]
                    if kcm < 0:
                        texp = fw.op('scalar', lambda e, pt=pt: e.activation(pt, psS[:], AF.Exp, scale=C.negc[:, 0:1]),
                                     deps=mts + prel[pi % 4])
                    elif kcm > 3:
                        texp = fw.op('scalar', lambda e, pt=pt: e.activation(pt, psS[:], AF.Exp),
                                     deps=mts + prel[pi % 4])
                    else:
                        tsg = fw.op('vector', lambda e, m=kcm: e.tensor_tensor(
                            psS[:, m * 128:(m + 1) * 128], psS[:, m * 128:(m + 1) * 128], C.sgn, ALU.mult), deps=[tm])
                        mts.append(tsg)
                        texp = None
                        for (a, b, neg) in [(0, kcm + 1, False), (kcm + 1, 4, True)]:
                            if a >= b:
                                continue
                            if neg:
                                texp = fw.op('scalar', lambda e, pt=pt, a=a, b=b: e.activation(
                                    pt[:, a * 128:b * 128], psS[:, a * 128:b * 128], AF.Exp, scale=C.negc[:, 0:1]),
                                    deps=mts + prel[pi % 4])
                            else:
                                texp = fw.op('scalar', lambda e, pt=pt, a=a, b=b: e.activation(
                                    pt[:, a * 128:b * 128], psS[:, a * 128:b * 128], AF.Exp),
                                    deps=mts + prel[pi % 4])
                    pp.done(iS, [texp])
                    if pend is not None:
                        (ptx, texpx, kcx, ipx) = pend
                        lastR = fw.op('tensor', lambda e, pt=ptx, s0=(kcx == 0): e.matmul(
                            psR[0:1, :], C.onescb, pt, start=s0, stop=False),
                            deps=[texpx] + (relR if kcx == 0 else []))
                        lastO = fw.op('tensor', lambda e, pt=ptx, kc=kcx, h=h, s0=(kcx == 0): e.matmul(
                            psO[0:64, :], vtv[:, kc, h * 64:(h + 1) * 64], pt, start=s0, stop=False),
                            deps=[texpx] + (relO if kcx == 0 else []))
                        prel[ipx] = [lastR, lastO]
                    pend = (pt, texp, kc, pi % 4)
                    pi += 1
                (ptx, texpx, kcx, ipx) = pend
                lastR = fw.op('tensor', lambda e, pt=ptx, s0=(kcx == 0): e.matmul(
                    psR[0:1, :], C.onescb, pt, start=s0, stop=True),
                    deps=[texpx] + (relR if kcx == 0 else []))
                lastO = fw.op('tensor', lambda e, pt=ptx, kc=kcx, h=h, s0=(kcx == 0): e.matmul(
                    psO[0:64, :], vtv[:, kc, h * 64:(h + 1) * 64], pt, start=s0, stop=True),
                    deps=[texpx] + (relO if kcx == 0 else []))
                prel[ipx] = [lastR, lastO]
                t1 = fw.op('vector', lambda e: e.tensor_copy(rrow[0:1, :], psR[0:1, :]), deps=[lastR])
                pp.done(iR, [t1])
                t2 = fw.op('vector', lambda e: e.reciprocal(rrec[0:1, :], rrow[0:1, :]), deps=[t1])
                bi, bp, tb = bcast_row(C, rrec[0:1, :], 512, [t2])
                t3 = fw.op('scalar', lambda e: e.activation(rbv[0:64, :], bp[0:64, :], AF.Identity), deps=[tb, otmp])
                pp.done(bi, [t3])
                t4 = fw.op('vector', lambda e, gh=gh: e.tensor_tensor(
                    osv[ho:ho + 64, gh // 2, sl], psO[0:64, :], rbv[0:64, :], ALU.mult), deps=[t3, lastO])
                pp.done(iO, [t4])
                otmp = t4
        fw.barrier()
        sb.pop()
    sb.push()
    vt = fw.tok('vector')
    tiles = {(mc, tt): osv[:, mc, tt * 512:(tt + 1) * 512] for mc in range(NCH) for tt in range(NTT)}
    ttoks = {k: vt for k in tiles}
    mub, rstdb, stoks = ln_norm_tiles(C, tiles, ttoks, [])
    gcol = sb.alloc(8)
    bcol = sb.alloc(8)
    tgb = None
    for mc in range(NCH):
        fw.dma('sync', gcol[:, mc:mc + 1], dr['L1_lng'][mc])
        tgb = fw.dma('sync', bcol[:, mc:mc + 1], dr['L1_lnb'][mc])
    tmp = sb.alloc(512)
    tmpo = sb.alloc(512)
    dta = sb.alloc(512)
    pv1 = pv2 = pv0 = None
    for mc in range(NCH):
        for tt in range(NTT):
            sl = slice(tt * 512, (tt + 1) * 512)
            t1 = fw.op('vector', lambda e, a=tiles[(mc, tt)], m=mub[:, sl]: e.tensor_tensor(tmp, a, m, ALU.subtract),
                       deps=stoks + [pv0])
            t2 = fw.op('vector', lambda e, r=rstdb[:, sl]: e.tensor_tensor(tmp, tmp, r, ALU.mult), deps=[t1])
            t3 = fw.op('scalar', lambda e, m=mc: e.activation(tmpo, tmp, AF.Identity, scale=gcol[:, m:m + 1], bias=bcol[:, m:m + 1]),
                       deps=[t2, tgb, pv1])
            t4 = fw.dma('sync', dta, dr['dlt'][seq, mc, :, sl], deps=[pv2], ch=1)
            t5 = fw.op('vector', lambda e: e.tensor_tensor(dta, dta, tmpo, ALU.add), deps=[t3, t4])
            pv0 = t3
            pv1 = t5
            pv2 = fw.dma('sync', dr['dlt'][seq, mc, :, sl], dta, deps=[t5], ch=2)
    fw.barrier()
    sb.pop()
    sb.pop()


def l2_attn_phase(C, seq):
    fw, pp, sb, dr = C.fw, C.pp, C.sb, C.dr
    sb.push()
    vtm = sb.alloc(16 * 512, BF16)
    vtv = vtm.rearrange("p (c d) -> p c d", c=16)
    Pre = sb.alloc(17 * 512, BF16)
    Prv = Pre.rearrange("p (c d) -> p c d", c=17)
    Pim = sb.alloc(17 * 512, BF16)
    Piv = Pim.rearrange("p (c d) -> p c d", c=17)
    sb.push()
    xnb = sb.alloc(8 * 1024, BF16)
    xnbv = xnb.rearrange("p (c t) -> p c t", c=8)
    ldx = [fw.dma('gpsimd', xnbv[:, mc, :], dr['xnh'][seq, mc], ch=2) for mc in range(NCH)]
    wslot = [sb.alloc(8 * 64, BF16) for _ in range(2)]
    upc = sb.alloc(1026, BF16)
    dwt = [sb.alloc(64 * 6, BF16) for _ in range(2)]
    wcol = sb.alloc(4)
    x1h = sb.alloc(2048)
    vtmp = sb.alloc(2048)
    ucres = sb.alloc(2048)
    bic = sb.alloc(1)
    sfc_ = sb.alloc(1)
    wrel = [[], []]
    drel = [[], []]
    pup = None
    pdi = None
    puc = {}

    def upconv(c, dst, ddeps):
        nonlocal pup, pdi
        s_ = c % 2
        wv = wslot[s_].rearrange("p (c n) -> p c n", c=8)
        tw = fw.dma('gpsimd', wv, dr['L2_w_in'][:, c * 128:(c + 1) * 128].rearrange("(c p) n -> p c n", p=128),
                    deps=wrel[s_], ch=s_)
        tbi = fw.dma('sync', bic, dr['L2_b_in'][c], deps=[pup], ch=3)
        tsf = fw.dma('sync', sfc_, dr['L2_sf_b'][c], deps=[pup], ch=3)
        dv6 = dwt[s_].rearrange("p (j n) -> p j n", j=6)
        dts = []
        pdil = drel[s_]
        for j in range(3):
            twc = fw.dma('sync', wcol[:, j:j + 1], dr['L2_sfw_cols'][j, c], deps=pdil + [pdi], ch=3)
            t1 = fw.op('vector', lambda e, j=j: e.tensor_copy(wcol[:, 3:4].bitcast(BF16)[:, 0:1], wcol[:, j:j + 1]), deps=[twc])
            t2 = fw.op('vector', lambda e, j=j: e.tensor_copy(wcol[:, 3:4], wcol[:, 3:4].bitcast(BF16)[:, 0:1]), deps=[t1])
            t3 = fw.op('vector', lambda e, j=j: e.tensor_scalar(dv6[:, 2 * j, :], C.idn, wcol[:, 3:4], None, ALU.mult), deps=[t2] + pdil)
            t4 = fw.op('vector', lambda e, j=j: e.tensor_tensor(wcol[:, 3:4], wcol[:, j:j + 1], wcol[:, 3:4], ALU.subtract), deps=[t3])
            t5 = fw.op('vector', lambda e, j=j: e.tensor_scalar(dv6[:, 2 * j + 1, :], C.idn, wcol[:, 3:4], None, ALU.mult), deps=[t4] + pdil)
            dts.append(t5)
            pdi = t5
        tz = fw.memset('vector', upc[:, 0:2], 0.0, deps=[pup])
        evs = []
        for tt in range(NTT):
            sl = slice(tt * 512, (tt + 1) * 512)
            ip, ps, rel = pp.get()
            last = None
            for mc in range(NCH):
                last = fw.op('tensor', lambda e, m=mc, sl=sl, ps=ps, wv=wv, s0=(mc == 0): e.matmul(
                    ps[:], wv[:, m, :], xnbv[:, m, sl], start=s0, stop=(m == NCH - 1)),
                    deps=[tw] + rel + ldx)
            ev = fw.op('scalar', lambda e, tt=tt, ps=ps: e.activation(
                upc[:, 2 + tt * 512:2 + (tt + 1) * 512], ps[:], AF.Identity, bias=bic[:, 0:1]),
                deps=[last, tbi, pup, tz])
            pp.done(ip, [ev])
            evs.append(ev)
        cts = []
        for tt in range(NTT):
            ip, ps, rel = pp.get()
            last = None
            for j in range(3):
                for k in range(2):
                    last = fw.op('tensor', lambda e, j=j, k=k, tt=tt, ps=ps, dv6=dv6: e.matmul(
                        ps[:], dv6[:, 2 * j + k, :],
                        upc[:, tt * 512 + j:tt * 512 + j + 512],
                        start=(j == 0 and k == 0), stop=(j == 2 and k == 1)),
                        deps=dts + evs + rel)
            ev = fw.op('scalar', lambda e, tt=tt, ps=ps, dst=dst: e.activation(
                dst[:, tt * 512:(tt + 1) * 512], ps[:], AF.Identity, bias=sfc_[:, 0:1]),
                deps=[last, tsf] + list(ddeps))
            pp.done(ip, [ev])
            cts.append(ev)
        pup = cts[-1]
        wrel[s_] = [fw.tok('tensor')]
        drel[s_] = [fw.tok('tensor')]
        return cts

    pvw = None
    ptr = None
    ptm = None
    for i in range(8):
        t_x0 = upconv(i, ucres, [pvw])
        pvw = fw.dma('sync', dr['x0h'][i], ucres, deps=t_x0, ch=2)
        t_x1 = upconv(8 + i, x1h, [ptm])
        t_v = upconv(16 + i, vtmp, [ptr, ptm])
        tm = fw.op('vector', lambda e: e.tensor_tensor(vtmp, vtmp, x1h, ALU.mult), deps=t_x1 + t_v)
        ptm = tm
        tvw = fw.dma('sync', dr['vh'][i], vtmp, deps=[tm], ch=2)
        for tc in range(NTC):
            it, pst, relt = pp.get()
            tt_ = fw.op('tensor', lambda e, tc=tc, pst=pst: e.transpose(
                pst[:, 0:128], vtmp[:, tc * 128:(tc + 1) * 128], C.idn), deps=[tm] + relt)
            ev = fw.op('scalar', lambda e, tc=tc, i=i, pst=pst: e.activation(
                vtv[:, tc, i * 128:(i + 1) * 128], pst[:, 0:128], AF.Identity), deps=[tt_, ptr])
            pp.done(it, [ev])
            ptr = ev
        pvw = tvw
    fw.barrier()
    sb.pop()
    sb.push()
    crs = [sb.alloc(64, BF16) for _ in range(2)]
    cis = [sb.alloc(64, BF16) for _ in range(2)]
    vre = sb.alloc(1024)
    vim = sb.alloc(1024)
    kre = sb.alloc(1024)
    kim = sb.alloc(1024)
    ptmp = sb.alloc(1024)
    pcr = [None, None]
    pev = None
    pkl = None
    for fc in range(17):
        ire0, pre0, r0 = pp.get(hold=True)
        ire1, pre1, r1 = pp.get(hold=True)
        iim0, pim0, r2 = pp.get(hold=True)
        iim1, pim1, r3 = pp.get(hold=True)
        lasts = []
        for tc in range(NTC):
            s_ = tc % 2
            tcr = fw.dma('gpsimd', crs[s_], dr['C4R'][tc * 128:(tc + 1) * 128, fc * 128:(fc + 1) * 128],
                         deps=[pcr[s_]], ch=s_)
            tci = fw.dma('gpsimd', cis[s_], dr['C4I'][tc * 128:(tc + 1) * 128, fc * 128:(fc + 1) * 128],
                         deps=[pcr[s_]], ch=s_)
            lasts = []
            lst = None
            for (pp_, cc, hh, rr) in ((pre0, crs[s_], 0, r0), (pre1, crs[s_], 1, r1),
                                      (pim0, cis[s_], 0, r2), (pim1, cis[s_], 1, r3)):
                lst = fw.op('tensor', lambda e, p=pp_, c=cc, hh=hh, tc=tc: e.matmul(
                    p[:], c, vtv[:, tc, hh * 512:(hh + 1) * 512], start=(tc == 0), stop=(tc == NTC - 1)),
                    deps=[tcr, tci] + (list(rr) if tc == 0 else []))
                lasts.append(lst)
            pcr[s_] = lst
        e1 = fw.op('scalar', lambda e: e.activation(vre[:, 0:512], pre0[:], AF.Identity), deps=[lasts[0], pev])
        e2 = fw.op('scalar', lambda e: e.activation(vre[:, 512:1024], pre1[:], AF.Identity), deps=[lasts[1], pev])
        e3 = fw.op('scalar', lambda e: e.activation(vim[:, 0:512], pim0[:], AF.Identity), deps=[lasts[2], pev])
        e4 = fw.op('scalar', lambda e: e.activation(vim[:, 512:1024], pim1[:], AF.Identity), deps=[lasts[3], pev])
        pp.done(ire0, [e1])
        pp.done(ire1, [e2])
        pp.done(iim0, [e3])
        pp.done(iim1, [e4])
        tk1 = fw.dma('sync', kre, dr['KRE'][fc], deps=[pkl], ch=1)
        tk2 = fw.dma('sync', kim, dr['KIM'][fc], deps=[pkl], ch=1)
        warm(C, 4)
        t1 = fw.op('vector', lambda e: e.tensor_tensor(ptmp, vre, kre, ALU.mult), deps=[e1, e2, tk1])
        t2 = fw.op('vector', lambda e, fc=fc: e.tensor_tensor(Prv[:, fc, :], vim, kim, ALU.mult), deps=[e3, e4, tk2])
        t3 = fw.op('vector', lambda e, fc=fc: e.tensor_tensor(ptmp, ptmp, Prv[:, fc, :], ALU.subtract), deps=[t1, t2])
        t3b = fw.op('vector', lambda e, fc=fc: e.tensor_copy(Prv[:, fc, :], ptmp), deps=[t3])
        t4 = fw.op('vector', lambda e: e.tensor_tensor(ptmp, vre, kim, ALU.mult), deps=[t3b, tk2])
        t5 = fw.op('vector', lambda e, fc=fc: e.tensor_tensor(Piv[:, fc, :], vim, kre, ALU.mult), deps=[t4, tk1])
        t6 = fw.op('vector', lambda e, fc=fc: e.tensor_tensor(ptmp, ptmp, Piv[:, fc, :], ALU.add), deps=[t5])
        t6b = fw.op('vector', lambda e, fc=fc: e.tensor_copy(Piv[:, fc, :], ptmp), deps=[t6])
        pkl = t6b
        pev = t6b
    fw.barrier()
    sb.pop()
    sb.push()
    obf = sb.alloc(8 * 1024, BF16)
    obv = obf.rearrange("p (c t) -> p c t", c=8)
    sb.push()
    abs_ = sb.alloc(17 * 256, BF16)
    abv = abs_.rearrange("p (c n) -> p c n", c=17)
    bbs = sb.alloc(17 * 256, BF16)
    bbv = bbs.rearrange("p (c n) -> p c n", c=17)
    vt_ = sb.alloc(512)
    x0t = sb.alloc(512)
    fbc = sb.alloc(8)
    ytmp = sb.alloc(512)
    tfb = None
    for mc in range(NCH):
        tfb = fw.dma('sync', fbc[:, mc:mc + 1], dr['L2_filt_bias'][mc], ch=3)
    pab = None
    pvl = None
    pgl = None
    for it in range(NTT):
        tas = []
        for fc in range(17):
            tas.append(fw.dma('gpsimd', abv[:, fc, :], dr['AMAT'][fc * 128:(fc + 1) * 128, it * 512:(it + 1) * 512],
                              deps=[pab], ch=0))
            tas.append(fw.dma('gpsimd', bbv[:, fc, :], dr['BMAT'][fc * 128:(fc + 1) * 128, it * 512:(it + 1) * 512],
                              deps=[pab], ch=1))
        for ch in range(NCH):
            ip, ps, rel = pp.get()
            last = None
            for fc in range(17):
                last = fw.op('tensor', lambda e, fc=fc, ch=ch, ps=ps, s0=(fc == 0): e.matmul(
                    ps[:], Prv[:, fc, ch * 128:(ch + 1) * 128], abv[:, fc, :], start=s0, stop=False),
                    deps=tas + rel)
                last = fw.op('tensor', lambda e, fc=fc, ch=ch, ps=ps, st=(fc == 16): e.matmul(
                    ps[:], Piv[:, fc, ch * 128:(ch + 1) * 128], bbv[:, fc, :], start=False, stop=st),
                    deps=[])
            tvl = fw.dma('sync', vt_, dr['vh'][ch, :, it * 512:(it + 1) * 512], deps=[pvl], ch=1)
            txl = fw.dma('sync', x0t, dr['x0h'][ch, :, it * 512:(it + 1) * 512], deps=[pvl], ch=2)
            t1 = fw.op('vector', lambda e, ch=ch: e.tensor_scalar(ytmp, vt_, fbc[:, ch:ch + 1], None, ALU.mult),
                       deps=[tvl, tfb, pgl])
            t2 = fw.op('vector', lambda e, ps=ps: e.tensor_tensor(ytmp, ytmp, ps[:], ALU.add), deps=[t1, last])
            t3 = fw.op('vector', lambda e, ch=ch, it=it: e.tensor_tensor(
                obv[:, ch, it * 512:(it + 1) * 512], ytmp, x0t, ALU.mult), deps=[t2, txl])
            pp.done(ip, [t2])
            pvl = t3
            pgl = t3
        pab = fw.tok('tensor')
    fw.barrier()
    sb.pop()
    wslot2 = sb.alloc(8 * 512, BF16)
    wv2 = wslot2.rearrange("p (c n) -> p c n", c=8)
    tw = fw.dma('gpsimd', wv2, dr['L2_w_out'].rearrange("(c p) n -> p c n", p=128), ch=0)
    boc = sb.alloc(8)
    tbo = None
    for mc in range(NCH):
        tbo = fw.dma('sync', boc[:, mc:mc + 1], dr['L2_b_out'][mc], ch=3)
    tmpo = sb.alloc(512)
    dta = sb.alloc(512)
    pv1 = pv2 = None
    for mc in range(NCH):
        for tt in range(NTT):
            sl = slice(tt * 512, (tt + 1) * 512)
            ip, ps, rel = pp.get()
            last = None
            for c in range(NCH):
                last = fw.op('tensor', lambda e, c=c, m=mc, sl=sl, ps=ps, s0=(c == 0): e.matmul(
                    ps[:], wv2[:, c, m * 128:(m + 1) * 128], obv[:, c, sl],
                    start=s0, stop=(c == NCH - 1)), deps=[tw] + rel)
            t1 = fw.op('scalar', lambda e, m=mc, ps=ps: e.activation(tmpo, ps[:], AF.Identity, bias=boc[:, m:m + 1]),
                       deps=[last, tbo, pv1])
            pp.done(ip, [t1])
            warm(C, 2)
            t2 = fw.dma('sync', dta, dr['dlt'][seq, mc, :, sl], deps=[pv2], ch=1)
            t3 = fw.op('vector', lambda e: e.tensor_tensor(dta, dta, tmpo, ALU.add), deps=[t1, t2])
            pv1 = t3
            pv2 = fw.dma('sync', dr['dlt'][seq, mc, :, sl], dta, deps=[t3], ch=2)
    fw.barrier()
    sb.pop()
    sb.pop()



def l2_filter_phase(C):
    fw, pp, sb, dr = C.fw, C.pp, C.sb, C.dr
    sb.push()
    zaug = sb.alloc(4096)
    tz = fw.dma('sync', zaug[0:4, :], dr['ZAUG'])
    h1 = sb.alloc(4096)
    h2 = sb.alloc(4096)
    tmpa = sb.alloc(512)
    tmpb = sb.alloc(512)
    wa = sb.alloc(64)
    fq = sb.alloc(1)
    t1_ = fw.memset('vector', h1[64:65, :], 1.0)
    t2_ = fw.memset('vector', h2[64:65, :], 1.0)
    PI = math.pi

    def mlp_layer(wname, bname, fname, K, hsrc, hdst, deps):
        nw = 3 if K == 4 else 64
        tw = fw.dma('sync', wa[0:nw, 0:64], dr[wname], deps=[fw.tok('tensor'), fw.tok('scalar')])
        tb = fw.dma('sync', wa[nw:nw + 1, 0:64], dr[bname], deps=[])
        tf = fw.dma('sync', fq[0:64, :], dr[fname], deps=[fw.tok('vector')])
        toks = []
        for nt in range(8):
            sl = slice(nt * 512, (nt + 1) * 512)
            ip, ps, rel = pp.get()
            tm = fw.op('tensor', lambda e, sl=sl, K=K: e.matmul(
                ps[0:64, :], wa[0:K, 0:64], hsrc[0:K, sl], start=True, stop=True),
                deps=[tw, tb] + rel + list(deps))
            warm(C, 8)
            tv = fw.op('vector', lambda e: e.tensor_scalar(tmpa[0:64, :], ps[0:64, :], fq[0:64, 0:1], None, ALU.mult),
                       deps=[tm, tf, fw.tok('scalar')])
            pp.done(ip, [tv])
            for _ in range(3):
                tv = fw.op('vector', lambda e: e.tensor_scalar(tmpb[0:64, :], tmpa[0:64, :], PI, None, ALU.is_gt), deps=[tv])
                tv = fw.op('vector', lambda e: e.scalar_tensor_tensor(tmpa[0:64, :], tmpb[0:64, :], -2 * PI, tmpa[0:64, :], ALU.mult, ALU.add), deps=[tv])
                tv = fw.op('vector', lambda e: e.tensor_scalar(tmpb[0:64, :], tmpa[0:64, :], -PI, None, ALU.is_lt), deps=[tv])
                tv = fw.op('vector', lambda e: e.scalar_tensor_tensor(tmpa[0:64, :], tmpb[0:64, :], 2 * PI, tmpa[0:64, :], ALU.mult, ALU.add), deps=[tv])
            ts_ = fw.op('scalar', lambda e, sl=sl: e.activation(hdst[0:64, sl], tmpa[0:64, :], AF.Sin), deps=[tv])
            toks.append(ts_)
        return toks

    tk = mlp_layer('L2_f_w1', 'L2_f_b1', 'L2_freq1', 4, zaug, h1, [tz, t1_])
    tk = mlp_layer('L2_f_w2', 'L2_f_b2', 'L2_freq2', 65, h1, h2, tk + [t2_])
    tk = mlp_layer('L2_f_w3', 'L2_f_b3', 'L2_freq3', 65, h2, h1, tk + [t1_])
    fw.barrier()
    ktm = sb.alloc(32 * 512, BF16)
    kv = ktm.rearrange("p (c n) -> p c n", c=32)
    fwout = sb.alloc(1024)
    tfo = fw.dma('sync', fwout[0:64, :], dr['L2_f_wout'])
    dec = [sb.alloc(512) for _ in range(2)]
    pdec = [None, None]
    for tc in range(32):
        for chh in range(2):
            s_ = chh
            td = fw.dma('sync', dec[s_], dr['DECAY'][tc * 128:(tc + 1) * 128, chh * 512:(chh + 1) * 512],
                        deps=[pdec[s_]], ch=1)
            ip, ps, rel = pp.get()
            tm = fw.op('tensor', lambda e, tc=tc, chh=chh: e.matmul(
                ps[:], h1[0:64, tc * 128:(tc + 1) * 128], fwout[0:64, chh * 512:(chh + 1) * 512],
                start=True, stop=True), deps=[tfo] + rel)
            tv = fw.op('vector', lambda e, tc=tc, chh=chh, s_=s_: e.tensor_tensor(
                kv[:, tc, chh * 512:(chh + 1) * 512], ps[:], dec[s_], ALU.mult), deps=[tm, td])
            pp.done(ip, [tv])
            pdec[s_] = tv
    crs = [sb.alloc(64, BF16) for _ in range(2)]
    cis = [sb.alloc(64, BF16) for _ in range(2)]
    kreb = sb.alloc(1024)
    kimb = sb.alloc(1024)
    pcr = [None, None]
    pev = None
    ktv = fw.tok('vector')
    for fc in range(17):
        ire0, pre0, r0 = pp.get(hold=True)
        ire1, pre1, r1 = pp.get(hold=True)
        iim0, pim0, r2 = pp.get(hold=True)
        iim1, pim1, r3 = pp.get(hold=True)
        lasts = []
        for tc in range(32):
            s_ = tc % 2
            tcr = fw.dma('gpsimd', crs[s_], dr['C4R'][tc * 128:(tc + 1) * 128, fc * 128:(fc + 1) * 128],
                         deps=[pcr[s_]], ch=s_)
            tci = fw.dma('gpsimd', cis[s_], dr['C4I'][tc * 128:(tc + 1) * 128, fc * 128:(fc + 1) * 128],
                         deps=[pcr[s_]], ch=s_)
            lst = None
            lasts = []
            for (pp_, cc, hh, rr) in ((pre0, crs[s_], 0, r0), (pre1, crs[s_], 1, r1),
                                      (pim0, cis[s_], 0, r2), (pim1, cis[s_], 1, r3)):
                lst = fw.op('tensor', lambda e, p=pp_, c=cc, hh=hh, tc=tc: e.matmul(
                    p[:], c, kv[:, tc, hh * 512:(hh + 1) * 512], start=(tc == 0), stop=(tc == 31)),
                    deps=[tcr, tci, ktv] + (list(rr) if tc == 0 else []))
                lasts.append(lst)
            pcr[s_] = lst
        e1 = fw.op('scalar', lambda e: e.activation(kreb[:, 0:512], pre0[:], AF.Identity), deps=[lasts[0], pev])
        e2 = fw.op('scalar', lambda e: e.activation(kreb[:, 512:1024], pre1[:], AF.Identity), deps=[lasts[1]])
        e3 = fw.op('scalar', lambda e: e.activation(kimb[:, 0:512], pim0[:], AF.Identity), deps=[lasts[2]])
        e4 = fw.op('scalar', lambda e: e.activation(kimb[:, 512:1024], pim1[:], AF.Identity), deps=[lasts[3]])
        pp.done(ire0, [e1])
        pp.done(ire1, [e2])
        pp.done(iim0, [e3])
        pp.done(iim1, [e4])
        s1 = fw.dma('sync', dr['KRE'][fc], kreb, deps=[e1, e2], ch=2)
        s2 = fw.dma('sync', dr['KIM'][fc], kimb, deps=[e3, e4], ch=2)
        pev = s2
    fw.barrier()
    sb.pop()


def l2_ffn_phase(C, seq):
    fw, pp, sb, dr = C.fw, C.pp, C.sb, C.dr
    sb.push()
    xi = sb.alloc(8 * 2052)
    xiv = xi.rearrange("p (c t) -> p c t", c=8)
    tz = fw.memset('vector', xiv[:, :, 0:1], 0.0)
    tz2 = fw.memset('vector', xiv[:, :, 2049:2052], 0.0)
    ldx = [fw.dma('sync', xiv[:, mc, 1:2049], dr['xnh'][seq, mc], deps=[tz, tz2]) for mc in range(NCH)]
    fw.barrier()
    for li in range(2):
        sb.push()
        xb = sb.alloc(8 * 1026, BF16)
        xbv = xb.rearrange("p (c t) -> p c t", c=8)
        cts = []
        pcv = None
        for mc in range(NCH):
            pcv = fw.op('vector', lambda e, mc=mc: e.tensor_copy(xbv[:, mc, :], xiv[:, mc, 0:2052]), deps=[pcv])
            cts.append(pcv)
        qfm = sb.alloc(4 * 1024, BF16)
        qfv = qfm.rearrange("p (c t) -> p c t", c=4)
        kfm = sb.alloc(4 * 1024, BF16)
        kfv = kfm.rearrange("p (c t) -> p c t", c=4)
        vtm = sb.alloc(16 * 256, BF16)
        vtv = vtm.rearrange("p (c d) -> p c d", c=16)
        osb = sb.alloc(4 * 1024, BF16)
        ov = osb.rearrange("p (c t) -> p c t", c=4)
        wslot = sb.alloc(8 * 256, BF16)
        wv = wslot.rearrange("p (c n) -> p c n", c=8)
        xpos = [sb.alloc(64) for _ in range(4)]
        qsb = sb.alloc(64)
        ksb = sb.alloc(64)
        qrow = sb.alloc(64)
        qtmp = sb.alloc(512)
        qsq = sb.alloc(512)
        qrt = sb.alloc(512)
        qb = sb.alloc(256, BF16)
        sscol = sb.alloc(16)
        t = fw.dma('sync', qrow[0:1, 0:64], dr['L2f_qs'][li], ch=3)
        bi, bp, tb = bcast_row(C, qrow[0:1, 0:64], 64, [t])
        tqs = fw.op('scalar', lambda e: e.activation(qsb, bp[:, 0:64], AF.Identity), deps=[tb])
        pp.done(bi, [tqs])
        t2 = fw.dma('sync', qrow[0:1, 0:64], dr['L2f_ks'][li], deps=[tb], ch=3)
        bi, bp, tb = bcast_row(C, qrow[0:1, 0:64], 64, [t2])
        tks = fw.op('scalar', lambda e: e.activation(ksb, bp[:, 0:64], AF.Identity), deps=[tb])
        pp.done(bi, [tks])

        def shslice(mc, a, b):
            off = 1 if mc < 4 else 0
            return xbv[:, mc, off + a: off + b]

        def qkv_pass(kind, c0):
            tw = fw.dma('gpsimd', wv, dr['L2f_wqkv'][li][:, c0:c0 + 512].rearrange("(c p) n -> p c n", p=128),
                        deps=[fw.tok('tensor')])
            prev_cons = None
            for tc in range(NTC):
                ip, ps, rel = pp.get()
                last = None
                for mc in range(NCH):
                    last = fw.op('tensor', lambda e, m=mc, tc=tc, s0=(mc == 0): e.matmul(
                        ps[:], shslice(m, tc * 128, (tc + 1) * 128), wv[:, m, :],
                        start=s0, stop=(m == NCH - 1)), deps=[tw] + rel + cts)
                if kind == 'v':
                    t1 = fw.op('scalar', lambda e, tc=tc: e.activation(vtv[:, tc, :], ps[:], AF.Identity), deps=[last])
                    pp.done(ip, [t1])
                    continue
                s_ = tc % 2
                ct, st_ = xpos[2 * s_], xpos[2 * s_ + 1]
                cn = 'XCQ' if kind == 'q' else 'XCK'
                sn = 'XSQ' if kind == 'q' else 'XSK'
                tct = fw.dma('sync', ct, dr[cn][tc * 128:(tc + 1) * 128, :], deps=[prev_cons], ch=1)
                tst = fw.dma('sync', st_, dr[sn][tc * 128:(tc + 1) * 128, :], deps=[prev_cons], ch=1)
                t1 = fw.op('scalar', lambda e: e.activation(qtmp, ps[:], AF.Identity), deps=[last, prev_cons])
                pp.done(ip, [t1])
                t3 = fw.op('scalar', lambda e: e.activation(qsq, qtmp, AF.Square), deps=[t1, prev_cons])
                q3 = qtmp.rearrange("p (h d) -> p h d", h=8)
                t4 = fw.op('vector', lambda e: e.reduce_sum(sscol[:, 0:8].rearrange("p (h o) -> p h o", h=8),
                                                            qsq.rearrange("p (h d) -> p h d", h=8), AX.X), deps=[t3])
                t4 = fw.op('scalar', lambda e: e.activation(sscol[:, 8:16], sscol[:, 0:8], AF.Sqrt), deps=[t4])
                t4 = fw.op('vector', lambda e: e.tensor_scalar(sscol[:, 8:16], sscol[:, 8:16], 1e-12, None, ALU.max), deps=[t4])
                t4 = fw.op('vector', lambda e: e.reciprocal(sscol[:, 0:8], sscol[:, 8:16]), deps=[t4])
                tn = []
                for h in range(8):
                    tn.append(fw.op('vector', lambda e, h=h: e.tensor_scalar(
                        q3[:, h, :], q3[:, h, :], sscol[:, h:h + 1], None, ALU.mult), deps=[t4]))
                scb = (qsb if kind == 'q' else ksb).rearrange("p (o d) -> p o d", o=1).broadcast_to([128, 8, 64])
                tqn = fw.op('vector', lambda e, scb=scb: e.tensor_tensor(q3, q3, scb, ALU.mult), deps=tn + [tqs, tks])
                cb = ct.rearrange("p (o d) -> p o d", o=1).broadcast_to([128, 8, 64])
                sb_ = st_.rearrange("p (o d) -> p o d", o=1).broadcast_to([128, 8, 64])
                r3 = qrt.rearrange("p (h d) -> p h d", h=8)
                s3 = qsq.rearrange("p (h d) -> p h d", h=8)
                u1 = fw.op('vector', lambda e, cb=cb: e.tensor_tensor(r3, q3, cb, ALU.mult), deps=[tqn, tct])
                u2 = fw.op('vector', lambda e, sb_=sb_: e.tensor_tensor(s3[:, :, 0:32], q3[:, :, 32:64], sb_[:, :, 0:32], ALU.mult), deps=[tqn, tst, t4])
                u3 = fw.op('vector', lambda e: e.tensor_tensor(r3[:, :, 0:32], r3[:, :, 0:32], s3[:, :, 0:32], ALU.subtract), deps=[u1, u2])
                u4 = fw.op('vector', lambda e, sb_=sb_: e.tensor_tensor(s3[:, :, 32:64], q3[:, :, 0:32], sb_[:, :, 32:64], ALU.mult), deps=[u1, u2])
                u5 = fw.op('vector', lambda e: e.tensor_tensor(r3[:, :, 32:64], r3[:, :, 32:64], s3[:, :, 32:64], ALU.add), deps=[u3, u4])
                dv = qfv if kind == 'q' else kfv
                qb_rel = None
                t6 = fw.op('vector', lambda e: e.tensor_copy(qb, qrt), deps=[u3, u5])
                for h2 in range(4):
                    it, pst, relt = pp.get()
                    pb16 = pst[:, 0:64].bitcast(BF16)
                    tt_ = fw.op('tensor', lambda e, h2=h2, pb16=pb16: e.transpose(
                        pb16, qb[:, h2 * 128:(h2 + 1) * 128], C.idnb), deps=[t6] + relt)
                    ev = fw.op('scalar', lambda e, h2=h2, tc=tc, pb16=pb16, dv=dv: e.activation(
                        dv[:, h2, tc * 128:(tc + 1) * 128], pb16, AF.Identity), deps=[tt_])
                    pp.done(it, [ev])
                    qb_rel = tt_
                prev_cons = fw.tok('vector')
        qkv_pass('q', 0)
        qkv_pass('k', 512)
        qkv_pass('v', 1024)
        fw.barrier()
        pts = [sb.alloc(256, BF16) for _ in range(4)]
        rrow = qtmp
        rrec = qsq
        rbv = qrt
        otmp = None
        prel = [[], [], [], []]
        pi = 0
        for h in range(8):
            hp, ho = h // 2, (h % 2) * 64
            for qw in range(NTT):
                sl = slice(qw * 512, (qw + 1) * 512)
                iR, psR, relR = pp.get(hold=True)
                iO, psO, relO = pp.get(hold=True)
                kcs = [kc for kc in range(qw * 4 - 4, qw * 4 + 4) if kc >= 0]
                lastR = lastO = None
                pend = None
                for ik, kc in enumerate(kcs):
                    iS, psS, relS = pp.get()
                    tm = fw.op('tensor', lambda e, kc=kc, sl=sl, hp=hp, ho=ho, psS=psS: e.matmul(
                        psS[:], kfv[ho:ho + 64, hp, kc * 128:(kc + 1) * 128],
                        qfv[ho:ho + 64, hp, sl], start=True, stop=True), deps=relS)
                    mts = [tm]
                    for m in range(4):
                        qc = qw * 4 + m
                        if kc == qc:
                            mts.append(fw.op('vector', lambda e, m=m, psS=psS: e.tensor_tensor(
                                psS[:, m * 128:(m + 1) * 128], psS[:, m * 128:(m + 1) * 128], C.trig, ALU.add), deps=[tm]))
                        elif kc == qc - 4:
                            mts.append(fw.op('vector', lambda e, m=m, psS=psS: e.tensor_tensor(
                                psS[:, m * 128:(m + 1) * 128], psS[:, m * 128:(m + 1) * 128], C.tril, ALU.add), deps=[tm]))
                        elif kc > qc or kc < qc - 4:
                            mts.append(fw.op('vector', lambda e, m=m, psS=psS: e.tensor_scalar(
                                psS[:, m * 128:(m + 1) * 128], psS[:, m * 128:(m + 1) * 128], -1e30, None, ALU.add), deps=[tm]))
                    pt = pts[pi % 4]
                    texp = fw.op('scalar', lambda e, pt=pt, psS=psS: e.activation(pt, psS[:], AF.Exp), deps=mts + prel[pi % 4])
                    pp.done(iS, [texp])
                    if pend is not None:
                        (ptx, texpx, kcx, ikx, ipx) = pend
                        lastR = fw.op('tensor', lambda e, pt=ptx, psR=psR, s0=(ikx == 0): e.matmul(
                            psR[0:1, :], C.onescb, pt, start=s0, stop=False),
                            deps=[texpx] + (relR if ikx == 0 else []))
                        lastO = fw.op('tensor', lambda e, pt=ptx, kc=kcx, h=h, psO=psO, s0=(ikx == 0): e.matmul(
                            psO[0:64, :], vtv[:, kc, h * 64:(h + 1) * 64], pt, start=s0, stop=False),
                            deps=[texpx] + (relO if ikx == 0 else []))
                        prel[ipx] = [lastR, lastO]
                    pend = (pt, texp, kc, ik, pi % 4)
                    pi += 1
                (ptx, texpx, kcx, ikx, ipx) = pend
                lastR = fw.op('tensor', lambda e, pt=ptx, psR=psR, s0=(ikx == 0): e.matmul(
                    psR[0:1, :], C.onescb, pt, start=s0, stop=True),
                    deps=[texpx] + (relR if ikx == 0 else []))
                lastO = fw.op('tensor', lambda e, pt=ptx, kc=kcx, h=h, psO=psO, s0=(ikx == 0): e.matmul(
                    psO[0:64, :], vtv[:, kc, h * 64:(h + 1) * 64], pt, start=s0, stop=True),
                    deps=[texpx] + (relO if ikx == 0 else []))
                prel[ipx] = [lastR, lastO]
                t1 = fw.op('vector', lambda e, psR=psR: e.tensor_copy(rrow[0:1, :], psR[0:1, :]), deps=[lastR])
                pp.done(iR, [t1])
                t2 = fw.op('vector', lambda e: e.reciprocal(rrec[0:1, :], rrow[0:1, :]), deps=[t1])
                bi, bp, tb = bcast_row(C, rrec[0:1, :], 512, [t2])
                t3 = fw.op('scalar', lambda e, bp=bp: e.activation(rbv[0:64, :], bp[0:64, :], AF.Identity), deps=[tb, otmp])
                pp.done(bi, [t3])
                t4 = fw.op('vector', lambda e, h=h, sl=sl, hp=hp, ho=ho, psO=psO: e.tensor_tensor(
                    ov[ho:ho + 64, hp, sl], psO[0:64, :], rbv[0:64, :], ALU.mult), deps=[t3, lastO])
                pp.done(iO, [t4])
                otmp = t4
        fw.barrier()
        wv4 = wslot.rearrange("p (c n) -> p c n", c=4)
        tw = fw.dma('gpsimd', wv4, dr['L2f_wo'][li].rearrange("(c p) n -> p c n", p=128), deps=[fw.tok('tensor')])
        tmpo = qtmp
        pv1 = None
        for mc in range(NCH):
            for tt in range(NTT):
                sl = slice(tt * 512, (tt + 1) * 512)
                ip, ps, rel = pp.get()
                last = None
                for c in range(4):
                    last = fw.op('tensor', lambda e, c=c, m=mc, sl=sl, ps=ps, s0=(c == 0): e.matmul(
                        ps[:], wv4[:, c, m * 128:(m + 1) * 128], ov[:, c, sl],
                        start=s0, stop=(c == 3)), deps=[tw] + rel)
                t1 = fw.op('scalar', lambda e, ps=ps: e.activation(tmpo, ps[:], AF.Identity), deps=[last, pv1])
                pp.done(ip, [t1])
                t2 = fw.op('vector', lambda e, mc=mc, tt=tt: e.tensor_tensor(
                    xiv[:, mc, 1 + tt * 512:1 + (tt + 1) * 512], xiv[:, mc, 1 + tt * 512:1 + (tt + 1) * 512],
                    tmpo, ALU.add), deps=[t1])
                pv1 = t2
        fw.barrier()
        sb.pop()
        sb.push()
        xnin = sb.alloc(8 * 1024, BF16)
        xnv = xnin.rearrange("p (c t) -> p c t", c=8)
        vt0 = fw.tok('vector')

        def shx(mc, tt):
            off = 1 if mc < 4 else 0
            return xiv[:, mc, off + tt * 512: off + (tt + 1) * 512]
        tiles = {(mc, tt): shx(mc, tt) for mc in range(NCH) for tt in range(NTT)}
        ttoks = {k: vt0 for k in tiles}
        sb.push()
        mub, rstdb, stoks = ln_norm_tiles(C, tiles, ttoks, [])
        gcol = sb.alloc(8)
        bcol = sb.alloc(8)
        tgb = None
        for mc in range(NCH):
            fw.dma('sync', gcol[:, mc:mc + 1], dr['L2f_lng'][li, mc], ch=3)
            tgb = fw.dma('sync', bcol[:, mc:mc + 1], dr['L2f_lnb'][li, mc], ch=3)
        tmp = sb.alloc(512)
        pvx = None
        for mc in range(NCH):
            for tt in range(NTT):
                sl = slice(tt * 512, (tt + 1) * 512)
                t1 = fw.op('vector', lambda e, a=tiles[(mc, tt)], m=mub[:, sl]: e.tensor_tensor(tmp, a, m, ALU.subtract),
                           deps=stoks + [pvx])
                t2 = fw.op('vector', lambda e, r=rstdb[:, sl]: e.tensor_tensor(tmp, tmp, r, ALU.mult), deps=[t1])
                pvx = fw.op('scalar', lambda e, m=mc, o=xnv[:, mc, sl]: e.activation(
                    o, tmp, AF.Identity, scale=gcol[:, m:m + 1], bias=bcol[:, m:m + 1]), deps=[t2, tgb])
        fw.barrier()
        sb.pop()
        NI = 22
        hsb = sb.alloc(NI * 512, BF16)
        hv = hsb.rearrange("p (c t) -> p c t", c=NI)
        w1a = [sb.alloc(8 * 64, BF16) for _ in range(2)]
        w1g = [sb.alloc(8 * 64, BF16) for _ in range(2)]
        w2s = [sb.alloc(NI * 64, BF16) for _ in range(2)]
        b1ac = sb.alloc(NI)
        b1gc = sb.alloc(NI)
        b2c = sb.alloc(8)
        tmpf = sb.alloc(512)
        tb1 = None
        for i in range(NI):
            fw.dma('sync', b1ac[:, i:i + 1], dr['L2f_b1p'][li, i], ch=3)
            tb1 = fw.dma('sync', b1gc[:, i:i + 1], dr['L2f_b1p'][li, 22 + i], ch=3)
        tb2 = None
        for mc in range(NCH):
            tb2 = fw.dma('sync', b2c[:, mc:mc + 1], dr['L2f_b2'][li, mc], ch=3)
        HP = 2816
        for th in range(2):
            wrel1 = [[], []]
            hrel = []
            prev_ga = None
            prev_h = None
            for i in range(NI):
                s_ = i % 2
                wva = w1a[s_].rearrange("p (c n) -> p c n", c=8)
                wvg = w1g[s_].rearrange("p (c n) -> p c n", c=8)
                ta = fw.dma('gpsimd', wva, dr['L2f_w1p'][li][:, i * 128:(i + 1) * 128].rearrange("(c p) n -> p c n", p=128), deps=wrel1[s_], ch=s_)
                tg = fw.dma('gpsimd', wvg, dr['L2f_w1p'][li][:, HP + i * 128:HP + (i + 1) * 128].rearrange("(c p) n -> p c n", p=128), deps=wrel1[s_], ch=2 + s_)
                for tt2 in range(2):
                    sl = slice(th * 1024 + tt2 * 512, th * 1024 + (tt2 + 1) * 512)
                    ia, pa, ra = pp.get()
                    ig, pg, rg = pp.get()
                    lasta = lastg = None
                    for mc in range(NCH):
                        lasta = fw.op('tensor', lambda e, m=mc, sl=sl, pa=pa, wva=wva, s0=(mc == 0): e.matmul(
                            pa[:], wva[:, m, :], xnv[:, m, sl], start=s0, stop=(m == NCH - 1)),
                            deps=[ta] + ra)
                    for mc in range(NCH):
                        lastg = fw.op('tensor', lambda e, m=mc, sl=sl, pg=pg, wvg=wvg, s0=(mc == 0): e.matmul(
                            pg[:], wvg[:, m, :], xnv[:, m, sl], start=s0, stop=(m == NCH - 1)),
                            deps=[tg] + rg)
                    t1 = fw.op('scalar', lambda e, i=i, pg=pg: e.activation(tmpf, pg[:], AF.Gelu, bias=b1gc[:, i:i + 1]),
                               deps=[lastg, tb1, prev_h])
                    t2 = fw.op('vector', lambda e, i=i, tt2=tt2, pa=pa: e.scalar_tensor_tensor(
                        hv[:, i, tt2 * 512:(tt2 + 1) * 512], pa[:], b1ac[:, i:i + 1], tmpf, ALU.add, ALU.mult),
                        deps=[t1, lasta] + hrel)
                    pp.done(ia, [t2])
                    pp.done(ig, [t1])
                    prev_ga = t2
                    prev_h = t2
                wrel1[s_] = [fw.tok('tensor')]
            hrel = []
            wrel2 = [[], []]
            for mc in range(NCH):
                s_ = mc % 2
                wv2 = w2s[s_].rearrange("p (c n) -> p c n", c=NI)
                t2w = fw.dma('gpsimd', wv2, dr['L2f_w2p'][li][:, mc * 128:(mc + 1) * 128].rearrange("(c p) n -> p c n", p=128), deps=wrel2[s_], ch=s_)
                for tt2 in range(2):
                    sl = slice(th * 1024 + tt2 * 512, th * 1024 + (tt2 + 1) * 512)
                    ip, ps, rel = pp.get()
                    last = None
                    for i in range(NI):
                        last = fw.op('tensor', lambda e, i=i, tt2=tt2, ps=ps, wv2=wv2, s0=(i == 0): e.matmul(
                            ps[:], wv2[:, i, :], hv[:, i, tt2 * 512:(tt2 + 1) * 512],
                            start=s0, stop=(i == NI - 1)), deps=[t2w, prev_ga] + rel)
                    t1 = fw.op('scalar', lambda e, m=mc, ps=ps: e.activation(tmpf, ps[:], AF.Identity, bias=b2c[:, m:m + 1]),
                               deps=[last, tb2, fw.tok('vector')])
                    t2 = fw.op('vector', lambda e, mc=mc, sl2=(1 + th * 1024 + tt2 * 512): e.tensor_tensor(
                        xiv[:, mc, sl2:sl2 + 512], xiv[:, mc, sl2:sl2 + 512], tmpf, ALU.add), deps=[t1])
                    pp.done(ip, [t2])
                wrel2[s_] = [fw.tok('tensor')]
            fw.barrier()
        sb.pop()
    for mc in range(NCH):
        fw.dma('sync', dr['dlt'][seq, mc], xiv[:, mc, 1:2049], deps=[fw.tok('vector')], ch=2)
    fw.barrier()
    sb.pop()






def l3_attn_phase(C, seq):
    fw, pp, sb, dr = C.fw, C.pp, C.sb, C.dr
    NH = 16
    sb.push()
    o_sb = sb.alloc(8 * 1024, BF16)
    ov = o_sb.rearrange("p (c t) -> p c t", c=8)
    q_fm = sb.alloc(8 * 1024, BF16)
    qfv = q_fm.rearrange("p (c t) -> p c t", c=8)
    k_fm = sb.alloc(8 * 1024, BF16)
    kfv = k_fm.rearrange("p (c t) -> p c t", c=8)
    v_tm = sb.alloc(16 * 512, BF16)
    vtv = v_tm.rearrange("p (c d) -> p c d", c=16)
    sb.push()
    binr = sb.alloc(1536, BF16)
    tbin = fw.dma('gpsimd', binr[0:1, :], dr['L3_b_in'])
    xns = [sb.alloc(8 * 64, BF16) for _ in range(2)]
    wslot = sb.alloc(8 * 512, BF16)
    wv = wslot.rearrange("p (c n) -> p c n", c=8)
    qb = sb.alloc(256, BF16)

    def qkv_pass(kind, c0):
        tw = fw.dma('gpsimd', wv, dr['L3_winT'][:, c0:c0 + 1024].rearrange("(c p) n -> p c n", p=128),
                    deps=[fw.tok('tensor')])
        prevx = [None, None]
        prev_cons = None
        for tc in range(NTC):
            s_ = tc % 2
            xa = xns[s_].rearrange("p (c n) -> p c n", c=8)
            tl = fw.dma('gpsimd', xa, dr['xnh'][seq].rearrange("c p t -> p c t")[:, :, tc * 128:(tc + 1) * 128],
                        deps=[prevx[s_]], ch=s_)
            evs = []
            for half in range(2):
                ip, ps, rel = pp.get()
                last = None
                for mc in range(NCH):
                    last = fw.op('tensor', lambda e, m=mc, h=half, s0=(mc == 0): e.matmul(
                        ps[:], xa[:, m, :], wv[:, m, h * 512:(h + 1) * 512], start=s0, stop=False),
                        deps=[tl, tw] + rel)
                last = fw.op('tensor', lambda e, h=half: e.matmul(
                    ps[:], C.onesrb[0:1, :], binr[0:1, c0 + half * 512:c0 + (half + 1) * 512],
                    start=False, stop=True), deps=[tbin])
                prevx[s_] = last
                if kind == 'v':
                    t1 = fw.op('scalar', lambda e, h=half, tc=tc: e.activation(
                        vtv[:, tc, h * 512:(h + 1) * 512], ps[:], AF.Identity), deps=[last])
                    pp.done(ip, [t1])
                    continue
                if kind == 'q':
                    t1 = fw.op('vector', lambda e: e.tensor_scalar(qb, ps[:], 0.125, None, ALU.mult),
                               deps=[last, prev_cons])
                else:
                    t1 = fw.op('vector', lambda e: e.tensor_copy(qb, ps[:]), deps=[last, prev_cons])
                pp.done(ip, [t1])
                dv = qfv if kind == 'q' else kfv
                qb_rel = None
                for h2 in range(4):
                    it, pst, relt = pp.get()
                    pb16a = pst[:, 0:64].bitcast(BF16)
                    tt_ = fw.op('tensor', lambda e, h2=h2, pb16a=pb16a: e.transpose(
                        pb16a, qb[:, h2 * 128:(h2 + 1) * 128], C.idnb), deps=[t1] + relt)
                    c = half * 4 + h2
                    ev = fw.op('scalar', lambda e, c=c, tc=tc, pb16a=pb16a: e.activation(
                        dv[:, c, tc * 128:(tc + 1) * 128], pb16a, AF.Identity), deps=[tt_])
                    pp.done(it, [ev])
                    qb_rel = tt_
                prev_cons = qb_rel
    qkv_pass('q', 0)
    qkv_pass('k', 1024)
    qkv_pass('v', 2048)
    fw.barrier()
    import os
    if os.environ.get('L3DBG'):
        for c in range(NCH):
            fw.dma('gpsimd', dr['OUT'][0, c], qfv[:, c, :])
            fw.dma('gpsimd', dr['OUT'][1, c], kfv[:, c, :])
        fw.barrier()
    sb.pop()
    sb.push()
    pts = [sb.alloc(256, BF16) for _ in range(4)]
    rrow = sb.alloc(512)
    rrec = sb.alloc(512)
    rbv = sb.alloc(512)
    otmp = None
    prel = [[], [], [], []]
    pi = 0
    for h in range(NH):
        hp, ho = h // 2, (h % 2) * 64
        for qt in range(NTT):
            sl = slice(qt * 512, (qt + 1) * 512)
            iR, psR, relR = pp.get(hold=True)
            iO, psO, relO = pp.get(hold=True)
            started = [False] * 4
            lastR = lastO = None
            pieces = []
            for m in range(-1, 4):
                kc = qt * 4 + m
                if kc < 0:
                    continue
                qlo, qhi = max(m, 0), min(m + 1, 3)
                pieces.append((kc, m, qlo, qhi))
            for (kc, m, qlo, qhi) in pieces:
                N = (qhi - qlo + 1) * 128
                iS, psS, relS = pp.get()
                tm = fw.op('tensor', lambda e, kc=kc, qlo=qlo, N=N: e.matmul(
                    psS[:, 0:N], kfv[ho:ho + 64, hp, kc * 128:(kc + 1) * 128],
                    qfv[ho:ho + 64, hp, qt * 512 + qlo * 128:qt * 512 + qlo * 128 + N],
                    start=True, stop=True), deps=relS)
                mts = [tm]
                for qc in range(qlo, qhi + 1):
                    off = (qc - qlo) * 128
                    tri = C.trig if (qt * 4 + qc) == kc else C.tril
                    mts.append(fw.op('vector', lambda e, off=off, tri=tri: e.tensor_tensor(
                        psS[:, off:off + 128], psS[:, off:off + 128], tri, ALU.add), deps=[tm]))
                pt = pts[pi % 4]
                texp = fw.op('scalar', lambda e, pt=pt, N=N: e.activation(pt[:, 0:N], psS[:, 0:N], AF.Exp),
                             deps=mts + prel[pi % 4])
                pp.done(iS, [texp])
                for qc in range(qlo, qhi + 1):
                    off = (qc - qlo) * 128
                    s0 = not started[qc]
                    started[qc] = True
                    stp = (m == qc)
                    lastR = fw.op('tensor', lambda e, pt=pt, off=off, qc=qc, s0=s0, stp=stp: e.matmul(
                        psR[0:1, qc * 128:(qc + 1) * 128], C.onescb, pt[:, off:off + 128],
                        start=s0, stop=stp), deps=[texp] + (relR if s0 else []))
                    lastO = fw.op('tensor', lambda e, pt=pt, off=off, qc=qc, kc=kc, s0=s0, stp=stp: e.matmul(
                        psO[0:64, qc * 128:(qc + 1) * 128], vtv[:, kc, h * 64:(h + 1) * 64],
                        pt[:, off:off + 128], start=s0, stop=stp), deps=[texp] + (relO if s0 else []))
                prel[pi % 4] = [lastR, lastO]
                pi += 1
            t1 = fw.op('vector', lambda e: e.tensor_copy(rrow[0:1, :], psR[0:1, :]), deps=[lastR])
            pp.done(iR, [t1])
            t2 = fw.op('vector', lambda e: e.reciprocal(rrec[0:1, :], rrow[0:1, :]), deps=[t1])
            bi, bp, tb = bcast_row(C, rrec[0:1, :], 512, [t2])
            t3 = fw.op('scalar', lambda e: e.activation(rbv[0:64, :], bp[0:64, :], AF.Identity), deps=[tb, otmp])
            pp.done(bi, [t3])
            t4 = fw.op('vector', lambda e: e.tensor_tensor(
                ov[ho:ho + 64, hp, sl], psO[0:64, :], rbv[0:64, :], ALU.mult), deps=[t3, lastO])
            pp.done(iO, [t4])
            otmp = t4
    fw.barrier()
    sb.pop()
    sb.push()
    wslot = sb.alloc(8 * 512, BF16)
    wv2 = wslot.rearrange("p (c n) -> p c n", c=8)
    tw = fw.dma('gpsimd', wv2, dr['L3_woutT'].rearrange("(c p) n -> p c n", p=128))
    boc = sb.alloc(8)
    tbo = None
    for mc in range(NCH):
        tbo = fw.dma('sync', boc[:, mc:mc + 1], dr['L3_b_out'][mc])
    tmpo = sb.alloc(512)
    dta = sb.alloc(512)
    pv1 = pv2 = None
    for mc in range(NCH):
        for tt in range(NTT):
            sl = slice(tt * 512, (tt + 1) * 512)
            ip, ps, rel = pp.get()
            last = None
            for c in range(NCH):
                last = fw.op('tensor', lambda e, c=c, m=mc, s0=(c == 0): e.matmul(
                    ps[:], wv2[:, c, m * 128:(m + 1) * 128], ov[:, c, sl],
                    start=s0, stop=(c == NCH - 1)), deps=[tw] + rel)
            t1 = fw.op('scalar', lambda e, m=mc: e.activation(tmpo, ps[:], AF.Identity, bias=boc[:, m:m + 1]),
                       deps=[last, tbo, pv1])
            pp.done(ip, [t1])
            warm(C, 2)
            t2 = fw.dma('sync', dta, dr['dlt'][seq, mc, :, sl], deps=[pv2], ch=1)
            t3 = fw.op('vector', lambda e: e.tensor_tensor(dta, dta, tmpo, ALU.add), deps=[t1, t2])
            pv1 = t3
            pv2 = fw.dma('sync', dr['dlt'][seq, mc, :, sl], dta, deps=[t3], ch=2)
    fw.barrier()
    sb.pop()
    sb.pop()


def l3_ffn_phase(C, seq):
    fw, pp, sb, dr = C.fw, C.pp, C.sb, C.dr
    sb.push()
    acc = sb.alloc(8 * 2048)
    accv = acc.rearrange("p (c t) -> p c t", c=8)
    xnb = sb.alloc(8 * 1024, BF16)
    xnbv = xnb.rearrange("p (c t) -> p c t", c=8)
    Mfm = sb.alloc(2048)
    ldx = [fw.dma('gpsimd', xnbv[:, mc, :], dr['xnh'][seq, mc]) for mc in range(NCH)]
    tz = fw.memset('vector', acc, 0.0)
    acc_tok = {(mc, tt): tz for mc in range(NCH) for tt in range(NTT)}
    sb.push()
    wgr = sb.alloc(64)
    wgv = wgr.rearrange("p (c e) -> p c e", c=8)
    t_w = fw.dma('sync', wgv, dr['L3_Wg'].rearrange("(c p) e -> p c e", p=128))
    bgrow = sb.alloc(8)
    t_b = fw.dma('sync', bgrow[0:1, :], dr['L3_bg'])
    bgb = sb.alloc(8)
    bi, bp, tb = bcast_row(C, bgrow[0:1, 0:8], 8, [t_b])
    t = fw.op('scalar', lambda e: e.activation(bgb, bp[:, 0:8], AF.Identity), deps=[tb])
    pp.done(bi, [t])
    xf = [sb.alloc(128) for _ in range(2)]
    lrow = sb.alloc(8)
    prob = sb.alloc(8)
    cols = sb.alloc(4)
    mtm = sb.alloc(8)
    prevx = [None, None]
    prev_tr = None
    prev_v = None
    for tc in range(NTC):
        ip, ps, rel = pp.get()
        lastmm = None
        for mc in range(NCH):
            xa = xf[mc % 2]
            tl = fw.dma('sync', xa, dr['xnh'][seq, mc, :, tc * 128:(tc + 1) * 128],
                        deps=[prevx[mc % 2]], ch=mc % 2)
            lastmm = fw.op('tensor', lambda e, a=xa, m=mc, s0=(mc == 0): e.matmul(
                ps[:, 0:8], a, wgv[:, m, :], start=s0, stop=(m == NCH - 1)),
                deps=[tl, t_w] + rel)
            prevx[mc % 2] = lastmm
        t1 = fw.op('vector', lambda e: e.tensor_tensor(lrow, ps[:, 0:8], bgb, ALU.add), deps=[lastmm, t, prev_v])
        pp.done(ip, [t1])
        warm(C, 6)
        t2 = fw.op('vector', lambda e: e.reduce_max(cols[:, 0:1], lrow, AX.X), deps=[t1])
        t2 = fw.op('vector', lambda e: e.tensor_scalar(cols[:, 1:2], cols[:, 0:1], -1.0, None, ALU.mult), deps=[t2])
        t3 = fw.op('scalar', lambda e: e.activation(prob, lrow, AF.Exp, bias=cols[:, 1:2]), deps=[t2, prev_v])
        t4 = fw.op('vector', lambda e: e.reduce_sum(cols[:, 2:3], prob, AX.X), deps=[t3])
        t4 = fw.op('vector', lambda e: e.reciprocal(cols[:, 3:4], cols[:, 2:3]), deps=[t4])
        t5 = fw.op('vector', lambda e: e.tensor_scalar(mtm, prob, 1.0, None, ALU.is_ge), deps=[t3])
        t5 = fw.op('vector', lambda e: e.tensor_tensor(mtm, mtm, prob, ALU.mult), deps=[t5])
        t5 = fw.op('vector', lambda e: e.tensor_scalar(mtm, mtm, cols[:, 3:4], None, ALU.mult), deps=[t5, t4])
        prev_v = t5
        it, pst, relt = pp.get()
        tt_ = fw.op('tensor', lambda e: e.transpose(pst[0:8, 0:128], mtm, C.idn), deps=[t5] + relt)
        tev = fw.op('scalar', lambda e, o=Mfm[0:8, tc * 128:(tc + 1) * 128]: e.activation(o, pst[0:8, 0:128], AF.Identity),
                    deps=[tt_, prev_tr])
        pp.done(it, [tev])
        prev_tr = tev
    tch = fw.dma('sync', dr['coefh'][0:8, :], Mfm[0:8, :], deps=[prev_tr])
    fw.barrier()
    sb.pop()
    sb.push()
    w1s = [sb.alloc(8 * 512, BF16)]
    w2s = [sb.alloc(8 * 512, BF16)]
    hm = sb.alloc(8 * 1024, BF16)
    hmv = hm.rearrange("p (c t) -> p c t", c=8)
    mb = sb.alloc(2048)
    b1c = sb.alloc(8)
    b2c = sb.alloc(8)
    crow = sb.alloc(512)
    crow_rel = []
    tmpf = sb.alloc(512)
    wrel = [[], []]
    w2rel = []
    mb_rel = []
    hm_rel = []
    prev_gelu = None
    for e_ in range(8):
        s = 0
        w1v = w1s[0].rearrange("p (c n) -> p c n", c=8)
        w2v = w2s[0].rearrange("p (c n) -> p c n", c=8)
        tw1 = fw.dma('gpsimd', w1v, dr['L3_W1'][e_].rearrange("(c p) n -> p c n", p=128), deps=wrel[s], ch=0)
        tw2 = fw.dma('gpsimd', w2v, dr['L3_W2'][e_].rearrange("(c p) n -> p c n", p=128), deps=w2rel, ch=2)
        tsc = fw.tok('scalar')
        tvv = fw.tok('vector')
        tb1 = tb2 = None
        for ic in range(NCH):
            tb1 = fw.dma('sync', b1c[:, ic:ic + 1], dr['L3_b1'][e_, ic], deps=[tsc])
            tb2 = fw.dma('sync', b2c[:, ic:ic + 1], dr['L3_b2'][e_, ic], deps=[tvv])
        mbts = []
        for tt in range(NTT):
            sl = slice(tt * 512, (tt + 1) * 512)
            tl = fw.dma('sync', crow[0:1, :], dr['coefh'][e_, sl], deps=crow_rel)
            bi, bp, tb = bcast_row(C, crow[0:1, :], 512, mb_rel + [tl])
            crow_rel = [tb]
            t = fw.op('scalar', lambda e, o=mb[:, sl], p=bp: e.activation(o, p[:], AF.Identity), deps=[tb] + mb_rel)
            pp.done(bi, [t])
            mbts.append(t)
        mb_rel = []
        for ic in range(NCH):
            for tt in range(NTT):
                sl = slice(tt * 512, (tt + 1) * 512)
                ip, ps, rel = pp.get()
                last = None
                for mc in range(NCH):
                    last = fw.op('tensor', lambda e, m=mc, i=ic, s0=(mc == 0): e.matmul(
                        ps[:], w1v[:, m, i * 128:(i + 1) * 128], xnbv[:, m, sl],
                        start=s0, stop=(m == NCH - 1)), deps=[tw1] + rel)
                t1 = fw.op('scalar', lambda e, i=ic: e.activation(tmpf, ps[:], AF.Gelu, bias=b1c[:, i:i + 1]),
                           deps=[last, tb1, prev_gelu])
                t2 = fw.op('vector', lambda e, o=hmv[:, ic, sl], c=mb[:, sl]: e.tensor_tensor(o, tmpf, c, ALU.mult),
                           deps=[t1, mbts[tt]] + hm_rel)
                pp.done(ip, [t1])
                prev_gelu = t2
                mb_rel.append(t2)
        hm_rel = []
        for mc in range(NCH):
            for tt in range(NTT):
                sl = slice(tt * 512, (tt + 1) * 512)
                ip, ps, rel = pp.get()
                last = None
                for ic in range(NCH):
                    last = fw.op('tensor', lambda e, i=ic, m=mc, s0=(ic == 0): e.matmul(
                        ps[:], w2v[:, i, m * 128:(m + 1) * 128], hmv[:, i, sl],
                        start=s0, stop=(ic == NCH - 1)), deps=[tw2] + rel + ([prev_gelu] if ic == 0 else []))
                t4 = fw.op('vector', lambda e, o=accv[:, mc, sl]: e.tensor_tensor(o, o, ps[:], ALU.add),
                           deps=[last, acc_tok[(mc, tt)]])
                t4 = fw.op('vector', lambda e, o=accv[:, mc, sl], m=mc: e.scalar_tensor_tensor(
                    o, mb[:, sl], b2c[:, m:m + 1], o, ALU.mult, ALU.add), deps=[t4, tb2, mbts[tt]])
                pp.done(ip, [t4])
                acc_tok[(mc, tt)] = t4
                mb_rel.append(t4)
            hm_rel = [fw.tok('tensor')]
        wrel[s] = [fw.tok('tensor')]
        w2rel = [fw.tok('tensor')]
    for mc in range(NCH):
        fw.dma('sync', dr['dlt'][seq, mc], accv[:, mc, :],
               deps=[acc_tok[(mc, tt)] for tt in range(NTT)])
    fw.barrier()
    sb.pop()
    sb.pop()


def kernel(upto=None, skip=None, **inputs):
    _upto = 4 if upto is None else upto
    _skip = set() if skip is None else set(skip)
    inputs = {k: np.asarray(v, dtype=np.float32) for k, v in inputs.items()}
    weights = prep_weights(inputs)
    consts = host_consts()
    x = inputs['L0_attn_x']
    nc = build_program(weights, consts, upto=_upto, skip=_skip)
    shared = {}
    for k, v in list(weights.items()) + list(consts.items()):
        shared[k] = np.ascontiguousarray(v, dtype=np.float32)
    in_maps = []
    for c in range(8):
        m = dict(shared)
        xs = x[2 * c:2 * c + 2]
        xs = np.ascontiguousarray(xs.transpose(0, 2, 1)).reshape(NSEQ, NCH, 128, T)
        m['XIN'] = xs
        in_maps.append(m)
    res = run_bass_kernel_spmd(nc, in_maps, core_ids=list(range(8)))
    outs = []
    for c in range(8):
        o = res.results[c]['OUT'].reshape(NSEQ, D, T).transpose(0, 2, 1)
        outs.append(o)
    return np.ascontiguousarray(np.concatenate(outs, axis=0), dtype=np.float32)
